# version 01
# baseline (speedup 1.0000x reference)
import numpy as np
from contextlib import ExitStack
import concourse.bass as bass
import concourse.mybir as mybir
from concourse.bass_utils import run_bass_kernel_spmd




F32 = mybir.dt.float32
F16 = mybir.dt.float16
BF16 = mybir.dt.bfloat16
I32 = mybir.dt.int32
AF = mybir.ActivationFunctionType
ALU = mybir.AluOpType
AX = mybir.AxisListType.X

D = 2048
TOK = 1024
NF = D // 128
N_CORES = 8


class Env:
    def __init__(self, nc, stack):
        self.nc = nc
        self.stack = stack
        self.sems = {}
        self.sb = {}
        self.pb = []

    def sem(self, name):
        assert name not in self.sems, name
        self.sems[name] = self.stack.enter_context(self.nc.semaphore(name))
        return self.sems[name]

    def sbuf(self, name, shape, dtype=F32):
        assert name not in self.sb, name
        self.sb[name] = self.stack.enter_context(
            self.nc.sbuf_tensor(name, list(shape), dtype))
        return self.sb[name]


class DSem:

    def __init__(self, env, tag):
        self.s = [env.sem(tag + "a"), env.sem(tag + "b")]

    def h(self, i):
        return self.s[i % 2]

    def val(self, i):
        return 16 * (i // 2 + 1)

    def wait(self, eng, i):
        eng.wait_ge(self.s[i % 2], self.val(i))

    def done(self, n):
        out = []
        if n >= 1:
            out.append((self.s[(n - 1) % 2], self.val(n - 1)))
        if n >= 2:
            out.append((self.s[(n - 2) % 2], self.val(n - 2)))
        return out


def make_env(nc, stack, arena_words=46080):
    env = Env(nc, stack)
    for i in range(8):
        env.pb.append(stack.enter_context(nc.psum_tensor("pb%d" % i, [128, 512], F32)))
    env.sbuf("ones128", (128, 1))
    env.sbuf("onesrow", (1, 128))
    env.sbuf("arena", (128, arena_words))
    env.arena = env.sb["arena"].ap()
    return env


def carve(env, spec):
    return {name: env.arena[:, off:off + n] for name, (off, n) in spec.items()}


def chunk_ap(dram_ap, r0, nr, c0, ncols):
    return (dram_ap[r0:r0 + nr, c0:c0 + ncols]
            .rearrange("(k p) m -> k p m", p=128).transpose([1, 0, 2]))


def sb3(buf_ap, k):
    return buf_ap.rearrange("p (k m) -> p k m", k=k)


def emit_const_init(env, block, loads):
    s_ld = env.sem("init_ld")
    total = 16 * len(loads) + 1

    @block.sync
    def _(q):
        for dst, src in loads:
            q.dma_start(out=dst, in_=src).then_inc(s_ld, 16)

    @block.vector
    def _(v):
        v.memset(env.sb["ones128"].ap()[:, :], 1.0)
        v.memset(env.sb["onesrow"].ap()[:, :], 1.0).then_inc(s_ld, 1)

    return [(s_ld, total)]


def emit_ln(env, block, tag, xin_ap, xn_dram_ap, bufs, waits=()):
    ds_x = DSem(env, tag + "_x")
    s_sq = env.sem(tag + "_sq")
    s_te = env.sem(tag + "_te")
    s_rw = env.sem(tag + "_rw")
    s_bc = env.sem(tag + "_bc")
    ds_x2 = DSem(env, tag + "_y")
    s_vc = env.sem(tag + "_vc")
    s_ap = env.sem(tag + "_ap")
    ds_wr = DSem(env, tag + "_w")
    xb = [bufs["xb0"], bufs["xb1"]]
    sq = [bufs["sq0"], bufs["sq1"]]
    xn = [bufs["xn0"], bufs["xn1"]]
    rows = bufs["rows"]
    ones128 = env.sb["ones128"].ap()[:, :]
    onesrow = env.sb["onesrow"].ap()[:, :]
    pb = env.pb
    mu = rows[0:1, 0:TOK]
    msq = rows[0:1, TOK:2 * TOK]
    var = rows[0:1, 2 * TOK:3 * TOK]
    r0 = rows[0:1, 3 * TOK:4 * TOK]
    s0 = rows[0:1, 4 * TOK:5 * TOK]
    t1 = rows[0:1, 5 * TOK:6 * TOK]
    t3 = rows[0:1, 6 * TOK:7 * TOK]
    rstd = rows[0:1, 7 * TOK:8 * TOK]

    @block.sync
    def _(q):
        for s, v in waits:
            q.wait_ge(s, v)
        for f in range(NF):
            if f >= 2:
                q.wait_ge(s_te, 4 * (f - 1))
                q.wait_ge(s_sq, f - 1)
            q.dma_start(out=xb[f % 2],
                        in_=xin_ap[f * 128:(f + 1) * 128, :]).then_inc(ds_x.h(f), 16)
        q.wait_ge(s_te, 64)
        for f in range(NF):
            if f >= 2:
                q.wait_ge(s_ap, f - 1)
            q.dma_start(out=xb[f % 2],
                        in_=xin_ap[f * 128:(f + 1) * 128, :]).then_inc(ds_x2.h(f), 16)

    @block.scalar
    def _(s):
        for f in range(NF):
            ds_x.wait(s, f)
            if f >= 2:
                s.wait_ge(s_te, 4 * (f - 1))
            s.square(sq[f % 2], xb[f % 2]).then_inc(s_sq, 1)
        s.wait_ge(s_rw, 1)
        s.sqrt(s0, r0).then_inc(s_rw, 1)

    @block.tensor
    def _(t):
        for f in range(NF):
            ds_x.wait(t, f)
            for h in range(2):
                t.matmul(pb[h].ap()[0:1, :], ones128,
                         xb[f % 2][:, h * 512:(h + 1) * 512],
                         start=(f == 0), stop=(f == NF - 1)).then_inc(s_te, 1)
            t.wait_ge(s_sq, f + 1)
            for h in range(2):
                t.matmul(pb[2 + h].ap()[0:1, :], ones128,
                         sq[f % 2][:, h * 512:(h + 1) * 512],
                         start=(f == 0), stop=(f == NF - 1)).then_inc(s_te, 1)
        t.wait_ge(s_rw, 3)
        for h in range(2):
            t.matmul(pb[4 + h].ap()[:, :], onesrow, mu[0:1, h * 512:(h + 1) * 512],
                     start=True, stop=True).then_inc(s_bc, 1)
        for h in range(2):
            t.matmul(pb[6 + h].ap()[:, :], onesrow, rstd[0:1, h * 512:(h + 1) * 512],
                     start=True, stop=True).then_inc(s_bc, 1)

    @block.vector
    def _(v):
        cnt = [0]

        def step(ins):
            cnt[0] += 1
            ins.then_inc(s_vc, 1)

        def sync():
            v.wait_ge(s_vc, cnt[0])

        v.wait_ge(s_te, 64)
        for h in range(2):
            sl = slice(h * 512, (h + 1) * 512)
            step(v.tensor_scalar(mu[0:1, sl], pb[h].ap()[0:1, :], 1.0 / D, None, ALU.mult))
            step(v.tensor_scalar(msq[0:1, sl], pb[2 + h].ap()[0:1, :], 1.0 / D, None, ALU.mult))
        sync()
        step(v.tensor_tensor(var, mu, mu, ALU.mult))
        sync()
        step(v.tensor_tensor(var, msq, var, ALU.subtract))
        sync()
        step(v.tensor_scalar(var, var, 1e-5, None, ALU.add))
        sync()
        v.reciprocal(r0, var).then_inc(s_rw, 1)
        v.wait_ge(s_rw, 2)
        step(v.tensor_tensor(t1, s0, s0, ALU.mult))
        sync()
        step(v.tensor_tensor(t1, t1, var, ALU.mult))
        sync()
        step(v.tensor_scalar(t3, t1, -0.5, 1.5, ALU.mult, ALU.add))
        sync()
        v.tensor_tensor(rstd, s0, t3, ALU.mult).then_inc(s_rw, 1)
        for f in range(NF):
            ds_x2.wait(v, f)
            v.wait_ge(s_bc, 4)
            if f >= 2:
                ds_wr.wait(v, f - 2)
            for h in range(2):
                sl = slice(h * 512, (h + 1) * 512)
                step(v.tensor_tensor(xn[f % 2][:, sl], xb[f % 2][:, sl],
                                     pb[4 + h].ap()[:, :], ALU.subtract))
                sync()
                ins = v.tensor_tensor(xn[f % 2][:, sl], xn[f % 2][:, sl],
                                      pb[6 + h].ap()[:, :], ALU.mult)
                if h == 0:
                    step(ins)
                    sync()
            ins.then_inc(s_ap, 1)

    @block.gpsimd
    def _(g):
        for f in range(NF):
            g.wait_ge(s_ap, f + 1)
            g.dma_start(out=xn_dram_ap[f * 128:(f + 1) * 128, :],
                        in_=xn[f % 2]).then_inc(ds_wr.h(f), 16)

    return ds_wr.done(NF)


def emit_scalepass(env, block, tag, w1sh_ap, w2sh_ap, ccin_ap, ccout_ap, bufs,
                   waits=()):
    ds_in = DSem(env, tag + "_i")
    s_red = env.sem(tag + "_red")
    s_mm = env.sem(tag + "_mm")
    s_io = env.sem(tag + "_io")
    s_io2 = env.sem(tag + "_io2")
    s_cc = env.sem(tag + "_cc")
    s_thr = env.sem(tag + "_thr")
    cb = [bufs["cbuf0"], bufs["cbuf1"]]
    accs = bufs["accs"]
    crow = bufs["crow"]
    thr4 = bufs["thr4"]
    pb = env.pb
    onesrow = env.sb["onesrow"].ap()[:, :]
    ones128 = env.sb["ones128"].ap()[:, :]
    chunks = [(w1sh_ap[0:128, :], False), (w1sh_ap[128:256, :], False),
              (chunk_ap(w2sh_ap, 0, 512, 0, 2048), True),
              (chunk_ap(w2sh_ap, 512, 512, 0, 2048), True)]

    @block.gpsimd
    def _(g):
        for s, v in waits:
            g.wait_ge(s, v)
        for i, (src, is3) in enumerate(chunks):
            if i >= 2:
                g.wait_ge(s_red, i - 1)
            g.dma_start(out=(sb3(cb[i % 2], 4) if is3 else cb[i % 2]),
                        in_=src).then_inc(ds_in.h(i), 16)
        g.wait_ge(s_red, 6)
        g.dma_start(out=ccin_ap, in_=crow[0:1, 0:2]).then_inc(s_io, 16)
        g.wait_ge(s_io, 16)
        g.collective_compute(
            "AllReduce", ALU.add, replica_groups=[list(range(N_CORES))],
            ins=[ccin_ap], outs=[ccout_ap]).then_inc(s_cc, 1)
        g.wait_ge(s_cc, 1)
        g.dma_start(out=crow[0:1, 2:4], in_=ccout_ap).then_inc(s_io2, 16)

    @block.vector
    def _(v):
        for i in range(4):
            ds_in.wait(v, i)
            v.tensor_reduce(accs[:, i:i + 1], cb[i % 2], AX, ALU.add,
                            apply_absolute_value=True).then_inc(s_red, 1)
        v.wait_ge(s_red, 4)
        v.tensor_tensor(accs[:, 4:5], accs[:, 0:1], accs[:, 1:2], ALU.add)
        v.tensor_tensor(accs[:, 5:6], accs[:, 2:3], accs[:, 3:4],
                        ALU.add).then_inc(s_red, 1)
        v.wait_ge(s_mm, 1)
        v.tensor_copy(crow[0:1, 0:2], pb[0].ap()[0:1, 0:2]).then_inc(s_red, 1)
        v.wait_ge(s_io2, 16)
        v.tensor_scalar(crow[0:1, 4:6], crow[0:1, 2:4], 1.0 / 16777216.0, None,
                        ALU.mult).then_inc(s_red, 1)
        v.wait_ge(s_red, 7)
        v.tensor_scalar(crow[0:1, 4:6], crow[0:1, 4:6], 1e-8, 0.5, ALU.max,
                        ALU.mult).then_inc(s_red, 1)
        v.wait_ge(s_red, 8)
        v.tensor_scalar(crow[0:1, 6:8], crow[0:1, 4:6], -1.0, None,
                        ALU.mult).then_inc(s_red, 1)
        v.wait_ge(s_mm, 2)
        v.tensor_copy(thr4, pb[0].ap()[:, 0:4]).then_inc(s_thr, 1)

    @block.tensor
    def _(t):
        t.wait_ge(s_red, 5)
        t.matmul(pb[0].ap()[0:1, 0:2], ones128, accs[:, 4:6],
                 start=True, stop=True).then_inc(s_mm, 1)
        t.wait_ge(s_red, 9)
        t.matmul(pb[0].ap()[:, 0:4], onesrow, crow[0:1, 4:8],
                 start=True, stop=True).then_inc(s_mm, 1)

    return [(s_thr, 1)]


def emit_quantize(env, block, tag, wt_ap, wq_ap, nrows, ncols, thr4, w2, bufs,
                  waits=()):
    ncb = ncols // 512
    nrb = nrows // 2048
    nchunks = ncb * nrb
    tc, ncc = (1, 3) if w2 else (0, 2)
    ds_in = DSem(env, tag + "_i")
    s_q = env.sem(tag + "_q")
    s_q2 = env.sem(tag + "_q2")
    ds_wr = DSem(env, tag + "_w")
    win = [bufs["win0"], bufs["win1"]]
    tmpq = bufs["tmpq"]
    qout = [bufs["qout0"], bufs["qout1"]]

    def blk(mc):
        rb, cbk = divmod(mc, ncb)
        return rb * 2048, cbk * 512

    @block.sync
    def _(q):
        for s, v in waits:
            q.wait_ge(s, v)
        for mc in range(nchunks):
            r0, c0 = blk(mc)
            if mc >= 2:
                q.wait_ge(s_q, mc - 1)
            q.dma_start(out=sb3(win[mc % 2], 16),
                        in_=chunk_ap(wt_ap, r0, 2048, c0, 512)).then_inc(ds_in.h(mc), 16)

    @block.vector
    def _(v):
        for mc in range(nchunks):
            ds_in.wait(v, mc)
            if mc >= 2:
                ds_wr.wait(v, mc - 2)
            if mc >= 1:
                v.wait_ge(s_q, mc)
            v.tensor_scalar(tmpq, win[mc % 2], thr4[:, ncc:ncc + 1], None,
                            ALU.is_lt).then_inc(s_q2, 1)
            v.wait_ge(s_q2, mc + 1)
            v.scalar_tensor_tensor(qout[mc % 2], win[mc % 2], thr4[:, tc:tc + 1],
                                   tmpq, ALU.is_gt, ALU.subtract).then_inc(s_q, 1)

    @block.gpsimd
    def _(g):
        for mc in range(nchunks):
            r0, c0 = blk(mc)
            g.wait_ge(s_q, mc + 1)
            g.dma_start(out=chunk_ap(wq_ap, r0, 2048, c0, 512),
                        in_=sb3(qout[mc % 2], 16)).then_inc(ds_wr.h(mc), 16)

    return ds_wr.done(nchunks)


def emit_ffn0_gemm(env, block, tag, xin_ap, xn_ap, w1q_ap, w2q_ap, b1_sb, halfb2_sb,
                   xout_ap, bufs, waits=()):
    s_xn = env.sem(tag + "_xn")
    ds_w1 = DSem(env, tag + "_a")
    s_hm = env.sem(tag + "_hm")
    s_he = env.sem(tag + "_he")
    ds_w2 = DSem(env, tag + "_b")
    s_kb = env.sem(tag + "_kb")
    s_ym = env.sem(tag + "_ym")
    s_yr = env.sem(tag + "_yr")
    ds_xr = DSem(env, tag + "_c")
    s_ad = env.sem(tag + "_ad")
    ds_out = DSem(env, tag + "_o")
    xntt = bufs["xntt"]
    wb = [bufs["wb0"], bufs["wb1"]]
    hb = bufs["hb"]
    w2b = [bufs["w2b0"], bufs["w2b1"]]
    tres = [bufs["tres0"], bufs["tres1"]]
    xres = [bufs["xres0"], bufs["xres1"]]
    pb = env.pb
    NTT, NMG, NDG, NKB = 4, 16, 4, 8

    @block.sync
    def _(q):
        for s, v in waits:
            q.wait_ge(s, v)
        for tt in range(NTT):
            if tt >= 1:
                q.wait_ge(s_hm, 64 * tt)
            q.dma_start(out=sb3(xntt, 16),
                        in_=chunk_ap(xn_ap, 0, 2048, tt * 256, 256)).then_inc(s_xn, 16)
            for mg in range(NMG):
                i = tt * NMG + mg
                if i >= 2:
                    q.wait_ge(s_hm, 4 * (i - 1))
                q.dma_start(out=sb3(wb[i % 2], 16),
                            in_=chunk_ap(w1q_ap, 0, 2048, mg * 512, 512)
                            ).then_inc(ds_w1.h(i), 16)
            for dg in range(NDG):
                for kb in range(NKB):
                    j = (tt * NDG + dg) * NKB + kb
                    if j >= 2:
                        gp, kp = divmod(j - 2, NKB)
                        if kp == NKB - 1:
                            q.wait_ge(s_ym, 4 * (gp + 1))
                        else:
                            q.wait_ge(s_kb, gp * (NKB - 1) + kp + 1)
                    q.dma_start(out=sb3(w2b[j % 2], 8),
                                in_=chunk_ap(w2q_ap, kb * 1024, 1024, dg * 512, 512)
                                ).then_inc(ds_w2.h(j), 16)

    @block.tensor
    def _(t):
        for tt in range(NTT):
            t.wait_ge(s_xn, 16 * (tt + 1))
            for mg in range(NMG):
                i = tt * NMG + mg
                ds_w1.wait(t, i)
                if i >= 2:
                    t.wait_ge(s_he, 4 * (i - 1))
                bp = (mg % 2) * 2
                for j in range(4):
                    for k in range(16):
                        ins = t.matmul(
                            pb[bp + j // 2].ap()[:, (j % 2) * 256:(j % 2) * 256 + 256],
                            wb[i % 2][:, k * 512 + j * 128: k * 512 + (j + 1) * 128],
                            xntt[:, k * 256:(k + 1) * 256],
                            start=(k == 0), stop=(k == 15))
                    ins.then_inc(s_hm, 1)
            for dg in range(NDG):
                gi = tt * NDG + dg
                if gi >= 1:
                    t.wait_ge(s_yr, 4 * gi)
                t.wait_ge(s_he, 64 * (tt + 1))
                for kb in range(NKB):
                    j = gi * NKB + kb
                    ds_w2.wait(t, j)
                    for jd in range(4):
                        for k in range(8):
                            kk = kb * 8 + k
                            ins = t.matmul(
                                pb[4 + jd].ap()[:, 0:256],
                                w2b[j % 2][:, k * 512 + jd * 128: k * 512 + (jd + 1) * 128],
                                hb[:, kk * 256:(kk + 1) * 256],
                                start=(kb == 0 and k == 0), stop=(kb == NKB - 1 and k == 7))
                        if kb == NKB - 1:
                            ins.then_inc(s_ym, 1)
                    if kb < NKB - 1:
                        ins.then_inc(s_kb, 1)

    @block.scalar
    def _(s):
        for tt in range(NTT):
            for mg in range(NMG):
                i = tt * NMG + mg
                bp = (mg % 2) * 2
                for j in range(4):
                    m = mg * 4 + j
                    s.wait_ge(s_hm, 4 * i + (2 if j < 2 else 4))
                    s.activation(hb[:, m * 256:(m + 1) * 256],
                                 pb[bp + j // 2].ap()[:, (j % 2) * 256:(j % 2) * 256 + 256],
                                 AF.Gelu, bias=b1_sb[:, m:m + 1]).then_inc(s_he, 1)
            for dg in range(NDG):
                gi = tt * NDG + dg
                for jd in range(4):
                    d = dg * 4 + jd
                    n = gi * 4 + jd
                    s.wait_ge(s_ym, gi * 4 + jd + 1)
                    if n >= 2:
                        ds_out.wait(s, n - 2)
                    s.activation(tres[n % 2],
                                 pb[4 + jd].ap()[:, 0:256],
                                 AF.Identity, bias=halfb2_sb[:, d:d + 1],
                                 scale=0.5).then_inc(s_yr, 1)

    @block.vector
    def _(v):
        for n in range(64):
            v.wait_ge(s_yr, n + 1)
            ds_xr.wait(v, n)
            v.tensor_tensor(tres[n % 2], tres[n % 2], xres[n % 2],
                            ALU.add).then_inc(s_ad, 1)

    @block.gpsimd
    def _(g):
        for tt in range(NTT):
            for dg in range(NDG):
                for jd in range(4):
                    d = dg * 4 + jd
                    n = (tt * NDG + dg) * 4 + jd
                    if n >= 2:
                        g.wait_ge(s_ad, n - 1)
                    g.dma_start(out=xres[n % 2],
                                in_=xin_ap[d * 128:(d + 1) * 128,
                                           tt * 256:(tt + 1) * 256]).then_inc(ds_xr.h(n), 16)
                    g.wait_ge(s_ad, n + 1)
                    g.dma_start(out=xout_ap[d * 128:(d + 1) * 128,
                                            tt * 256:(tt + 1) * 256],
                                in_=tres[n % 2]).then_inc(ds_out.h(n), 16)

    return ds_out.done(64)



FFN0_QZ = {"win0": (0, 8192), "win1": (8192, 8192), "tmpq": (16384, 8192),
           "qout0": (24576, 8192), "qout1": (32768, 8192)}
FFN0_SP = {"cbuf0": (0, 8192), "cbuf1": (8192, 8192)}
LN_BUFS = {"xb0": (0, 1024), "xb1": (1024, 1024), "sq0": (2048, 1024),
           "sq1": (3072, 1024), "xn0": (4096, 1024), "xn1": (5120, 1024),
           "rows": (6144, 8192)}
FFN0_GM = {"xntt": (0, 4096), "wb0": (4096, 8192), "wb1": (12288, 8192),
           "hb": (20480, 16384), "w2b0": (36864, 4096), "w2b1": (40960, 4096),
           "tres0": (45056, 256), "tres1": (45312, 256),
           "xres0": (45568, 256), "xres1": (45824, 256)}


def emit_ffn0_block(env, block, tag, xin_ap, xout_ap, ext, dram, first=True,
                    waits=()):
    sp = carve(env, FFN0_SP)
    qz = carve(env, FFN0_QZ)
    ln = carve(env, LN_BUFS)
    gm = carve(env, FFN0_GM)
    if "accs" not in env.sb:
        env.sbuf("accs", (128, 16))
        env.sbuf("crow", (1, 16))
        env.sbuf("thr4", (128, 4))
    accs = env.sb["accs"].ap()[:, :]
    crow = env.sb["crow"].ap()[:, :]
    thr4 = env.sb["thr4"].ap()[:, :]
    sp["accs"], sp["crow"], sp["thr4"] = accs, crow, thr4

    if first:
        w_thr = emit_scalepass(env, block, tag + "sp", ext["w1sh"], ext["w2sh"],
                               dram["ccin"], dram["ccout"], sp, waits=waits)
        w_q1 = emit_quantize(env, block, tag + "q1", ext["w1t"], dram["w1q"],
                             2048, 8192, thr4, False, qz, waits=w_thr)
        w_q2 = emit_quantize(env, block, tag + "q2", ext["w2t"], dram["w2q"],
                             8192, 2048, thr4, True, qz, waits=w_q1)
        qdone = w_q2
    else:
        qdone = []

    w_ln = emit_ln(env, block, tag + "ln", xin_ap, dram["xn"], ln,
                   waits=list(waits) + qdone)
    w_out = emit_ffn0_gemm(env, block, tag + "gm", xin_ap, dram["xn"],
                           dram["w1q"], dram["w2q"],
                           env.sb["b1c"].ap()[:, :], env.sb["halfb2"].ap()[:, :],
                           xout_ap, gm, waits=w_ln + qdone)
    return w_out


def host_ffn0_inputs(w1, b1, w2, b2, core):
    w1t = np.ascontiguousarray(w1.T)
    w2t = np.ascontiguousarray(w2.T)
    return {
        "f_w1t": w1t,
        "f_w2t": w2t,
        "f_w1sh": np.ascontiguousarray(w1t[core * 256:(core + 1) * 256, :]),
        "f_w2sh": np.ascontiguousarray(w2t[core * 1024:(core + 1) * 1024, :]),
        "f_b1c": np.ascontiguousarray(b1.reshape(64, 128).T),
        "f_b2c": np.ascontiguousarray(b2.reshape(16, 128).T),
    }



ENGINES = ("sync", "gpsimd", "tensor", "vector", "scalar")


class Sched:

    DPOOL = 6

    def __init__(self, env, tag, base_waits=()):
        self.env = env
        self.tag = tag
        self.base_waits = list(base_waits)
        self.steps = {e: [] for e in ENGINES}
        self.esem = {e: [env.sem(tag + "_e_" + e), 0] for e in ENGINES}
        self.dsem = {e: [[env.sem(tag + "_d%s%d" % (e, i)), 0]
                         for i in range(self.DPOOL)] for e in ("sync", "gpsimd")}
        self.dcnt = {e: 0 for e in ("sync", "gpsimd")}
        self.last_w = {}
        self.readers = {}
        self.clock = {e: {} for e in ENGINES}

    def _need(self, eng, ident):
        if ident is None:
            return False
        sem, val = ident
        if self.clock[eng].get(sem.name, 0) >= val:
            return False
        return True

    def _mark(self, eng, ident):
        sem, val = ident
        if self.clock[eng].get(sem.name, 0) < val:
            self.clock[eng][sem.name] = val

    def add(self, eng, fn, reads=(), writes=(), dma=False, extra_waits=()):
        deps = []
        for tk in reads:
            deps.append(self.last_w.get(tk))
        for tk in writes:
            deps.extend(self.readers.get(tk, ()))
            deps.append(self.last_w.get(tk))
        deps = [d for d in deps if d is not None]
        for w in extra_waits:
            deps.append(w)
        deps.extend(self.base_waits)
        if dma:
            slot = self.dcnt[eng] % self.DPOOL
            rounds = self.dcnt[eng] // self.DPOOL
            self.dcnt[eng] += 1
            sem = self.dsem[eng][slot][0]
            val = 16 * (rounds + 1)
            ident = (sem, val)
            if rounds >= 1:
                deps.append((sem, 16 * rounds))
            inc = 16
        else:
            es = self.esem[eng]
            es[1] += 1
            ident = (es[0], es[1])
            inc = 1
        self.steps[eng].append((fn, deps, ident, inc))
        for tk in writes:
            self.last_w[tk] = ident
            self.readers[tk] = []
        for tk in reads:
            self.readers.setdefault(tk, []).append(ident)
        return ident

    def barrier_ident(self):
        out = []
        for e in ENGINES:
            if self.esem[e][1] > 0:
                out.append((self.esem[e][0], self.esem[e][1]))
        for e in ("sync", "gpsimd"):
            n = self.dcnt[e]
            for i in range(self.DPOOL):
                rounds = (n - 1 - i) // self.DPOOL + 1 if n > i else 0
                if rounds > 0:
                    out.append((self.dsem[e][i][0], 16 * rounds))
        return out

    def finalize(self, block):
        for ename in ENGINES:
            steps = self.steps[ename]
            if not steps:
                continue

            def make(ename=ename, steps=steps):
                def body(eng):
                    for fn, deps, ident, inc in steps:
                        for sem, val in deps:
                            if self._need(ename, (sem, val)):
                                eng.wait_ge(sem, val)
                                self._mark(ename, (sem, val))
                        ins = fn(eng)
                        ins.then_inc(ident[0], inc)
                return body
            getattr(block, ename)(make())



"""L0 mamba (SSD) block emitter: xout = x + Mamba(LN(x)).

Mirrors proto_l0.device_mamba_core (validated vs reference in fp64).
Built on mklib.Sched (auto semaphore insertion from read/write tokens).
Per core: 1024 timesteps, 8 chunks of 128, 64 heads = 32 pairs.
Pair-collectives: dtsum (cumdt offset) and H_final (state handoff).
"""


L = 128
NC = 8
NP = 32
INV2PI = float(1.0 / (2.0 * np.pi))
C1 = float(np.float32(6.283185005187988))
C2 = float(2.0 * np.pi - 6.283185005187988)
HPI = float(np.pi / 2.0)

L0_BUFS = {
    "xn": (0, 16384),
    "dtw": (16384, 1024), "dts": (17408, 1024), "clw": (18432, 1024),
    "cls": (19456, 1024), "clf": (20480, 1024), "s64": (21504, 1024),
    "onesd": (22528, 1024),
    "dtT": (23552, 512), "ecT": (24064, 512), "dcT": (24576, 512),
    "lamb": (25088, 256), "lamr": (25344, 64), "lrow": (25408, 128),
    "clpk": (25536, 128), "rpk": (25664, 256),
    "bc": (25920, 1024), "B2": (26944, 1024), "B2s": (27968, 1024),
    "C2": (28992, 1024), "C2s": (30016, 1024),
    "sqs": (31040, 1024), "stg": (32064, 1024), "rmr": (33088, 1024),
    "wob0": (34112, 2048), "wob1": (36160, 2048),
    "ph0": (34112, 4096), "ph1": (38208, 4096),
    "zw": (34112, 2048),
    "chm": (38208, 1024), "shm": (39232, 1024),
    "t1s": (40256, 1024), "t2s": (41280, 1024),
    "xsp": (42304, 1024), "brt": (43328, 1024), "crt": (44352, 1024),
    "ytl": (45376, 128), "Hc": (45504, 64), "ut": (45568, 128),
    "bdt": (45696, 128), "et": (45824, 256),
    "gsc": (16384, 1024), "zsl": (17408, 1024), "ysc": (18432, 1024),
    "gtl": (19456, 256), "ggt": (20480, 1024),
    "Hall": (21504, 2048),
    "tres0": (25920, 512), "tres1": (26432, 512),
    "xres0": (26944, 512), "xres1": (27456, 512),
    "H0": (28992, 2048),
}


def emit_l0(env, block, tag, xin_ap, xout_ap, ext, dram, bufs, waits=()):
    pb = env.pb
    ones128 = env.sb["ones128"].ap()[:, :]
    onesrow = env.sb["onesrow"].ap()[:, :]
    S = Sched(env, tag, base_waits=list(waits))
    Bf = bufs
    identsb = ext["ident"]
    dtb, alog = ext["dtb"], ext["alog"]
    DrepT, nwrep = ext["Drep"], ext["nwrep"]
    BbT, BbTs, CbT, CbTs = ext["BbT"], ext["BbTs"], ext["CbT"], ext["CbTs"]
    freqs = ext["freqs"]
    is1 = ext["is1"]
    triu2 = ext["triu2"]
    qsel = ext["qsel"]
    win_ap, wout_ap = ext["win"], ext["wout"]
    xn_ap, xs_ap = dram["xn"], dram["xs"]
    brd, crd = dram["brd"], dram["crd"]
    cosd, sind, nsind = dram["cosd"], dram["sind"], dram["nsind"]
    cld, ncld, clfd = dram["cld"], dram["ncld"], dram["clfd"]
    bcd = dram["bcd"]
    gd, ggd = dram["gd"], dram["ggd"]
    cc1i, cc1o = dram["cc1i"], dram["cc1o"]
    cc2i, cc2o = dram["cc2i"], dram["cc2o"]
    cos3 = cosd.rearrange("h (k t) -> h k t", k=32)
    sin3 = sind.rearrange("h (k t) -> h k t", k=32)
    nsin3 = nsind.rearrange("h (k t) -> h k t", k=32)

    S.add("sync", lambda q: q.dma_start(out=sb3(Bf["xn"], 16),
                                        in_=chunk_ap(xn_ap, 0, 2048, 0, TOK)),
          reads=["d:xn"], writes=["xn"], dma=True, extra_waits=list(waits))

    wbuf = [Bf["wob0"], Bf["wob1"]]

    def proj_cols(i):
        if i < 32:
            return 4096 + i * 128, 128
        if i == 32:
            return 8192, 128
        return 8320, 64

    for i in range(34):
        c0, mw = proj_cols(i)
        wb = wbuf[i % 2]
        wtk = "wob%d" % (i % 2)
        S.add("sync", lambda q, c0=c0, mw=mw, wb=wb:
              q.dma_start(out=sb3(wb[:, 0:16 * mw], 16),
                          in_=chunk_ap(win_ap, 0, 2048, c0, mw)),
              writes=[wtk], dma=True)
        bank = (i % 2) * 2
        for h in range(2):
            ptk = "p%d" % (bank + h)

            def mmfn(t, mw=mw, wb=wb, bank=bank, h=h):
                for k in range(16):
                    ins = t.matmul(pb[bank + h].ap()[0:mw, 0:512],
                                   wb[:, k * mw:(k + 1) * mw],
                                   Bf["xn"][:, k * 1024 + h * 512:
                                            k * 1024 + h * 512 + 512],
                                   start=(k == 0), stop=(k == 15))
                return ins
            S.add("tensor", mmfn, reads=[wtk, "xn"], writes=[ptk])
            if i < 32:
                S.add("scalar", lambda s, bank=bank, h=h:
                      s.activation(Bf["stg"][:, h * 512:(h + 1) * 512],
                                   pb[bank + h].ap()[0:128, 0:512], AF.Identity),
                      reads=[ptk], writes=["stg%d" % h])
                S.add("gpsimd", lambda g, i=i, h=h:
                      g.dma_start(out=xs_ap[i * 128:(i + 1) * 128,
                                            h * 512:(h + 1) * 512],
                                  in_=Bf["stg"][:, h * 512:(h + 1) * 512]),
                      reads=["stg%d" % h], writes=["d:xs"], dma=True)
            elif i == 32:
                S.add("scalar", lambda s, bank=bank, h=h:
                      s.activation(Bf["bc"][:, h * 512:(h + 1) * 512],
                                   pb[bank + h].ap()[0:128, 0:512], AF.Identity),
                      reads=[ptk], writes=["bc"])
            else:
                S.add("scalar", lambda s, bank=bank, h=h:
                      s.activation(Bf["dtw"][0:64, h * 512:(h + 1) * 512],
                                   pb[bank + h].ap()[0:64, 0:512], AF.Identity,
                                   bias=dtb[0:64, 0:1]),
                      reads=[ptk], writes=["dtw"])

    dt0 = Bf["dtw"][0:64, :]
    dtv = Bf["dts"][0:64, :]
    lg = Bf["clw"][0:64, :]
    clv = Bf["cls"][0:64, :]
    clfv = Bf["clf"][0:64, :]
    s64 = Bf["s64"]
    lrow = Bf["lrow"]
    S.add("gpsimd", lambda g: g.memset(Bf["onesd"][0:64, :], 1.0),
          writes=["onesd"])
    S.add("vector", lambda v: v.memset(Bf["lrow"][:, 72:73], HPI),
          writes=["hpic"])
    S.add("scalar", lambda s: s.activation(s64[0:64, :], dt0, AF.Abs),
          reads=["dtw"], writes=["s64"])
    S.add("scalar", lambda s: s.activation(s64[0:64, :], s64[0:64, :], AF.Exp,
                                           scale=-1.0),
          reads=["s64"], writes=["s64"])
    S.add("scalar", lambda s: s.activation(s64[0:64, :], s64[0:64, :], AF.Ln,
                                           bias=1.0),
          reads=["s64"], writes=["s64"])
    S.add("vector", lambda v: v.tensor_scalar(dtv, dt0, 0.0, None, ALU.max),
          reads=["dtw"], writes=["dts"])
    S.add("vector", lambda v: v.tensor_tensor(dtv, dtv, s64[0:64, :], ALU.add),
          reads=["dts", "s64"], writes=["dts"])
    S.add("vector", lambda v: v.tensor_reduce(lrow[0:64, 0:1], dtv, AX, ALU.add),
          reads=["dts"], writes=["lrow0"])
    S.add("gpsimd", lambda g: g.dma_start(out=cc1i, in_=lrow[0:64, 0:1]),
          reads=["lrow0"], writes=["d:cc1i"], dma=True)
    S.add("gpsimd", lambda g: g.collective_compute(
        "AllGather", ALU.bypass, replica_groups=[[0, 1], [2, 3], [4, 5], [6, 7]],
        ins=[cc1i], outs=[cc1o]),
        reads=["d:cc1i"], writes=["d:cc1o"])
    S.add("gpsimd", lambda g: g.dma_start(out=lrow[0:64, 3:4],
                                          in_=cc1o[0:64, 0:1]),
          reads=["d:cc1o"], writes=["lrow3"], dma=True)
    S.add("scalar", lambda s: s.activation(lrow[0:64, 1:2], alog[0:64, 0:1],
                                           AF.Exp),
          writes=["lrow1"])
    S.add("vector", lambda v: v.tensor_scalar(lrow[0:64, 2:3], lrow[0:64, 1:2],
                                              -1.0, None, ALU.mult),
          reads=["lrow1"], writes=["lrow2"])
    S.add("vector", lambda v: v.tensor_scalar(lg, dtv, lrow[0:64, 2:3], None,
                                              ALU.mult),
          reads=["dts", "lrow2"], writes=["clw"])
    S.add("vector", lambda v: v.tensor_tensor(lrow[0:64, 4:5], lrow[0:64, 3:4],
                                              is1[0:64, 0:1], ALU.mult),
          reads=["lrow3"], writes=["lrow4"])
    S.add("vector", lambda v: v.tensor_tensor_scan(
        Bf["dtw"][0:64, :], Bf["onesd"][0:64, :], dtv, lrow[0:64, 4:5],
        ALU.mult, ALU.add),
        reads=["onesd", "dts", "lrow4", "dtw"], writes=["dtw"])
    for c in range(NC):
        S.add("vector", lambda v, c=c: v.tensor_tensor_scan(
            clv[:, c * L:(c + 1) * L], Bf["onesd"][0:64, c * L:(c + 1) * L],
            lg[:, c * L:(c + 1) * L], 0.0, ALU.mult, ALU.add),
            reads=["onesd", "clw"], writes=["cls"])
    S.add("vector", lambda v: v.tensor_tensor_scan(
        clfv, Bf["onesd"][0:64, :], lg, 0.0, ALU.mult, ALU.add),
        reads=["onesd", "clw"], writes=["clf"])
    S.add("vector", lambda v: v.tensor_scalar(s64[0:64, :], clv, -1.0, None,
                                              ALU.mult),
          reads=["cls"], writes=["s64"])
    S.add("gpsimd", lambda g: g.dma_start(out=cld, in_=clv),
          reads=["cls"], writes=["d:cld"], dma=True)
    S.add("gpsimd", lambda g: g.dma_start(out=ncld, in_=s64[0:64, :]),
          reads=["s64"], writes=["d:ncld"], dma=True)
    S.add("gpsimd", lambda g: g.dma_start(out=clfd, in_=clfv),
          reads=["clf"], writes=["d:clfd"], dma=True)

    for c in range(NC):
        S.add("tensor", lambda t, c=c: t.transpose(
            pb[4].ap()[0:128, 0:64], Bf["dts"][0:64, c * L:(c + 1) * L],
            identsb[0:64, 0:64]),
            reads=["dts"], writes=["p4"])
        S.add("vector", lambda v, c=c: v.tensor_copy(
            Bf["dtT"][:, c * 64:(c + 1) * 64], pb[4].ap()[0:128, 0:64]),
            reads=["p4"], writes=["dtT"])
        S.add("tensor", lambda t, c=c: t.transpose(
            pb[5].ap()[0:128, 0:64], Bf["cls"][0:64, c * L:(c + 1) * L],
            identsb[0:64, 0:64]),
            reads=["cls"], writes=["p5"])
        S.add("vector", lambda v, c=c: v.tensor_copy(
            Bf["ecT"][:, c * 64:(c + 1) * 64], pb[5].ap()[0:128, 0:64]),
            reads=["p5"], writes=["ecT"])
        S.add("gpsimd", lambda g, c=c: g.dma_start(
            out=lrow[0:1, 8:72],
            in_=cld[0:64, c * L + 127:c * L + 128].rearrange("h a -> a h"),
            allow_slow_non_contiguous=True),
            reads=["d:cld"], writes=["lrow8"], dma=True)
        S.add("tensor", lambda t: t.matmul(pb[6].ap()[:, 0:64], onesrow,
                                           lrow[0:1, 8:72], start=True, stop=True),
              reads=["lrow8"], writes=["p6"])
        S.add("vector", lambda v, c=c: v.tensor_tensor(
            Bf["dcT"][:, c * 64:(c + 1) * 64], pb[6].ap()[:, 0:64],
            Bf["ecT"][:, c * 64:(c + 1) * 64], ALU.subtract),
            reads=["p6", "ecT"], writes=["dcT"])
        S.add("gpsimd", lambda g, c=c: g.dma_start(
            out=Bf["lamr"][0:2, 0:32],
            in_=cld.rearrange("(p q) t -> q p t", q=2)[:, :, c * L + 127:c * L + 128],
            allow_slow_non_contiguous=True),
            reads=["d:cld"], writes=["lamr"], dma=True)
        S.add("scalar", lambda s: s.activation(Bf["lamr"][0:2, 0:32],
                                               Bf["lamr"][0:2, 0:32], AF.Exp),
              reads=["lamr"], writes=["lamr"])
        S.add("tensor", lambda t: t.matmul(pb[7].ap()[:, 0:32], qsel,
                                           Bf["lamr"][0:2, 0:32],
                                           start=True, stop=True),
              reads=["lamr"], writes=["p7"])
        S.add("vector", lambda v, c=c: v.tensor_copy(
            Bf["lamb"][:, c * 32:(c + 1) * 32], pb[7].ap()[:, 0:32]),
            reads=["p7"], writes=["lamb"])
    S.add("scalar", lambda s: s.activation(Bf["ecT"][:, :], Bf["ecT"][:, :],
                                           AF.Exp),
          reads=["ecT"], writes=["ecT"])
    S.add("scalar", lambda s: s.activation(Bf["dcT"][:, :], Bf["dcT"][:, :],
                                           AF.Exp),
          reads=["dcT"], writes=["dcT"])

    S.add("scalar", lambda s: s.square(Bf["sqs"][:, :], Bf["bc"][:, :]),
          reads=["bc"], writes=["sqs"])
    for h in range(2):
        S.add("tensor", lambda t, h=h: t.matmul(
            pb[6].ap()[0:1, 0:512], ones128[0:64, 0:1],
            Bf["sqs"][0:64, h * 512:(h + 1) * 512], start=True, stop=True),
            reads=["sqs"], writes=["p6"])
        S.add("vector", lambda v, h=h: v.tensor_scalar(
            Bf["rmr"][0:1, h * 512:(h + 1) * 512], pb[6].ap()[0:1, 0:512],
            1.0 / 64, None, ALU.mult),
            reads=["p6"], writes=["rmrB"])
        S.add("tensor", lambda t, h=h: t.matmul(
            pb[7].ap()[0:1, 0:512], ones128[64:128, 0:1],
            Bf["sqs"][64:128, h * 512:(h + 1) * 512], start=True, stop=True),
            reads=["sqs"], writes=["p7"])
        S.add("vector", lambda v, h=h: v.tensor_scalar(
            s64[0:1, h * 512:(h + 1) * 512], pb[7].ap()[0:1, 0:512],
            1.0 / 64, None, ALU.mult),
            reads=["p7"], writes=["rmrC"])

    def newton_rstd(rowtk, row_ap, scr_ap, scr_alias=()):
        S.add("vector", lambda v: v.tensor_scalar(row_ap, row_ap, 1e-6, None,
                                                  ALU.add),
              reads=[rowtk], writes=[rowtk])
        S.add("vector", lambda v: v.reciprocal(scr_ap, row_ap),
              reads=[rowtk], writes=["nsc"] + list(scr_alias))
        S.add("scalar", lambda s: s.sqrt(scr_ap, scr_ap),
              reads=["nsc"], writes=["nsc"])
        S.add("vector", lambda v: v.tensor_tensor(row_ap, row_ap, scr_ap,
                                                  ALU.mult),
              reads=[rowtk, "nsc"], writes=[rowtk])
        S.add("vector", lambda v: v.tensor_tensor(row_ap, row_ap, scr_ap,
                                                  ALU.mult),
              reads=[rowtk, "nsc"], writes=[rowtk])
        S.add("vector", lambda v: v.tensor_scalar(row_ap, row_ap, -0.5, 1.5,
                                                  ALU.mult, ALU.add),
              reads=[rowtk], writes=[rowtk])
        S.add("vector", lambda v: v.tensor_tensor(row_ap, row_ap, scr_ap,
                                                  ALU.mult),
              reads=[rowtk, "nsc"], writes=[rowtk])

    newton_rstd("rmrB", Bf["rmr"][0:1, 0:1024], Bf["stg"][0:1, 0:1024],
                scr_alias=["stg0", "stg1"])
    newton_rstd("rmrC", s64[0:1, 0:1024], Bf["stg"][0:1, 0:1024])
    for h in range(2):
        S.add("tensor", lambda t, h=h: t.matmul(
            pb[6].ap()[:, 0:512], onesrow,
            Bf["rmr"][0:1, h * 512:(h + 1) * 512], start=True, stop=True),
            reads=["rmrB"], writes=["p6"])
        S.add("vector", lambda v, h=h: v.tensor_tensor(
            Bf["bc"][0:64, h * 512:(h + 1) * 512],
            Bf["bc"][0:64, h * 512:(h + 1) * 512],
            pb[6].ap()[0:64, 0:512], ALU.mult),
            reads=["p6", "bc"], writes=["bc"])
        S.add("tensor", lambda t, h=h: t.matmul(
            pb[7].ap()[:, 0:512], onesrow,
            s64[0:1, h * 512:(h + 1) * 512], start=True, stop=True),
            reads=["rmrC"], writes=["p7"])
        S.add("vector", lambda v, h=h: v.tensor_tensor(
            Bf["bc"][64:128, h * 512:(h + 1) * 512],
            Bf["bc"][64:128, h * 512:(h + 1) * 512],
            pb[7].ap()[64:128, 0:512], ALU.mult),
            reads=["p7", "bc"], writes=["bc"])
    S.add("gpsimd", lambda g: g.dma_start(out=bcd, in_=Bf["bc"][:, :]),
          reads=["bc"], writes=["d:bcd"], dma=True)
    for nm, rows in (("B2", [0, 32, 0, 32]), ("B2s", [32, 0, 32, 0]),
                     ("C2", [64, 96, 64, 96]), ("C2s", [96, 64, 96, 64])):
        for blk in range(4):
            S.add("sync", lambda q, nm=nm, blk=blk, r0=rows[blk]: q.dma_start(
                out=Bf[nm][blk * 32:(blk + 1) * 32, :],
                in_=bcd[r0:r0 + 32, :]),
                reads=["d:bcd"], writes=[nm], dma=True)

    ph0 = Bf["ph0"]
    ph1 = Bf["ph1"]
    ph2 = env.arena[:, 17408:21504]
    PI = float(np.pi)
    A = ph0[0:64, 0:4096]
    Bx = ph1[0:64, 0:4096]
    Cx = ph2[0:64, 0:4096]
    for c in range(NC):
        cslice = Bf["dtw"][0:64, c * L:(c + 1) * L]
        al = (["wob0", "wob1"] if c == 0 else [])
        al2 = (["dts", "clw", "cls", "clf"] if c == 0 else [])
        S.add("vector", lambda v, cslice=cslice: v.tensor_tensor(
            sb3(A, 32),
            freqs[0:64, :].unsqueeze(2).broadcast_to([64, 32, L]),
            cslice.unsqueeze(1).broadcast_to([64, 32, L]), ALU.mult),
            reads=["dtw"], writes=["ph0"] + al)
        S.add("vector", lambda v: v.tensor_scalar(Bx, A, INV2PI, None, ALU.mult),
              reads=["ph0"], writes=["ph1"] + al)
        S.add("vector", lambda v: v.tensor_copy(Cx.bitcast(I32), Bx),
              reads=["ph1"], writes=["ph2"] + al2)
        S.add("vector", lambda v: v.tensor_copy(Bx, Cx.bitcast(I32)),
              reads=["ph2"], writes=["ph1"])
        S.add("vector", lambda v: v.scalar_tensor_tensor(
            Cx, Bx, -C1, A, ALU.mult, ALU.add),
            reads=["ph1", "ph0"], writes=["ph2"])
        S.add("vector", lambda v: v.scalar_tensor_tensor(
            Cx, Bx, -C2, Cx, ALU.mult, ALU.add),
            reads=["ph1", "ph2"], writes=["ph2"])
        S.add("vector", lambda v: v.tensor_scalar(Bx, Cx, PI, None, ALU.is_gt),
              reads=["ph2"], writes=["ph1"])
        S.add("vector", lambda v: v.scalar_tensor_tensor(
            Cx, Bx, -C1, Cx, ALU.mult, ALU.add),
            reads=["ph1", "ph2"], writes=["ph2"])
        S.add("vector", lambda v: v.scalar_tensor_tensor(
            Cx, Bx, -C2, Cx, ALU.mult, ALU.add),
            reads=["ph1", "ph2"], writes=["ph2"])
        S.add("vector", lambda v: v.tensor_scalar(Bx, Cx, -PI, None, ALU.is_lt),
              reads=["ph2"], writes=["ph1"])
        S.add("vector", lambda v: v.scalar_tensor_tensor(
            Cx, Bx, C1, Cx, ALU.mult, ALU.add),
            reads=["ph1", "ph2"], writes=["ph2"])
        S.add("vector", lambda v: v.scalar_tensor_tensor(
            Cx, Bx, C2, Cx, ALU.mult, ALU.add),
            reads=["ph1", "ph2"], writes=["ph2"])
        S.add("scalar", lambda s: s.activation(A, Cx, AF.Abs),
              reads=["ph2"], writes=["ph0"])
        S.add("scalar", lambda s: s.activation(Bx, Cx, AF.Sin),
              reads=["ph2"], writes=["ph1"])
        S.add("scalar", lambda s: s.activation(A, A, AF.Sin, scale=-1.0,
                                               bias=Bf["lrow"][0:64, 72:73]),
              reads=["ph0", "hpic"], writes=["ph0"])
        S.add("vector", lambda v: v.tensor_scalar(Cx, Bx, -1.0, None, ALU.mult),
              reads=["ph1"], writes=["ph2"])
        S.add("gpsimd", lambda g, c=c: g.dma_start(
            out=cos3[:, :, c * L:(c + 1) * L], in_=sb3(A, 32)),
            reads=["ph0"], writes=["d:cosd"], dma=True)
        S.add("gpsimd", lambda g, c=c: g.dma_start(
            out=sin3[:, :, c * L:(c + 1) * L], in_=sb3(Bx, 32)),
            reads=["ph1"], writes=["d:sind"], dma=True)
        S.add("gpsimd", lambda g, c=c: g.dma_start(
            out=nsin3[:, :, c * L:(c + 1) * L], in_=sb3(Cx, 32)),
            reads=["ph2"], writes=["d:nsind"], dma=True)

    def rope_pair(p):
        for blk in range(4):
            q_ = blk // 2
            S.add("sync", lambda q, p=p, blk=blk, q_=q_: q.dma_start(
                out=Bf["chm"][blk * 32:(blk + 1) * 32, :],
                in_=cos3[2 * p + q_:2 * p + q_ + 1, :, :].squeeze(0)),
                reads=["d:cosd"],
                writes=(["chm", "ph1"] if p == 0 and blk == 0 else ["chm"]),
                dma=True)
        for blk in range(4):
            dd3 = nsin3 if blk % 2 == 0 else sin3
            tkn = "d:nsind" if blk % 2 == 0 else "d:sind"
            q_ = blk // 2
            S.add("sync", lambda q, p=p, dd3=dd3, blk=blk, q_=q_: q.dma_start(
                out=Bf["shm"][blk * 32:(blk + 1) * 32, :],
                in_=dd3[2 * p + q_:2 * p + q_ + 1, :, :].squeeze(0)),
                reads=[tkn], writes=(["shm", "ph1"] if p == 0 and blk == 0
                                     else ["shm"]), dma=True)
        for (base, basw, bias, biasw, dst) in (
                ("B2", "B2s", BbT, BbTs, "brt"), ("C2", "C2s", CbT, CbTs, "crt")):
            S.add("vector", lambda v, base=base, bias=bias, p=p:
                  v.tensor_scalar(Bf["t1s"], Bf[base], bias[:, p:p + 1], None,
                                  ALU.add),
                  reads=[base],
                  writes=(["t1s", "ph1"] if p == 0 and base == "B2" else ["t1s"]))
            S.add("vector", lambda v, basw=basw, biasw=biasw, p=p:
                  v.tensor_scalar(Bf["t2s"], Bf[basw], biasw[:, p:p + 1], None,
                                  ALU.add),
                  reads=[basw],
                  writes=(["t2s", "ph1"] if p == 0 and basw == "B2s" else ["t2s"]))
            S.add("vector", lambda v: v.tensor_tensor(
                Bf["t1s"], Bf["t1s"], Bf["chm"], ALU.mult),
                reads=["t1s", "chm"], writes=["t1s"])
            S.add("vector", lambda v: v.tensor_tensor(
                Bf["t2s"], Bf["t2s"], Bf["shm"], ALU.mult),
                reads=["t2s", "shm"], writes=["t2s"])
            S.add("vector", lambda v, dst=dst: v.tensor_tensor(
                Bf[dst], Bf["t1s"], Bf["t2s"], ALU.add),
                reads=["t1s", "t2s"], writes=[dst])

    S.add("vector", lambda v: v.memset(Bf["Hall"][:, :], 0.0),
          writes=["Hall", "s64", "onesd"])
    for p in range(NP):
        rope_pair(p)
        S.add("gpsimd", lambda g, p=p: g.dma_start(
            out=brd[p * 128:(p + 1) * 128, :], in_=Bf["brt"]),
            reads=["brt"], writes=["d:brd"], dma=True)
        S.add("gpsimd", lambda g, p=p: g.dma_start(
            out=crd[p * 128:(p + 1) * 128, :], in_=Bf["crt"]),
            reads=["crt"], writes=["d:crd"], dma=True)
        S.add("sync", lambda q, p=p: q.dma_start(
            out=Bf["xsp"], in_=xs_ap[p * 128:(p + 1) * 128, :]),
            reads=["d:xs"], writes=["xsp"], dma=True)
        for c in range(NC):
            S.add("tensor", lambda t, c=c: t.transpose(
                pb[4].ap()[0:128, 0:128], Bf["xsp"][:, c * L:(c + 1) * L],
                identsb),
                reads=["xsp"], writes=["p4"])
            for q_ in range(2):
                S.add("vector", lambda v, c=c, p=p, q_=q_: v.tensor_scalar(
                    Bf["ut"][:, q_ * 64:(q_ + 1) * 64],
                    pb[4].ap()[0:128, q_ * 64:(q_ + 1) * 64],
                    Bf["dtT"][:, c * 64 + 2 * p + q_:c * 64 + 2 * p + q_ + 1],
                    None, ALU.mult),
                    reads=["p4", "dtT"], writes=["ut"])
            S.add("tensor", lambda t, c=c: t.transpose(
                pb[5].ap()[0:128, 0:128], Bf["brt"][:, c * L:(c + 1) * L],
                identsb),
                reads=["brt"], writes=["p5"])
            for q_ in range(2):
                S.add("vector", lambda v, c=c, p=p, q_=q_: v.tensor_scalar(
                    Bf["bdt"][:, q_ * 64:(q_ + 1) * 64],
                    pb[5].ap()[0:128, q_ * 64:(q_ + 1) * 64],
                    Bf["dcT"][:, c * 64 + 2 * p + q_:c * 64 + 2 * p + q_ + 1],
                    None, ALU.mult),
                    reads=["p5", "dcT"], writes=["bdt"])
            for q_ in range(2):
                S.add("tensor", lambda t, q_=q_: t.matmul(
                    pb[6].ap()[q_ * 64:(q_ + 1) * 64, 0:64],
                    Bf["bdt"][:, q_ * 64:(q_ + 1) * 64],
                    Bf["ut"][:, q_ * 64:(q_ + 1) * 64],
                    start=True, stop=True,
                    tile_position=(0, q_ * 64)),
                    reads=["bdt", "ut"], writes=["p6"])
            S.add("vector", lambda v, c=c, p=p: v.scalar_tensor_tensor(
                Bf["Hall"][:, p * 64:(p + 1) * 64],
                Bf["Hall"][:, p * 64:(p + 1) * 64],
                Bf["lamb"][:, c * 32 + p:c * 32 + p + 1],
                pb[6].ap()[:, 0:64], ALU.mult, ALU.add),
                reads=["Hall", "lamb", "p6"], writes=["Hall"])

    S.add("gpsimd", lambda g: g.dma_start(out=cc2i, in_=Bf["Hall"][:, :]),
          reads=["Hall"], writes=["d:cc2i"], dma=True)
    S.add("gpsimd", lambda g: g.collective_compute(
        "AllGather", ALU.bypass, replica_groups=[[0, 1], [2, 3], [4, 5], [6, 7]],
        ins=[cc2i], outs=[cc2o]),
        reads=["d:cc2i"], writes=["d:cc2o"])
    S.add("gpsimd", lambda g: g.dma_start(out=Bf["H0"][:, :],
                                          in_=cc2o[0:128, :]),
          reads=["d:cc2o"], writes=["H0", "C2", "C2s"], dma=True)
    S.add("vector", lambda v: v.tensor_scalar(Bf["H0"][:, :], Bf["H0"][:, :],
                                              is1[:, 0:1], None, ALU.mult),
          reads=["H0"], writes=["H0"])

    S.add("sync", lambda q: q.dma_start(out=Bf["clpk"][0:1, :],
                                        in_=ext["ones1_d"]),
          writes=["clpk1r"], dma=True)
    S.add("sync", lambda q: q.dma_start(out=Bf["rpk"][1:3, :],
                                        in_=ext["rpkc_d"]),
          writes=["rpkc"], dma=True)
    for p in range(NP):
        S.add("sync", lambda q, p=p: q.dma_start(
            out=Bf["xsp"], in_=xs_ap[p * 128:(p + 1) * 128, :]),
            reads=["d:xs"], writes=["xsp"], dma=True)
        S.add("sync", lambda q, p=p: q.dma_start(
            out=Bf["brt"], in_=brd[p * 128:(p + 1) * 128, :]),
            reads=["d:brd"], writes=["brt"], dma=True)
        S.add("sync", lambda q, p=p: q.dma_start(
            out=Bf["crt"], in_=crd[p * 128:(p + 1) * 128, :]),
            reads=["d:crd"], writes=["crt"], dma=True)
        S.add("vector", lambda v, p=p: v.tensor_copy(
            Bf["Hc"][:, :], Bf["H0"][:, p * 64:(p + 1) * 64]),
            reads=["H0"], writes=["Hc"])
        for c in range(NC):
            S.add("tensor", lambda t, c=c: t.transpose(
                pb[4].ap()[0:128, 0:128], Bf["xsp"][:, c * L:(c + 1) * L],
                identsb),
                reads=["xsp"], writes=["p4"])
            for q_ in range(2):
                S.add("vector", lambda v, c=c, p=p, q_=q_: v.tensor_scalar(
                    Bf["ut"][:, q_ * 64:(q_ + 1) * 64],
                    pb[4].ap()[0:128, q_ * 64:(q_ + 1) * 64],
                    Bf["dtT"][:, c * 64 + 2 * p + q_:c * 64 + 2 * p + q_ + 1],
                    None, ALU.mult),
                    reads=["p4", "dtT"], writes=["ut"])
            for q_ in range(2):
                S.add("tensor", lambda t, c=c, q_=q_: t.matmul(
                    pb[0].ap()[:, q_ * 128:(q_ + 1) * 128],
                    Bf["brt"][q_ * 64:(q_ + 1) * 64, c * L:(c + 1) * L],
                    Bf["crt"][q_ * 64:(q_ + 1) * 64, c * L:(c + 1) * L],
                    start=True, stop=True),
                    reads=["brt", "crt"], writes=["p0"])
            S.add("gpsimd", lambda g, p=p, c=c: g.dma_start(
                out=Bf["clpk"][1:3, 0:128],
                in_=ncld[2 * p:2 * p + 2, c * L:(c + 1) * L]),
                reads=["d:ncld"], writes=["clpk2"], dma=True)
            S.add("gpsimd", lambda g, p=p, c=c: g.dma_start(
                out=Bf["rpk"][0:1, 0:256],
                in_=cld[2 * p:2 * p + 2, c * L:(c + 1) * L]),
                reads=["d:cld"], writes=["rpk0"], dma=True)
            S.add("tensor", lambda t: t.matmul(
                pb[1].ap()[:, 0:256], Bf["clpk"][0:3, 0:128],
                Bf["rpk"][0:3, 0:256], start=True, stop=True),
                reads=["clpk1r", "clpk2", "rpk0", "rpkc"], writes=["p1"])
            S.add("vector", lambda v: v.tensor_scalar(
                Bf["et"], pb[1].ap()[:, 0:256], 0.0, None, ALU.min),
                reads=["p1"], writes=["et"])
            S.add("scalar", lambda s: s.activation(Bf["et"], Bf["et"], AF.Exp),
                  reads=["et"], writes=["et"])
            S.add("vector", lambda v, c=c, p=p: v.tensor_tensor(
                Bf["gtl"], pb[0].ap()[:, 0:256], Bf["et"], ALU.mult),
                reads=["p0", "et"],
                writes=(["gtl", "ph2"] if p == 0 and c == 0 else ["gtl"]))
            S.add("vector", lambda v: v.tensor_tensor(
                Bf["gtl"], Bf["gtl"], triu2, ALU.mult),
                reads=["gtl"], writes=["gtl"])
            for q_ in range(2):
                S.add("tensor", lambda t, q_=q_: t.matmul(
                    pb[2].ap()[:, q_ * 64:(q_ + 1) * 64],
                    Bf["gtl"][:, q_ * 128:(q_ + 1) * 128],
                    Bf["ut"][:, q_ * 64:(q_ + 1) * 64],
                    start=True, stop=True),
                    reads=["gtl", "ut"], writes=["p2"])
                S.add("tensor", lambda t, c=c, q_=q_: t.matmul(
                    pb[3].ap()[:, q_ * 64:(q_ + 1) * 64],
                    Bf["crt"][q_ * 64:(q_ + 1) * 64, c * L:(c + 1) * L],
                    Bf["Hc"][q_ * 64:(q_ + 1) * 64, :],
                    start=True, stop=True),
                    reads=["crt", "Hc"], writes=["p3"])
            for q_ in range(2):
                S.add("vector", lambda v, q_=q_: v.tensor_copy(
                    Bf["ytl"][:, q_ * 64:(q_ + 1) * 64],
                    pb[2].ap()[:, q_ * 64:(q_ + 1) * 64]),
                    reads=["p2"], writes=["ytl"])
                S.add("vector", lambda v, c=c, p=p, q_=q_: v.scalar_tensor_tensor(
                    Bf["ytl"][:, q_ * 64:(q_ + 1) * 64],
                    pb[3].ap()[:, q_ * 64:(q_ + 1) * 64],
                    Bf["ecT"][:, c * 64 + 2 * p + q_:c * 64 + 2 * p + q_ + 1],
                    Bf["ytl"][:, q_ * 64:(q_ + 1) * 64], ALU.mult, ALU.add),
                    reads=["p3", "ecT", "ytl"], writes=["ytl"])
            S.add("tensor", lambda t: t.transpose(
                pb[5].ap()[0:128, 0:128], Bf["ytl"][:, 0:128], identsb),
                reads=["ytl"], writes=["p5"])
            S.add("vector", lambda v, c=c, p=p: v.tensor_copy(
                Bf["ysc"][:, c * L:(c + 1) * L], pb[5].ap()[0:128, 0:128]),
                reads=["p5"],
                writes=(["ysc", "ph2"] if p == 0 and c == 0 else ["ysc"]))
            S.add("tensor", lambda t, c=c: t.transpose(
                pb[4].ap()[0:128, 0:128], Bf["brt"][:, c * L:(c + 1) * L],
                identsb),
                reads=["brt"], writes=["p4"])
            for q_ in range(2):
                S.add("vector", lambda v, c=c, p=p, q_=q_: v.tensor_scalar(
                    Bf["bdt"][:, q_ * 64:(q_ + 1) * 64],
                    pb[4].ap()[0:128, q_ * 64:(q_ + 1) * 64],
                    Bf["dcT"][:, c * 64 + 2 * p + q_:c * 64 + 2 * p + q_ + 1],
                    None, ALU.mult),
                    reads=["p4", "dcT"], writes=["bdt"])
            for q_ in range(2):
                S.add("tensor", lambda t, q_=q_: t.matmul(
                    pb[6].ap()[q_ * 64:(q_ + 1) * 64, 0:64],
                    Bf["bdt"][:, q_ * 64:(q_ + 1) * 64],
                    Bf["ut"][:, q_ * 64:(q_ + 1) * 64],
                    start=True, stop=True,
                    tile_position=(0, q_ * 64)),
                    reads=["bdt", "ut"], writes=["p6"])
            S.add("vector", lambda v, c=c, p=p: v.scalar_tensor_tensor(
                Bf["Hc"][:, :], Bf["Hc"][:, :],
                Bf["lamb"][:, c * 32 + p:c * 32 + p + 1],
                pb[6].ap()[:, 0:64], ALU.mult, ALU.add),
                reads=["Hc", "lamb", "p6"], writes=["Hc"])
        S.add("vector", lambda v, p=p: v.tensor_scalar(
            Bf["t1s"], Bf["xsp"], DrepT[:, p:p + 1], None, ALU.mult),
            reads=["xsp"], writes=["t1s"])
        S.add("vector", lambda v: v.scalar_tensor_tensor(
            Bf["ysc"], Bf["ysc"], 0.125, Bf["t1s"], ALU.mult, ALU.add),
            reads=["ysc", "t1s"], writes=["ysc"])
        S.add("sync", lambda q, p=p: q.dma_start(
            out=sb3(Bf["zw"], 16),
            in_=chunk_ap(win_ap, 0, 2048, p * 128, 128)),
            writes=(["zw", "ph0"] if p == 0 else ["zw"]), dma=True)
        for h in range(2):
            S.add("tensor", lambda t, h=h: [
                t.matmul(pb[h].ap()[:, 0:512],
                         Bf["zw"][:, k * 128:(k + 1) * 128],
                         Bf["xn"][:, k * 1024 + h * 512:k * 1024 + h * 512 + 512],
                         start=(k == 0), stop=(k == 15))
                for k in range(16)][-1],
                  reads=["zw", "xn"], writes=["p%d" % h])
            S.add("scalar", lambda s, h=h, p=p: s.activation(
                Bf["zsl"][:, h * 512:(h + 1) * 512], pb[h].ap()[:, 0:512],
                AF.Silu),
                reads=["p%d" % h],
                writes=(["zsl", "ph2"] if p == 0 and h == 0 else ["zsl"]))
        S.add("vector", lambda v, p=p: v.tensor_tensor(Bf["gsc"], Bf["ysc"],
                                                  Bf["zsl"], ALU.mult),
              reads=["ysc", "zsl"],
              writes=(["gsc", "dtw"] if p == 0 else ["gsc"]))
        S.add("gpsimd", lambda g, p=p: g.dma_start(
            out=gd[p * 128:(p + 1) * 128, :], in_=Bf["gsc"]),
            reads=["gsc"], writes=["d:gd"], dma=True)
        S.add("scalar", lambda s: s.square(Bf["sqs"][:, :], Bf["gsc"]),
              reads=["gsc"], writes=["sqs"])
        for h in range(2):
            S.add("tensor", lambda t, h=h: t.matmul(
                pb[7].ap()[0:1, 0:512], ones128,
                Bf["sqs"][:, h * 512:(h + 1) * 512], start=True, stop=True),
                reads=["sqs"], writes=["p7"])
            if p == 0:
                S.add("vector", lambda v, h=h: v.tensor_copy(
                    Bf["rmr"][0:1, h * 512:(h + 1) * 512],
                    pb[7].ap()[0:1, 0:512]),
                    reads=["p7"], writes=["gsq"])
            else:
                S.add("vector", lambda v, h=h: v.tensor_tensor(
                    Bf["rmr"][0:1, h * 512:(h + 1) * 512],
                    Bf["rmr"][0:1, h * 512:(h + 1) * 512],
                    pb[7].ap()[0:1, 0:512], ALU.add),
                    reads=["p7", "gsq"], writes=["gsq"])

    S.add("vector", lambda v: v.tensor_scalar(
        Bf["rmr"][0:1, 0:1024], Bf["rmr"][0:1, 0:1024], 1.0 / 4096.0, None,
        ALU.mult),
        reads=["gsq"], writes=["gsq"])
    newton_rstd("gsq", Bf["rmr"][0:1, 0:1024], Bf["stg"][0:1, 0:1024])
    for h in range(2):
        S.add("tensor", lambda t, h=h: t.matmul(
            pb[6 + h].ap()[:, 0:512], onesrow,
            Bf["rmr"][0:1, h * 512:(h + 1) * 512], start=True, stop=True),
            reads=["gsq"], writes=["p%d" % (6 + h)])
    for p in range(NP):
        S.add("sync", lambda q, p=p: q.dma_start(
            out=Bf["gsc"], in_=gd[p * 128:(p + 1) * 128, :]),
            reads=["d:gd"], writes=["gsc"], dma=True)
        for h in range(2):
            S.add("vector", lambda v, h=h: v.tensor_tensor(
                Bf["gsc"][:, h * 512:(h + 1) * 512],
                Bf["gsc"][:, h * 512:(h + 1) * 512],
                pb[6 + h].ap()[:, 0:512], ALU.mult),
                reads=["gsc", "p%d" % (6 + h)], writes=["gsc"])
        S.add("vector", lambda v, p=p: v.tensor_scalar(
            Bf["gsc"], Bf["gsc"], nwrep[:, p:p + 1], None, ALU.mult),
            reads=["gsc"], writes=["gsc"])
        S.add("gpsimd", lambda g, p=p: g.dma_start(
            out=ggd[p * 128:(p + 1) * 128, :], in_=Bf["gsc"]),
            reads=["gsc"], writes=["d:ggd"], dma=True)

    for mg in range(4):
        for p in range(NP):
            wtk = "wob%d" % (p % 2)
            al9 = (["zw", "ph0"] if mg == 0 and p <= 1 else [])
            S.add("sync", lambda q, mg=mg, p=p: q.dma_start(
                out=wbuf[p % 2][:, 0:512],
                in_=wout_ap[p * 128:(p + 1) * 128, mg * 512:(mg + 1) * 512]),
                writes=[wtk] + al9, dma=True)
            S.add("sync", lambda q, p=p, mg=mg: q.dma_start(
                out=Bf["ggt"], in_=ggd[p * 128:(p + 1) * 128, :]),
                reads=["d:ggd"],
                writes=(["ggt", "ph2"] if mg == 0 and p == 0 else ["ggt"]),
                dma=True)
            for j in range(4):
                for h in range(2):
                    S.add("tensor", lambda t, p=p, j=j, h=h: t.matmul(
                        pb[j * 2 + h].ap()[:, 0:512],
                        wbuf[p % 2][:, j * 128:(j + 1) * 128],
                        Bf["ggt"][:, h * 512:(h + 1) * 512],
                        start=(p == 0), stop=(p == NP - 1)),
                        reads=[wtk, "ggt"], writes=["p%d" % (j * 2 + h)])
        for j in range(4):
            for h in range(2):
                n = (mg * 4 + j) * 2 + h
                dtile = mg * 4 + j
                S.add("gpsimd", lambda g, dtile=dtile, h=h, n=n: g.dma_start(
                    out=Bf["xres%d" % (n % 2)],
                    in_=xin_ap[dtile * 128:(dtile + 1) * 128,
                               h * 512:(h + 1) * 512]),
                    writes=["xres%d" % (n % 2), "B2"] if n <= 1 else
                    ["xres%d" % (n % 2)], dma=True)
                S.add("vector", lambda v, j=j, h=h, n=n: v.tensor_tensor(
                    Bf["tres%d" % (n % 2)], pb[j * 2 + h].ap()[:, 0:512],
                    Bf["xres%d" % (n % 2)], ALU.add),
                    reads=["p%d" % (j * 2 + h), "xres%d" % (n % 2)],
                    writes=["tres%d" % (n % 2), "bc"] if n <= 1 else
                    ["tres%d" % (n % 2)])
                S.add("gpsimd", lambda g, dtile=dtile, h=h, n=n: g.dma_start(
                    out=xout_ap[dtile * 128:(dtile + 1) * 128,
                                h * 512:(h + 1) * 512],
                    in_=Bf["tres%d" % (n % 2)]),
                    reads=["tres%d" % (n % 2)], writes=["d:xout"], dma=True)

    S.finalize(block)
    return S.barrier_ident()


def host_l0_inputs(W_in, W_out, dt_bias, A_log, D_param, B_bias, C_bias,
                   rope_freqs, norm_w, core):
    perm = np.concatenate([np.arange(0, 64, 2), np.arange(1, 64, 2)])
    W_in_p = np.array(W_in)
    W_in_p[:, 8192:8256] = W_in[:, 8192:8256][:, perm]
    W_in_p[:, 8256:8320] = W_in[:, 8256:8320][:, perm]
    BbT = np.zeros((128, 32), np.float32)
    CbT = np.zeros((128, 32), np.float32)
    for p in range(32):
        for q in range(2):
            h = 2 * p + q
            BbT[q * 64:q * 64 + 32, p] = B_bias[h, 0::2]
            BbT[q * 64 + 32:q * 64 + 64, p] = B_bias[h, 1::2]
            CbT[q * 64:q * 64 + 32, p] = C_bias[h, 0::2]
            CbT[q * 64 + 32:q * 64 + 64, p] = C_bias[h, 1::2]
    BbTs = np.concatenate([BbT[32:64], BbT[0:32], BbT[96:128], BbT[64:96]], 0)
    CbTs = np.concatenate([CbT[32:64], CbT[0:32], CbT[96:128], CbT[64:96]], 0)
    Drep = np.zeros((128, 32), np.float32)
    nwr = np.zeros((128, 32), np.float32)
    for p in range(32):
        Drep[0:64, p] = D_param[2 * p]
        Drep[64:128, p] = D_param[2 * p + 1]
        nwr[:, p] = norm_w[p * 128:(p + 1) * 128]
    col = np.zeros((128, 1), np.float32)
    col[0:64, 0] = dt_bias
    acol = np.zeros((128, 1), np.float32)
    acol[0:64, 0] = A_log
    fr = np.zeros((128, 32), np.float32)
    fr[0:64, :] = rope_freqs
    is1v = np.full((128, 1), float(core % 2), np.float32)
    tri = np.triu(np.ones((128, 128), np.float32))
    triu2 = np.concatenate([tri, tri], axis=1)
    qsel = np.zeros((2, 128), np.float32)
    qsel[0, 0:64] = 1.0
    qsel[1, 64:128] = 1.0
    rpkc = np.zeros((2, 256), np.float32)
    rpkc[0, 0:128] = 1.0
    rpkc[1, 128:256] = 1.0
    return {
        "l_rpkc": rpkc, "l_ones1": np.ones((1, 128), np.float32),
        "l_win": W_in_p.astype(np.float32),
        "l_wout": W_out.astype(np.float32),
        "l_dtb": col, "l_alog": acol, "l_Drep": Drep, "l_nwrep": nwr,
        "l_BbT": BbT, "l_BbTs": BbTs, "l_CbT": CbT, "l_CbTs": CbTs,
        "l_freqs": fr, "l_is1": is1v, "l_triu2": triu2, "l_qsel": qsel,
    }



"""L1 GQA causal attention block emitter (see plan.md).

xout = x + (Attn(LN(x)) @ Wo + bo)   [LN emitted separately by caller]

Uniform full-key program; causality via per-tile stt mask (per-core position
inputs); RoPE via DRAM block-swap re-read + sign-folded per-core host tables
(q tables pre-scaled by 1/8); per-head stabilizer c_qh = max_t||q_h/8|| *
max_t||k_h|| (exact-canceling); pair AllGather for K/V; denominator via
ones-column in AV lhsT; normalization via reciprocal + K=1 broadcast matmul.

PSUM banks: proj/out: 0..3 / 4..7 ping. attention: 0,1 scores; 2 AV; 3 bcast;
4,5 norm scratch; 6 transposes; 7 bias cols (0:32 = -c_qh).
"""


NQH = 32
NKV = 8
NTQ = 2
NTK = 16


def mask_needed(tqt, tkt):
    return not (128 * (tkt + 1) <= 512 * tqt)


MK_CUM = []
_c = 0
for _a in range(64):
    for _t in range(NTK):
        MK_CUM.append(_c)
        if mask_needed(_a % NTQ, _t):
            _c += 1

NRQ = 96


def emit_l1attn(env, block, tag, xin_ap, xout_ap, ext, dram, bufs, waits=()):
    pb = env.pb
    ones128 = env.sb["ones128"].ap()[:, :]
    onesrow = env.sb["onesrow"].ap()[:, :]

    s_xl = env.sem(tag + "xl")
    ds_w = DSem(env, tag + "w")
    s_pm = env.sem(tag + "pm")
    s_pe = env.sem(tag + "pe")
    s_qw = env.sem(tag + "qw")
    ds_sw = DSem(env, tag + "s")
    s_rp = env.sem(tag + "rp")
    s_kw = env.sem(tag + "kw")
    s_cc = env.sem(tag + "cc")
    s_nq = env.sem(tag + "nq")
    s_n2 = env.sem(tag + "n2")
    s_nr = env.sem(tag + "nr")
    s_nk = env.sem(tag + "nk")
    s_n2k = env.sem(tag + "n2k")
    s_sq2 = env.sem(tag + "sq2")
    s_bb = env.sem(tag + "bb")
    s_ng = env.sem(tag + "ng")
    s_kt = env.sem(tag + "kt")
    s_vt = env.sem(tag + "vt")
    s_tr = env.sem(tag + "tr")
    s_tv = env.sem(tag + "tv")
    s_vs = env.sem(tag + "vs")
    s_sm = env.sem(tag + "sm")
    s_ex = env.sem(tag + "ex")
    s_mk = env.sem(tag + "mk")
    s_av = env.sem(tag + "av")
    s_b2 = env.sem(tag + "b2")
    s_nm = env.sem(tag + "nm")
    s_ym = env.sem(tag + "ym")
    ds_wo = DSem(env, tag + "o")
    s_om = env.sem(tag + "om")
    s_or = env.sem(tag + "or")
    ds_xr = DSem(env, tag + "x")
    s_ad = env.sem(tag + "ad")
    ds_out = DSem(env, tag + "z")

    qsb = bufs["qsb"]
    ysb = bufs["ysb"]
    kvsb = bufs["kvsb"]
    wbuf = [bufs["wb0"], bufs["wb1"]]
    swt = bufs["swt"]
    kscr = bufs["kscr"]
    vtin = bufs["vtin"]
    vts = bufs["vts"]
    pt = [bufs["pt0"], bufs["pt1"]]
    osb = bufs["osb"]
    nrow = bufs["nrow"]
    sqs = bufs["sqs"]
    tres = [bufs["tres0"], bufs["tres1"]]
    xres = [bufs["xres0"], bufs["xres1"]]
    negc = bufs["negc"]
    onesw = bufs["onesw"]
    qcos, qsin = ext["qcos"], ext["qsin"]
    kcos, ksin = ext["kcos"], ext["ksin"]
    tqpos, tkpos = ext["tqpos"], ext["tkpos"]
    identsb = ext["ident"]
    bqc, bkc, bvc, boc = ext["bqc"], ext["bkc"], ext["bvc"], ext["boc"]
    wq_ap, wk_ap, wv_ap, wo_ap = ext["wq"], ext["wk"], ext["wv"], ext["wo"]
    xn_ap, q_ap = dram["xn"], dram["q"]
    kvin_ap, kvall_ap = dram["kvin"], dram["kvall"]
    xn_sb = ysb

    proj = [("q", wq_ap, m) for m in range(16)] + \
           [("k", wk_ap, m) for m in range(4)] + \
           [("v", wv_ap, m) for m in range(4)]

    def pdst(i):
        kind, wap, m = proj[i]
        if kind == "q":
            return qsb[:, m * 1024:(m + 1) * 1024], bqc[:, m:m + 1]
        if kind == "k":
            return kvsb[:, m * 1024:(m + 1) * 1024], bkc[:, m:m + 1]
        return kvsb[:, (4 + m) * 1024:(5 + m) * 1024], bvc[:, m:m + 1]

    omap = []
    _o = 0
    for a in range(64):
        if (a // NTQ) % 2 == 1:
            _o += 1
        omap.append(_o)

    def nrk(kvh):
        return NRQ + 10 * kvh

    @block.sync
    def _(q):
        for s, v in waits:
            q.wait_ge(s, v)
        q.dma_start(out=sb3(xn_sb, 16),
                    in_=chunk_ap(xn_ap, 0, 2048, 0, TOK)).then_inc(s_xl, 16)
        for i in range(24):
            kind, wap, m = proj[i]
            if i >= 2:
                q.wait_ge(s_pm, 2 * (i - 1))
            q.dma_start(out=sb3(wbuf[i % 2], 16),
                        in_=chunk_ap(wap, 0, 2048, m * 128, 128)).then_inc(ds_w.h(i), 16)
        for i in range(20):
            q.wait_ge(s_qw, 16 * 24)
            if i >= 1:
                q.wait_ge(s_rp, 3 * i)
            src = q_ap if i < 16 else kvin_ap
            row0 = (i % 16) * 128
            for blkk in range(4):
                soff = row0 + [32, 0, 96, 64][blkk]
                q.dma_start(out=swt[blkk * 32:(blkk + 1) * 32, :],
                            in_=src[soff:soff + 32, :]).then_inc(ds_sw.h(i), 16)
        for i in range(16):
            if i < 2:
                q.wait_ge(s_pm, 48)
            else:
                q.wait_ge(s_om, 2 * (i - 1))
            q.dma_start(out=sb3(wbuf[i % 2], 16),
                        in_=chunk_ap(wo_ap, 0, 2048, i * 128, 128)).then_inc(ds_wo.h(i), 16)

    @block.gpsimd
    def _(g):
        for i in range(24):
            g.wait_ge(s_pe, 2 * (i + 1))
            if i < 16:
                g.dma_start(out=q_ap[i * 128:(i + 1) * 128, :],
                            in_=qsb[:, i * 1024:(i + 1) * 1024]).then_inc(s_qw, 16)
            elif i < 20:
                m = i - 16
                g.dma_start(out=kvin_ap[m * 128:(m + 1) * 128, :],
                            in_=kvsb[:, m * 1024:(m + 1) * 1024]).then_inc(s_qw, 16)
            else:
                m = i - 20
                g.dma_start(out=kvin_ap[(4 + m) * 128:(5 + m) * 128, :],
                            in_=kvsb[:, (4 + m) * 1024:(5 + m) * 1024]).then_inc(s_qw, 16)
        for m in range(4):
            g.wait_ge(s_rp, 3 * (16 + m) + 3)
            g.dma_start(out=kvin_ap[m * 128:(m + 1) * 128, :],
                        in_=kvsb[:, m * 1024:(m + 1) * 1024]).then_inc(s_kw, 16)
        g.wait_ge(s_kw, 64)
        g.wait_ge(s_qw, 16 * 24)
        g.collective_compute(
            "AllGather", ALU.bypass,
            replica_groups=[[0, 1], [2, 3], [4, 5], [6, 7]],
            ins=[kvin_ap], outs=[kvall_ap]).then_inc(s_cc, 1)
        g.wait_ge(s_cc, 1)
        for kvh in range(NKV):
            if kvh >= 1:
                g.wait_ge(s_sm, 128 * kvh)
                g.wait_ge(s_nk, 4 * kvh)
            for half in range(2):
                for dup in range(2):
                    g.dma_start(
                        out=kscr[dup * 64:(dup + 1) * 64, half * 1024:(half + 1) * 1024],
                        in_=kvall_ap[half * 1024 + kvh * 64: half * 1024 + kvh * 64 + 64, :]
                        ).then_inc(s_kt, 16)
            for tkt in range(NTK):
                idx = kvh * NTK + tkt
                if idx >= 1:
                    g.wait_ge(s_tr, idx)
                half, loc = divmod(tkt, 8)
                g.dma_start(out=vtin[0:64, :],
                            in_=kvall_ap[half * 1024 + 512 + kvh * 64:
                                         half * 1024 + 512 + kvh * 64 + 64,
                                         loc * 128:(loc + 1) * 128]).then_inc(s_vt, 16)
            for a in range(kvh * 8, (kvh + 1) * 8):
                qh, tqt = divmod(a, NTQ)
                if qh % 2 == 1:
                    g.wait_ge(s_nm, 3 * a + 3)
                    g.dma_start(out=ysb[64:128, (qh // 2) * 1024 + tqt * 512:
                                        (qh // 2) * 1024 + tqt * 512 + 512],
                                in_=osb[0:64, :]).then_inc(s_ym, 16)
        for n in range(32):
            m, h = divmod(n, 2)
            if n >= 2:
                g.wait_ge(s_ad, n - 1)
            g.dma_start(out=xres[n % 2],
                        in_=xin_ap[m * 128:(m + 1) * 128,
                                   h * 512:(h + 1) * 512]).then_inc(ds_xr.h(n), 16)
            g.wait_ge(s_ad, n + 1)
            g.dma_start(out=xout_ap[m * 128:(m + 1) * 128, h * 512:(h + 1) * 512],
                        in_=tres[n % 2]).then_inc(ds_out.h(n), 16)

    @block.tensor
    def _(t):
        t.wait_ge(s_xl, 16)
        for i in range(24):
            ds_w.wait(t, i)
            if i >= 2:
                t.wait_ge(s_pe, 2 * (i - 1))
            bank = (i % 2) * 2
            for h in range(2):
                for k in range(16):
                    ins = t.matmul(pb[bank + h].ap()[:, :],
                                   wbuf[i % 2][:, k * 128:(k + 1) * 128],
                                   xn_sb[:, k * 1024 + h * 512: k * 1024 + h * 512 + 512],
                                   start=(k == 0), stop=(k == 15))
                ins.then_inc(s_pm, 1)
        for i in range(16):
            t.wait_ge(s_nq, i + 1)
            if i >= 1:
                t.wait_ge(s_nr, 6 * i)
            t.matmul(pb[4].ap()[0:1, :], ones128[0:64, 0:1], sqs[0:64, 0:512],
                     start=True, stop=True).then_inc(s_n2, 1)
            t.matmul(pb[5].ap()[0:1, :], ones128[64:128, 0:1], sqs[64:128, 0:512],
                     start=True, stop=True).then_inc(s_n2, 1)
            t.wait_ge(s_nr, 6 * i + 2)
            t.matmul(pb[4].ap()[0:1, :], ones128[0:64, 0:1], sqs[0:64, 512:1024],
                     start=True, stop=True).then_inc(s_n2, 1)
            t.matmul(pb[5].ap()[0:1, :], ones128[64:128, 0:1], sqs[64:128, 512:1024],
                     start=True, stop=True).then_inc(s_n2, 1)
        for kvh in range(NKV):
            for p in range(4):
                j = 4 * kvh + p
                t.wait_ge(s_nk, j + 1)
                t.wait_ge(s_nr, nrk(kvh) + 2 * p)
                t.matmul(pb[4].ap()[0:1, :], ones128[0:64, 0:1], sqs[0:64, 0:512],
                         start=True, stop=True).then_inc(s_n2k, 1)
            t.wait_ge(s_sq2, kvh + 1)
            t.wait_ge(s_nr, nrk(kvh) + 10)
            t.matmul(pb[7].ap()[:, kvh * 4:kvh * 4 + 4], onesrow,
                     nrow[0:1, 52:56], start=True, stop=True).then_inc(s_bb, 1)
            for tkt in range(NTK):
                idx = kvh * NTK + tkt
                t.wait_ge(s_vt, 16 * (idx + 1))
                if idx >= 1:
                    t.wait_ge(s_tv, idx)
                t.transpose(pb[6].ap()[0:128, 0:64], vtin[0:64, :],
                            identsb[0:64, 0:64]).then_inc(s_tr, 1)
            t.wait_ge(s_kt, 64 * (kvh + 1))
            t.wait_ge(s_tv, 16 * (kvh + 1))
            t.wait_ge(s_vs, 1)
            for g_ in range(4):
                qh = kvh * 4 + g_
                qb = (qh % 2) * 64
                for tqt in range(NTQ):
                    aidx = qh * NTQ + tqt
                    if aidx >= 1:
                        t.wait_ge(s_nm, 3 * (aidx - 1) + 2)
                    for tkt in range(NTK):
                        sidx = aidx * NTK + tkt
                        if sidx >= 2:
                            t.wait_ge(s_ex, sidx - 1)
                        t.matmul(pb[sidx % 2].ap()[:, :],
                                 kscr[qb:qb + 64, tkt * 128:(tkt + 1) * 128],
                                 qsb[qb:qb + 64, (qh // 2) * 1024 + tqt * 512:
                                     (qh // 2) * 1024 + tqt * 512 + 512],
                                 start=True, stop=True).then_inc(s_sm, 1)
                        if mask_needed(tqt, tkt):
                            t.wait_ge(s_mk, MK_CUM[sidx] + 1)
                        else:
                            t.wait_ge(s_ex, sidx + 1)
                        ins = t.matmul(pb[2].ap()[0:65, :],
                                       vts[:, tkt * 65:(tkt + 1) * 65],
                                       pt[sidx % 2], start=(tkt == 0),
                                       stop=(tkt == NTK - 1))
                    ins.then_inc(s_av, 1)
                    t.wait_ge(s_nm, 3 * aidx + 1)
                    t.matmul(pb[3].ap()[0:64, :], onesw[64:65, 0:64],
                             osb[64:65, :], start=True, stop=True).then_inc(s_b2, 1)
        for i in range(16):
            ds_wo.wait(t, i)
            if i >= 2:
                t.wait_ge(s_or, 2 * (i - 1))
            t.wait_ge(s_nm, 192)
            t.wait_ge(s_ym, 16 * 32)
            bank = 4 + (i % 2) * 2
            for h in range(2):
                for k in range(16):
                    ins = t.matmul(pb[bank + h].ap()[:, :],
                                   wbuf[i % 2][:, k * 128:(k + 1) * 128],
                                   ysb[:, k * 1024 + h * 512: k * 1024 + h * 512 + 512],
                                   start=(k == 0), stop=(k == 15))
                ins.then_inc(s_om, 1)

    @block.scalar
    def _(s):
        for i in range(24):
            bank = (i % 2) * 2
            dst, bias = pdst(i)
            for h in range(2):
                s.wait_ge(s_pm, 2 * i + h + 1)
                s.activation(dst[:, h * 512:(h + 1) * 512], pb[bank + h].ap()[:, :],
                             AF.Identity, bias=bias).then_inc(s_pe, 1)
        for i in range(16):
            s.wait_ge(s_rp, 3 * (i + 1))
            s.wait_ge(s_qw, 16 * 24)
            if i >= 1:
                s.wait_ge(s_n2, 4 * i)
            s.square(sqs[:, 0:1024], qsb[:, i * 1024:(i + 1) * 1024]).then_inc(s_nq, 1)
        for kvh in range(NKV):
            for p in range(4):
                j = 4 * kvh + p
                s.wait_ge(s_kt, 64 * (kvh + 1))
                if j == 0:
                    s.wait_ge(s_n2, 64)
                else:
                    s.wait_ge(s_n2k, j)
                s.square(sqs[0:64, 0:512],
                         kscr[0:64, p * 512:(p + 1) * 512]).then_inc(s_nk, 1)
            s.wait_ge(s_nr, nrk(kvh) + 9)
            s.sqrt(nrow[0:1, 52:56], nrow[0:1, 48:52]).then_inc(s_sq2, 1)
            for g_ in range(4):
                for tqt in range(NTQ):
                    aidx = (kvh * 4 + g_) * NTQ + tqt
                    qh = aidx // NTQ
                    for tkt in range(NTK):
                        sidx = aidx * NTK + tkt
                        s.wait_ge(s_sm, sidx + 1)
                        s.wait_ge(s_ng, qh // 4 + 1)
                        s.activation(pt[sidx % 2], pb[sidx % 2].ap()[:, :], AF.Exp,
                                     bias=negc[:, qh:qh + 1]).then_inc(s_ex, 1)
        for n in range(32):
            i = n // 2
            bank = 4 + (i % 2) * 2 + (n % 2)
            s.wait_ge(s_om, n + 1)
            if n >= 2:
                ds_out.wait(s, n - 2)
            s.activation(tres[n % 2], pb[bank].ap()[:, :],
                         AF.Identity, bias=boc[:, i:i + 1]).then_inc(s_or, 1)

    @block.vector
    def _(v):
        for i in range(20):
            v.wait_ge(ds_sw.s[i % 2], 64 * (i // 2 + 1))
            if i < 16:
                base = qsb[:, i * 1024:(i + 1) * 1024]
                cos, sin = qcos, qsin
            else:
                base = kvsb[:, (i - 16) * 1024:(i - 15) * 1024]
                cos, sin = kcos, ksin
            v.tensor_tensor(swt, swt, sin, ALU.mult).then_inc(s_rp, 1)
            v.wait_ge(s_rp, 3 * i + 1)
            v.tensor_tensor(base, base, cos, ALU.mult).then_inc(s_rp, 1)
            v.wait_ge(s_rp, 3 * i + 2)
            v.tensor_tensor(base, base, swt, ALU.add).then_inc(s_rp, 1)
        v.wait_ge(s_cc, 1)
        v.memset(onesw[:, :], 1.0)
        v.memset(vts[:, :], 1.0).then_inc(s_vs, 1)
        v.memset(nrow[0:1, 32:48], 0.0)
        for i in range(16):
            v.wait_ge(s_n2, 4 * i + 2)
            v.tensor_reduce(nrow[0:1, 2 * i:2 * i + 1], pb[4].ap()[0:1, :],
                            AX, ALU.max).then_inc(s_nr, 1)
            v.tensor_reduce(nrow[0:1, 2 * i + 1:2 * i + 2], pb[5].ap()[0:1, :],
                            AX, ALU.max).then_inc(s_nr, 1)
            v.wait_ge(s_n2, 4 * i + 4)
            v.tensor_reduce(nrow[0:1, 44:45], pb[4].ap()[0:1, :],
                            AX, ALU.max).then_inc(s_nr, 1)
            v.wait_ge(s_nr, 6 * i + 3)
            v.tensor_tensor(nrow[0:1, 2 * i:2 * i + 1], nrow[0:1, 2 * i:2 * i + 1],
                            nrow[0:1, 44:45], ALU.max).then_inc(s_nr, 1)
            v.tensor_reduce(nrow[0:1, 45:46], pb[5].ap()[0:1, :],
                            AX, ALU.max).then_inc(s_nr, 1)
            v.wait_ge(s_nr, 6 * i + 5)
            v.tensor_tensor(nrow[0:1, 2 * i + 1:2 * i + 2],
                            nrow[0:1, 2 * i + 1:2 * i + 2],
                            nrow[0:1, 45:46], ALU.max).then_inc(s_nr, 1)
        for kvh in range(NKV):
            for p in range(4):
                j = 4 * kvh + p
                v.wait_ge(s_n2k, j + 1)
                v.tensor_reduce(nrow[0:1, 44:45], pb[4].ap()[0:1, :], AX,
                                ALU.max).then_inc(s_nr, 1)
                v.wait_ge(s_nr, nrk(kvh) + 2 * p + 1)
                v.tensor_tensor(nrow[0:1, 32 + kvh:33 + kvh],
                                nrow[0:1, 32 + kvh:33 + kvh],
                                nrow[0:1, 44:45], ALU.max).then_inc(s_nr, 1)
            v.wait_ge(s_nr, nrk(kvh) + 8)
            v.tensor_scalar(nrow[0:1, 48:52], nrow[0:1, 4 * kvh:4 * kvh + 4],
                            nrow[0:1, 32 + kvh:33 + kvh], None,
                            ALU.mult).then_inc(s_nr, 1)
            v.wait_ge(s_sq2, kvh + 1)
            v.tensor_scalar(nrow[0:1, 52:56], nrow[0:1, 52:56], -1.0, None,
                            ALU.mult).then_inc(s_nr, 1)
            v.wait_ge(s_bb, kvh + 1)
            v.tensor_copy(negc[:, 4 * kvh:4 * kvh + 4],
                          pb[7].ap()[:, 4 * kvh:4 * kvh + 4]).then_inc(s_ng, 1)
            for tkt in range(NTK):
                idx = kvh * NTK + tkt
                v.wait_ge(s_tr, idx + 1)
                v.tensor_copy(vts[:, tkt * 65:tkt * 65 + 64],
                              pb[6].ap()[0:128, 0:64]).then_inc(s_tv, 1)
            for g_ in range(4):
                for tqt in range(NTQ):
                    aidx = (kvh * 4 + g_) * NTQ + tqt
                    qh = aidx // NTQ
                    for tkt in range(NTK):
                        sidx = aidx * NTK + tkt
                        if mask_needed(tqt, tkt):
                            v.wait_ge(s_ex, sidx + 1)
                            v.scalar_tensor_tensor(
                                pt[sidx % 2], tqpos[:, tqt * 512:(tqt + 1) * 512],
                                tkpos[:, tkt:tkt + 1], pt[sidx % 2],
                                ALU.is_ge, ALU.mult).then_inc(s_mk, 1)
                    v.wait_ge(s_av, aidx + 1)
                    if aidx >= 1 and omap[aidx - 1] >= 1:
                        v.wait_ge(s_ym, 16 * omap[aidx - 1])
                    v.reciprocal(osb[64:65, :], pb[2].ap()[64:65, :]).then_inc(s_nm, 1)
                    v.wait_ge(s_nm, 3 * aidx + 1)
                    v.tensor_copy(osb[0:64, :], pb[2].ap()[0:64, :]).then_inc(s_nm, 1)
                    v.wait_ge(s_b2, aidx + 1)
                    v.wait_ge(s_nm, 3 * aidx + 2)
                    if qh % 2 == 0:
                        v.tensor_tensor(
                            ysb[0:64, (qh // 2) * 1024 + tqt * 512:
                                (qh // 2) * 1024 + tqt * 512 + 512],
                            osb[0:64, :], pb[3].ap()[0:64, :],
                            ALU.mult).then_inc(s_nm, 1)
                    else:
                        v.tensor_tensor(osb[0:64, :], osb[0:64, :],
                                        pb[3].ap()[0:64, :],
                                        ALU.mult).then_inc(s_nm, 1)
        for n in range(32):
            v.wait_ge(s_or, n + 1)
            ds_xr.wait(v, n)
            v.tensor_tensor(tres[n % 2], tres[n % 2], xres[n % 2],
                            ALU.add).then_inc(s_ad, 1)

    return ds_out.done(32)


L1_BUFS = {
    "qsb": (0, 16384), "ysb": (16384, 16384), "kvsb": (32768, 8192),
    "wb0": (40960, 2048), "wb1": (43008, 2048),
    "swt": (45056, 1024),
}
L1_BUFS2 = {
    "kscr": (32768, 2048), "vts": (34816, 1040), "pt0": (35856, 512),
    "pt1": (36368, 512), "osb": (36880, 512), "nrow": (37392, 64),
    "sqs": (37456, 1024), "vtin": (38480, 128),
    "tres0": (38608, 512), "tres1": (39120, 512),
    "xres0": (39632, 512), "xres1": (40144, 512), "negc": (40656, 32),
    "onesw": (40688, 64),
}


def host_l1_inputs(Wq, bq, Wk, bk, Wv, bv, Wo, bo, core):
    s = core % 2
    pos = np.arange(1, 2048 + 1, dtype=np.float32)
    dpos = pos[s * 1024:(s + 1) * 1024]
    dhalf = 32
    freqs = 10000.0 ** (-(2.0 * np.arange(dhalf, dtype=np.float32)) / 64.0)
    theta = dpos[None, :] * freqs[:, None]
    cos = np.cos(theta).astype(np.float32)
    sin = np.sin(theta).astype(np.float32)
    cos4 = np.tile(cos, (4, 1))
    sin4 = np.concatenate([-sin, sin, -sin, sin], axis=0)
    tq = (s * 1024 + np.arange(1024, dtype=np.float32))
    tqpos = np.broadcast_to(tq[None, :], (128, 1024)).copy()
    tkp = np.arange(2048, dtype=np.float32).reshape(16, 128).T.copy()
    return {
        "a_wq": Wq, "a_wk": Wk, "a_wv": Wv, "a_wo": Wo,
        "a_bqc": np.ascontiguousarray(bq.reshape(16, 128).T),
        "a_bkc": np.ascontiguousarray(np.pad(bk, (0, 0)).reshape(4, 128).T),
        "a_bvc": np.ascontiguousarray(bv.reshape(4, 128).T),
        "a_boc": np.ascontiguousarray(bo.reshape(16, 128).T),
        "a_qcos": np.ascontiguousarray(cos4 * 0.125),
        "a_qsin": np.ascontiguousarray(sin4 * 0.125),
        "a_kcos": np.ascontiguousarray(cos4),
        "a_ksin": np.ascontiguousarray(sin4),
        "a_tqpos": tqpos,
        "a_tkpos": tkp,
        "a_ident": np.eye(128, dtype=np.float32),
    }



"""MoE block emitter (dense v1): xout = x + 0.5*MoE(LN(x)).  See plan.md.

Routing (per tq-tile of 128): logitsT [tq, 8] = xn.T @ gwT + gb (bias via K=1
ones mm); m1 = rowmax; eqm = (L==m1); lm = L - 1e30*eqm; m2 = rowmax(lm);
w1r = sigmoid(m1-m2), w2r = sigmoid(m2-m1) (== softmax of top2, exact);
wts = 0.5*(w1r*eqm + w2r*(lm==m2)); transpose -> wtsT [8, 1024] -> DRAM.

Dense experts per (tt of 256 tokens): seed acc = b2-term via K=8 mm
(b2stack x wtsT); for e: h = xn@w1p[e].T + b1p (GLU-permuted: glu rows 0:2048,
lin 2048:4096); y = (A*sigmoid(1.702A))*(B+1); ye = y@w2[e].T;
acc += ye * wbc_e (DRAM step-0 broadcast of wts row). xout = x + acc.

PSUM: routing: bank 0 (logits accum), bank 1 (transpose). seeds: banks 0/1.
experts: h m-groups of 4 -> banks {0..3}/{4..7} ping; y d-groups of 4 same.
"""


NE = 8
NTT = 4
NRT = 10

MOE_BUFS = {
    "rsb0": (0, 2048), "rsb1": (2048, 2048), "gwt": (4096, 128),
    "gbr": (4224, 16), "lts": (4240, 384), "wtsT": (4624, 1024),
    "b2s": (5648, 2048), "wbc": (7696, 256), "sgl": (7952, 512),
    "tres0": (8464, 256), "tres1": (8720, 256),
    "xres0": (8976, 256), "xres1": (9232, 256),
    "xnt": (10240, 4096), "hsb": (14336, 8192),
    "ws0": (22528, 8192), "ws1": (30720, 8192),
    "acc": (38912, 4096), "ascr": (43008, 256),
}


def emit_moe(env, block, tag, xin_ap, xout_ap, ext, dram, bufs, waits=()):
    pb = env.pb
    ones128 = env.sb["ones128"].ap()[:, :]
    onesrow = env.sb["onesrow"].ap()[:, :]

    s_gl = env.sem(tag + "gl")
    ds_rx = DSem(env, tag + "r")
    s_lm = env.sem(tag + "lm")
    s_rt = env.sem(tag + "rt")
    s_sg = env.sem(tag + "sg")
    s_tp = env.sem(tag + "tp")
    s_tv = env.sem(tag + "tv")
    s_ww = env.sem(tag + "ww")
    ds_wb = DSem(env, tag + "b")
    s_xt = env.sem(tag + "xt")
    ds_w = DSem(env, tag + "c")
    s_hm = env.sem(tag + "hm")
    s_he = env.sem(tag + "he")
    s_gs = env.sem(tag + "gs")
    s_gy = env.sem(tag + "gy")
    s_ym = env.sem(tag + "ym")
    s_sd = env.sem(tag + "sd")
    s_se = env.sem(tag + "se")
    s_ac = env.sem(tag + "ac")
    ds_xr = DSem(env, tag + "x")
    s_ad = env.sem(tag + "ad")
    ds_out = DSem(env, tag + "z")

    rsb = [bufs["rsb0"], bufs["rsb1"]]
    gwt, gbr, lts = bufs["gwt"], bufs["gbr"], bufs["lts"]
    wtsT, b2s, wbc, sgl = bufs["wtsT"], bufs["b2s"], bufs["wbc"], bufs["sgl"]
    xnt, hsb = bufs["xnt"], bufs["hsb"]
    wst = [bufs["ws0"], bufs["ws1"]]
    tres = [bufs["tres0"], bufs["tres1"]]
    xres = [bufs["xres0"], bufs["xres1"]]
    acc = bufs["acc"]
    ascr = bufs["ascr"]
    identsb = ext["ident"]
    gwt_ap, gbr_ap, b2s_ap = ext["gwt"], ext["gbr"], ext["b2s"]
    b1c_all = ext["b1c"]
    w1_ap, w2_ap = ext["w1all"], ext["w2all"]
    xn_ap = dram["xn"]
    wts_ap = dram["wts"]

    def wguard(q, i):
        if i < 2:
            q.wait_ge(s_tv, 8)
            return
        gp, kp = divmod(i - 2, 12)
        if kp < 8:
            q.wait_ge(s_hm, gp * 32 + (kp + 1) * 4)
        else:
            q.wait_ge(s_ym, gp * 16 + (kp - 8 + 1) * 4)

    @block.sync
    def _(q):
        for s, v in waits:
            q.wait_ge(s, v)
        q.dma_start(out=sb3(gwt, 16), in_=chunk_ap(gwt_ap, 0, 2048, 0, 8)
                    ).then_inc(s_gl, 16)
        q.dma_start(out=gbr[0:1, 0:8], in_=gbr_ap).then_inc(s_gl, 16)
        q.dma_start(out=b2s[0:8, :], in_=b2s_ap).then_inc(s_gl, 16)
        for tqt in range(8):
            if tqt >= 2:
                q.wait_ge(s_lm, tqt - 1)
            q.dma_start(out=sb3(rsb[tqt % 2], 16),
                        in_=chunk_ap(xn_ap, 0, 2048, tqt * 128, 128)
                        ).then_inc(ds_rx.h(tqt), 16)
        for tt in range(NTT):
            for e in range(NE):
                i12 = (tt * NE + e) * 12
                for mc in range(8):
                    i = i12 + mc
                    wguard(q, i)
                    q.dma_start(out=sb3(wst[i % 2], 16),
                                in_=chunk_ap(w1_ap, e * 2048, 2048, mc * 512, 512)
                                ).then_inc(ds_w.h(i), 16)
                for dc in range(4):
                    i = i12 + 8 + dc
                    wguard(q, i)
                    q.dma_start(out=sb3(wst[i % 2], 16),
                                in_=chunk_ap(w2_ap, e * 2048, 2048, dc * 512, 512)
                                ).then_inc(ds_w.h(i), 16)

    @block.gpsimd
    def _(g):
        g.wait_ge(s_tv, 8)
        g.dma_start(out=wts_ap, in_=wtsT[0:8, :]).then_inc(s_ww, 16)
        for tt in range(NTT):
            if tt >= 1:
                g.wait_ge(s_hm, 256 * tt)
            g.dma_start(out=sb3(xnt, 16),
                        in_=chunk_ap(xn_ap, 0, 2048, tt * 256, 256)).then_inc(s_xt, 16)
            for e in range(NE):
                i = tt * NE + e
                g.wait_ge(s_ww, 16)
                if i >= 1:
                    g.wait_ge(s_ac, 32 * i)
                g.dma_start(out=wbc,
                            in_=wts_ap[e:e + 1, tt * 256:(tt + 1) * 256]
                            .broadcast_to([128, 256])).then_inc(ds_wb.h(i), 16)
            for dt in range(16):
                n = tt * 16 + dt
                if n >= 2:
                    g.wait_ge(s_ad, n - 1)
                g.dma_start(out=xres[n % 2],
                            in_=xin_ap[dt * 128:(dt + 1) * 128,
                                       tt * 256:(tt + 1) * 256]).then_inc(ds_xr.h(n), 16)
                g.wait_ge(s_ad, n + 1)
                g.dma_start(out=xout_ap[dt * 128:(dt + 1) * 128,
                                        tt * 256:(tt + 1) * 256],
                            in_=tres[n % 2]).then_inc(ds_out.h(n), 16)

    @block.tensor
    def _(t):
        t.wait_ge(s_gl, 48)
        for tqt in range(8):
            ds_rx.wait(t, tqt)
            if tqt >= 1:
                t.wait_ge(s_rt, NRT * (tqt - 1) + 1)
            for k in range(16):
                t.matmul(pb[0].ap()[:, 0:8],
                         rsb[tqt % 2][:, k * 128:(k + 1) * 128],
                         gwt[:, k * 8:(k + 1) * 8],
                         start=(k == 0), stop=False)
            ins = t.matmul(pb[0].ap()[:, 0:8], onesrow, gbr[0:1, 0:8],
                           start=False, stop=True)
            ins.then_inc(s_lm, 1)
            t.wait_ge(s_rt, NRT * tqt + NRT)
            if tqt >= 1:
                t.wait_ge(s_tv, tqt)
            t.transpose(pb[1].ap()[0:8, 0:128], lts[:, tqt * 48 + 32:tqt * 48 + 40],
                        identsb).then_inc(s_tp, 1)
        for tt in range(NTT):
            t.wait_ge(s_tv, 8)
            t.wait_ge(s_xt, 16 * (tt + 1))
            for dt in range(16):
                if dt >= 2:
                    t.wait_ge(s_se, 16 * tt + dt - 1)
                elif tt >= 1:
                    t.wait_ge(s_ac, 256 * tt)
                t.matmul(pb[dt % 2].ap()[:, 0:256], b2s[0:8, dt * 128:(dt + 1) * 128],
                         wtsT[0:8, tt * 256:(tt + 1) * 256],
                         start=True, stop=True).then_inc(s_sd, 1)
            for e in range(NE):
                i = tt * NE + e
                i12 = i * 12
                for mg in range(8):
                    gidx = i * 8 + mg
                    ds_w.wait(t, i12 + mg)
                    if e == 0 and mg == 0:
                        t.wait_ge(s_se, 16 * (tt + 1))
                    if i >= 1 and mg < 2:
                        t.wait_ge(s_ac, 32 * i)
                    if gidx >= 2:
                        t.wait_ge(s_he, 4 * (gidx - 1))
                    bp = (mg % 2) * 4
                    for j in range(4):
                        m = mg * 4 + j
                        for k in range(16):
                            ins = t.matmul(
                                pb[bp + j].ap()[:, 0:256],
                                wst[(i12 + mg) % 2][:, k * 512 + j * 128:
                                                    k * 512 + (j + 1) * 128],
                                xnt[:, k * 256:(k + 1) * 256],
                                start=(k == 0), stop=(k == 15))
                        ins.then_inc(s_hm, 1)
                for dc in range(4):
                    ds_w.wait(t, i12 + 8 + dc)
                    t.wait_ge(s_gy, 32 * (i + 1))
                    if dc >= 2:
                        t.wait_ge(s_ac, 32 * i + 8 * (dc - 1))
                    bp = (dc % 2) * 4
                    for j in range(4):
                        for k in range(16):
                            ins = t.matmul(
                                pb[bp + j].ap()[:, 0:256],
                                wst[(i12 + 8 + dc) % 2][:, k * 512 + j * 128:
                                                        k * 512 + (j + 1) * 128],
                                hsb[:, k * 256:(k + 1) * 256],
                                start=(k == 0), stop=(k == 15))
                        ins.then_inc(s_ym, 1)

    @block.scalar
    def _(s):
        for tqt in range(8):
            c0 = tqt * 48
            s.wait_ge(s_rt, NRT * tqt + 6)
            s.activation(lts[:, c0 + 11:c0 + 12], lts[:, c0 + 10:c0 + 11],
                         AF.Sigmoid).then_inc(s_sg, 1)
            s.activation(lts[:, c0 + 12:c0 + 13], lts[:, c0 + 10:c0 + 11],
                         AF.Sigmoid, scale=-1.0).then_inc(s_sg, 1)
        for tt in range(NTT):
            for e in range(NE):
                i = tt * NE + e
                for mg in range(8):
                    bp = (mg % 2) * 4
                    for j in range(4):
                        m = mg * 4 + j
                        s.wait_ge(s_hm, 32 * i + mg * 4 + j + 1)
                        s.activation(hsb[:, m * 256:(m + 1) * 256],
                                     pb[bp + j].ap()[:, 0:256], AF.Identity,
                                     bias=b1c_all[:, e * 32 + m:e * 32 + m + 1]
                                     ).then_inc(s_he, 1)
                for gt in range(16):
                    s.wait_ge(s_he, 32 * (i + 1))
                    if gt >= 1:
                        s.wait_ge(s_gy, 32 * i + 2 * gt)
                    s.activation(sgl[:, 0:256], hsb[:, gt * 256:(gt + 1) * 256],
                                 AF.Sigmoid, scale=1.702).then_inc(s_gs, 1)

    @block.vector
    def _(v):
        for tqt in range(8):
            c0 = tqt * 48
            L = lts[:, c0:c0 + 8]
            m1 = lts[:, c0 + 8:c0 + 9]
            m2 = lts[:, c0 + 9:c0 + 10]
            dd = lts[:, c0 + 10:c0 + 11]
            w1r = lts[:, c0 + 11:c0 + 12]
            w2r = lts[:, c0 + 12:c0 + 13]
            eqm = lts[:, c0 + 16:c0 + 24]
            lm = lts[:, c0 + 24:c0 + 32]
            wt = lts[:, c0 + 32:c0 + 40]
            eq2 = lts[:, c0 + 40:c0 + 48]
            r0 = NRT * tqt
            v.wait_ge(s_lm, tqt + 1)
            v.tensor_copy(L, pb[0].ap()[:, 0:8]).then_inc(s_rt, 1)
            v.wait_ge(s_rt, r0 + 1)
            v.tensor_reduce(m1, L, AX, ALU.max).then_inc(s_rt, 1)
            v.wait_ge(s_rt, r0 + 2)
            v.tensor_scalar(eqm, L, m1, None, ALU.is_ge).then_inc(s_rt, 1)
            v.wait_ge(s_rt, r0 + 3)
            v.scalar_tensor_tensor(lm, eqm, -1e30, L, ALU.mult,
                                   ALU.add).then_inc(s_rt, 1)
            v.wait_ge(s_rt, r0 + 4)
            v.tensor_reduce(m2, lm, AX, ALU.max).then_inc(s_rt, 1)
            v.wait_ge(s_rt, r0 + 5)
            v.tensor_tensor(dd, m1, m2, ALU.subtract).then_inc(s_rt, 1)
            v.wait_ge(s_sg, 2 * tqt + 2)
            v.tensor_scalar(wt, eqm, w1r, None, ALU.mult).then_inc(s_rt, 1)
            v.wait_ge(s_rt, r0 + 7)
            v.tensor_scalar(eq2, lm, m2, None, ALU.is_ge).then_inc(s_rt, 1)
            v.wait_ge(s_rt, r0 + 8)
            v.scalar_tensor_tensor(wt, eq2, w2r, wt, ALU.mult,
                                   ALU.add).then_inc(s_rt, 1)
            v.wait_ge(s_rt, r0 + 9)
            v.tensor_scalar(wt, wt, 0.5, None, ALU.mult).then_inc(s_rt, 1)
            v.wait_ge(s_tp, tqt + 1)
            v.tensor_copy(wtsT[0:8, tqt * 128:(tqt + 1) * 128],
                          pb[1].ap()[0:8, 0:128]).then_inc(s_tv, 1)
        for tt in range(NTT):
            for dt in range(16):
                v.wait_ge(s_sd, 16 * tt + dt + 1)
                v.tensor_copy(acc[:, dt * 256:(dt + 1) * 256],
                              pb[dt % 2].ap()[:, 0:256]).then_inc(s_se, 1)
            for e in range(NE):
                i = tt * NE + e
                for gt in range(16):
                    v.wait_ge(s_gs, 16 * i + gt + 1)
                    v.tensor_tensor(sgl[:, 0:256], sgl[:, 0:256],
                                    hsb[:, gt * 256:(gt + 1) * 256],
                                    ALU.mult).then_inc(s_gy, 1)
                    v.wait_ge(s_gy, 32 * i + 2 * gt + 1)
                    v.scalar_tensor_tensor(hsb[:, gt * 256:(gt + 1) * 256],
                                           hsb[:, (16 + gt) * 256:(17 + gt) * 256],
                                           1.0, sgl[:, 0:256], ALU.add,
                                           ALU.mult).then_inc(s_gy, 1)
                ds_wb.wait(v, i)
                for dc in range(4):
                    bp = (dc % 2) * 4
                    for j in range(4):
                        dt = dc * 4 + j
                        v.wait_ge(s_ym, 16 * i + dc * 4 + j + 1)
                        v.wait_ge(s_ac, 32 * i + 8 * dc + 2 * j)
                        v.tensor_tensor(ascr, pb[bp + j].ap()[:, 0:256], wbc,
                                        ALU.mult).then_inc(s_ac, 1)
                        v.wait_ge(s_ac, 32 * i + 8 * dc + 2 * j + 1)
                        v.tensor_tensor(acc[:, dt * 256:(dt + 1) * 256],
                                        acc[:, dt * 256:(dt + 1) * 256],
                                        ascr, ALU.add).then_inc(s_ac, 1)
            for dt in range(16):
                n = tt * 16 + dt
                v.wait_ge(s_ac, 256 * (tt + 1))
                ds_xr.wait(v, n)
                if n >= 2:
                    ds_out.wait(v, n - 2)
                v.tensor_tensor(tres[n % 2], acc[:, dt * 256:(dt + 1) * 256],
                                xres[n % 2], ALU.add).then_inc(s_ad, 1)

    return ds_out.done(64)


def host_moe_inputs(gate_w, gate_b, w1, b1, w2, b2):
    perm = np.concatenate([np.arange(0, 4096, 2), np.arange(1, 4096, 2)])
    w1p = np.ascontiguousarray(
        np.transpose(w1, (0, 2, 1))[:, :, perm]).reshape(8 * 2048, 4096)
    w2t = np.ascontiguousarray(np.transpose(w2, (0, 2, 1))).reshape(8 * 2048, 2048)
    b1p = b1[:, perm]
    b1c = np.concatenate([np.ascontiguousarray(b1p[e].reshape(32, 128).T)
                          for e in range(8)], axis=1)
    return {
        "m_w1all": w1p.astype(np.float32),
        "m_w2all": w2t.astype(np.float32),
        "m_gwt": np.ascontiguousarray(gate_w.T),
        "m_gbr": gate_b.reshape(1, 8).astype(np.float32),
        "m_b2s": b2.astype(np.float32),
        "m_b1c": b1c.astype(np.float32),
    }



def build_ffn0_program():
    nc = bass.Bass("TRN2", target_bir_lowering=False)
    x = nc.dram_tensor("x", [2048, 1024], F32, kind="ExternalInput").ap()
    w1t = nc.dram_tensor("f_w1t", [2048, 8192], F32, kind="ExternalInput").ap()
    w2t = nc.dram_tensor("f_w2t", [8192, 2048], F32, kind="ExternalInput").ap()
    w1sh = nc.dram_tensor("f_w1sh", [256, 8192], F32, kind="ExternalInput").ap()
    w2sh = nc.dram_tensor("f_w2sh", [1024, 2048], F32, kind="ExternalInput").ap()
    b1c = nc.dram_tensor("f_b1c", [128, 64], F32, kind="ExternalInput").ap()
    b2c = nc.dram_tensor("f_b2c", [128, 16], F32, kind="ExternalInput").ap()
    xout = nc.dram_tensor("xout", [2048, 1024], F32, kind="ExternalOutput").ap()
    dram = {
        "w1q": nc.dram_tensor("w1q", [2048, 8192], F32).ap(),
        "w2q": nc.dram_tensor("w2q", [8192, 2048], F32).ap(),
        "xn": nc.dram_tensor("xn_d", [2048, 1024], F32).ap(),
        "ccin": nc.dram_tensor("ccin", [1, 2], F32).ap(),
        "ccout": nc.dram_tensor("ccout", [1, 2], F32).ap(),
    }
    ext = {"w1t": w1t, "w2t": w2t, "w1sh": w1sh, "w2sh": w2sh}

    with ExitStack() as stack:
        env = make_env(nc, stack)
        env.sbuf("b1c", (128, 64))
        env.sbuf("b2c", (128, 16))
        env.sbuf("halfb2", (128, 16))
        block = stack.enter_context(nc.Block())
        w_init = emit_const_init(env, block, [
            (env.sb["b1c"].ap()[:, :], b1c),
            (env.sb["b2c"].ap()[:, :], b2c),
        ])

        s_hb = env.sem("halfb2")

        @block.vector
        def _(v):
            for s, v_ in w_init:
                v.wait_ge(s, v_)
            v.tensor_scalar(env.sb["halfb2"].ap()[:, :], env.sb["b2c"].ap()[:, :],
                            0.5, None, ALU.mult).then_inc(s_hb, 1)

        w_out = emit_ffn0_block(
            env, block, "f", x, xout, ext, dram, first=True,
            waits=w_init + [(s_hb, 1)])

        @block.gpsimd
        def _(g):
            for s, v in w_out:
                g.wait_ge(s, v)
    return nc


SMALL = {
    "l_dtb": [128, 1], "l_alog": [128, 1], "l_Drep": [128, 32],
    "l_nwrep": [128, 32], "l_BbT": [128, 32], "l_BbTs": [128, 32],
    "l_CbT": [128, 32], "l_CbTs": [128, 32], "l_freqs": [128, 32],
    "l_is1": [128, 1], "l_triu2": [128, 256], "l_qsel": [2, 128],
    "l_ident": [128, 128],
}


def build_l0_program():
    nc = bass.Bass("TRN2", target_bir_lowering=False)
    x = nc.dram_tensor("x", [2048, 1024], F32, kind="ExternalInput").ap()
    eap = {"l_win": nc.dram_tensor("l_win", [2048, 8384], F32, kind="ExternalInput").ap(),
           "l_wout": nc.dram_tensor("l_wout", [4096, 2048], F32, kind="ExternalInput").ap(),
           "l_rpkc": nc.dram_tensor("l_rpkc", [2, 256], F32, kind="ExternalInput").ap(),
           "l_ones1": nc.dram_tensor("l_ones1", [1, 128], F32, kind="ExternalInput").ap()}
    for k, shp in SMALL.items():
        eap[k] = nc.dram_tensor(k, shp, F32, kind="ExternalInput").ap()
    xout = nc.dram_tensor("xout", [2048, 1024], F32, kind="ExternalOutput").ap()
    dshapes = {
        "xn": [2048, 1024], "xs": [4096, 1024], "brd": [4096, 1024],
        "crd": [4096, 1024], "cosd": [64, 32768], "sind": [64, 32768],
        "nsind": [64, 32768], "cld": [64, 1024], "ncld": [64, 1024],
        "clfd": [64, 1024], "bcd": [128, 1024], "gd": [4096, 1024],
        "ggd": [4096, 1024], "cc1i": [64, 1], "cc1o": [128, 1],
        "cc2i": [128, 2048], "cc2o": [256, 2048],
    }
    dram = {k: nc.dram_tensor("l0_" + k, v, F32).ap() for k, v in dshapes.items()}
    with ExitStack() as stack:
        env = make_env(nc, stack)
        for k, shp in SMALL.items():
            env.sbuf("sb_" + k, tuple(shp))
        block = stack.enter_context(nc.Block())
        w_init = emit_const_init(env, block, [
            (env.sb["sb_" + k].ap()[:, :], eap[k]) for k in SMALL])
        ln = carve(env, LN_BUFS)
        w_ln = emit_ln(env, block, "ln", x, dram["xn"], ln, waits=w_init)
        bufs = carve(env, L0_BUFS)
        ext = {"win": eap["l_win"], "wout": eap["l_wout"],
               "ident": env.sb["sb_l_ident"].ap()[:, :],
               "rpkc_d": eap["l_rpkc"], "ones1_d": eap["l_ones1"]}
        for k in ("dtb", "alog", "Drep", "nwrep", "BbT", "BbTs", "CbT",
                  "CbTs", "freqs", "is1", "triu2", "qsel"):
            ext[k] = env.sb["sb_l_" + k].ap()[:, :]
        w_out = emit_l0(env, block, "l", x, xout, ext, dram, bufs,
                           waits=w_ln)

        @block.gpsimd
        def _(g):
            for s, v in w_out:
                g.wait_ge(s, v)
    return nc


def build_l1_program():
    nc = bass.Bass("TRN2", target_bir_lowering=False)
    x = nc.dram_tensor("x", [2048, 1024], F32, kind="ExternalInput").ap()
    ext_specs = {
        "a_wq": [2048, 2048], "a_wk": [2048, 512], "a_wv": [2048, 512],
        "a_wo": [2048, 2048], "a_bqc": [128, 16], "a_bkc": [128, 4],
        "a_bvc": [128, 4], "a_boc": [128, 16],
        "a_qcos": [128, 1024], "a_qsin": [128, 1024],
        "a_kcos": [128, 1024], "a_ksin": [128, 1024],
        "a_tqpos": [128, 1024], "a_tkpos": [128, 16], "a_ident": [128, 128],
    }
    eap = {k: nc.dram_tensor(k, v, F32, kind="ExternalInput").ap()
           for k, v in ext_specs.items()}
    xout = nc.dram_tensor("xout", [2048, 1024], F32, kind="ExternalOutput").ap()
    dram = {
        "xn": nc.dram_tensor("xn_d", [2048, 1024], F32).ap(),
        "q": nc.dram_tensor("q_d", [2048, 1024], F32).ap(),
        "kvin": nc.dram_tensor("kvin_d", [1024, 1024], F32).ap(),
        "kvall": nc.dram_tensor("kvall_d", [2048, 1024], F32).ap(),
    }

    with ExitStack() as stack:
        env = make_env(nc, stack)
        small = ["a_bqc", "a_bkc", "a_bvc", "a_boc", "a_qcos", "a_qsin",
                 "a_kcos", "a_ksin", "a_tqpos", "a_tkpos", "a_ident"]
        for name in small:
            env.sbuf("sb_" + name, tuple(ext_specs[name]))
        block = stack.enter_context(nc.Block())
        w_init = emit_const_init(env, block, [
            (env.sb["sb_" + n].ap()[:, :], eap[n]) for n in small])
        ln = carve(env, LN_BUFS)
        w_ln = emit_ln(env, block, "ln", x, dram["xn"], ln, waits=w_init)

        bufs = carve(env, {**L1_BUFS, **L1_BUFS2})
        ext = {
            "wq": eap["a_wq"], "wk": eap["a_wk"], "wv": eap["a_wv"], "wo": eap["a_wo"],
            "bqc": env.sb["sb_a_bqc"].ap()[:, :], "bkc": env.sb["sb_a_bkc"].ap()[:, :],
            "bvc": env.sb["sb_a_bvc"].ap()[:, :], "boc": env.sb["sb_a_boc"].ap()[:, :],
            "qcos": env.sb["sb_a_qcos"].ap()[:, :], "qsin": env.sb["sb_a_qsin"].ap()[:, :],
            "kcos": env.sb["sb_a_kcos"].ap()[:, :], "ksin": env.sb["sb_a_ksin"].ap()[:, :],
            "tqpos": env.sb["sb_a_tqpos"].ap()[:, :], "tkpos": env.sb["sb_a_tkpos"].ap()[:, :],
            "ident": env.sb["sb_a_ident"].ap()[:, :],
        }
        w_out = emit_l1attn(env, block, "a", x, xout, ext, dram, bufs,
                               waits=w_ln)

        @block.gpsimd
        def _(g):
            for s, v in w_out:
                g.wait_ge(s, v)
    return nc


def build_moe_program():
    nc = bass.Bass("TRN2", target_bir_lowering=False)
    x = nc.dram_tensor("x", [2048, 1024], F32, kind="ExternalInput").ap()
    eap = {
        "m_w1all": nc.dram_tensor("m_w1all", [16384, 4096], F32, kind="ExternalInput").ap(),
        "m_w2all": nc.dram_tensor("m_w2all", [16384, 2048], F32, kind="ExternalInput").ap(),
        "m_gwt": nc.dram_tensor("m_gwt", [2048, 8], F32, kind="ExternalInput").ap(),
        "m_gbr": nc.dram_tensor("m_gbr", [1, 8], F32, kind="ExternalInput").ap(),
        "m_b2s": nc.dram_tensor("m_b2s", [8, 2048], F32, kind="ExternalInput").ap(),
        "m_b1c": nc.dram_tensor("m_b1c", [128, 256], F32, kind="ExternalInput").ap(),
        "m_ident": nc.dram_tensor("m_ident", [128, 128], F32, kind="ExternalInput").ap(),
    }
    xout = nc.dram_tensor("xout", [2048, 1024], F32, kind="ExternalOutput").ap()
    dram = {
        "xn": nc.dram_tensor("xn_d", [2048, 1024], F32).ap(),
        "wts": nc.dram_tensor("wts_d", [8, 1024], F32).ap(),
    }
    with ExitStack() as stack:
        env = make_env(nc, stack)
        env.sbuf("sb_b1c", (128, 256))
        env.sbuf("sb_ident", (128, 128))
        block = stack.enter_context(nc.Block())
        w_init = emit_const_init(env, block, [
            (env.sb["sb_b1c"].ap()[:, :], eap["m_b1c"]),
            (env.sb["sb_ident"].ap()[:, :], eap["m_ident"]),
        ])
        ln = carve(env, LN_BUFS)
        w_ln = emit_ln(env, block, "ln", x, dram["xn"], ln, waits=w_init)
        bufs = carve(env, MOE_BUFS)
        ext = {
            "gwt": eap["m_gwt"], "gbr": eap["m_gbr"], "b2s": eap["m_b2s"],
            "b1c": env.sb["sb_b1c"].ap()[:, :],
            "w1all": eap["m_w1all"], "w2all": eap["m_w2all"],
            "ident": env.sb["sb_ident"].ap()[:, :],
        }
        w_out = emit_moe(env, block, "m", x, xout, ext, dram, bufs, waits=w_ln)

        @block.gpsimd
        def _(g):
            for s, v in w_out:
                g.wait_ge(s, v)
    return nc




_PROGS = {}


def _get_progs():
    if not _PROGS:
        _PROGS["ffn0"] = build_ffn0_program()
        _PROGS["l0"] = build_l0_program()
        _PROGS["l1"] = build_l1_program()
        _PROGS["moe"] = build_moe_program()
    return _PROGS


def kernel(**inp):
    P = _get_progs()
    f = lambda k: np.ascontiguousarray(np.asarray(inp[k], dtype=np.float32))
    x = f("L0_attn_x")

    ffn0_h = [host_ffn0_inputs(f("L0_ffn_w1"), f("L0_ffn_b1"),
                               f("L0_ffn_w2"), f("L0_ffn_b2"), c)
              for c in range(8)]
    l0_h = []
    for c in range(8):
        m = host_l0_inputs(f("L0_attn_W_in"), f("L0_attn_W_out"),
                           f("L0_attn_dt_bias"), f("L0_attn_A_log"),
                           f("L0_attn_D_param"), f("L0_attn_B_bias"),
                           f("L0_attn_C_bias"), f("L0_attn_rope_freqs"),
                           f("L0_attn_norm_w"), c)
        m["l_ident"] = np.eye(128, dtype=np.float32)
        l0_h.append(m)
    l1_h = [host_l1_inputs(f("L1_attn_Wq"), f("L1_attn_bq"),
                           f("L1_attn_Wk"), f("L1_attn_bk"),
                           f("L1_attn_Wv"), f("L1_attn_bv"),
                           f("L1_attn_Wo"), f("L1_attn_bo"), c)
            for c in range(8)]
    moe_m = host_moe_inputs(f("L1_ffn_gate_w"), f("L1_ffn_gate_b"),
                            f("L1_ffn_w1"), f("L1_ffn_b1"),
                            f("L1_ffn_w2"), f("L1_ffn_b2"))
    moe_m["m_ident"] = np.eye(128, dtype=np.float32)
    moe_h = [moe_m for _ in range(8)]

    xs = [np.ascontiguousarray(x[c // 2, (c % 2) * 1024:(c % 2 + 1) * 1024, :].T)
          for c in range(8)]
    for prog, hm in ((P["ffn0"], ffn0_h), (P["l0"], l0_h), (P["ffn0"], ffn0_h),
                     (P["moe"], moe_h), (P["l1"], l1_h), (P["moe"], moe_h)):
        in_maps = [{"x": xs[c], **hm[c]} for c in range(8)]
        res = run_bass_kernel_spmd(prog, in_maps, core_ids=list(range(8)))
        xs = [np.asarray(res.results[c]["xout"]) for c in range(8)]

    out = np.zeros((4, 2048, 2048), np.float32)
    for c in range(8):
        out[c // 2, (c % 2) * 1024:(c % 2 + 1) * 1024, :] = xs[c].T
    return out


# version 02
# speedup vs baseline: 1.0110x; 1.0110x over previous
import numpy as np
from contextlib import ExitStack
import concourse.bass as bass
import concourse.mybir as mybir
from concourse.bass_utils import run_bass_kernel_spmd




F32 = mybir.dt.float32
F16 = mybir.dt.float16
BF16 = mybir.dt.bfloat16
I32 = mybir.dt.int32
AF = mybir.ActivationFunctionType
ALU = mybir.AluOpType
AX = mybir.AxisListType.X

D = 2048
TOK = 1024
NF = D // 128
N_CORES = 8


class Env:
    def __init__(self, nc, stack):
        self.nc = nc
        self.stack = stack
        self.sems = {}
        self.sb = {}
        self.pb = []
        self.epoch = []

    def sem(self, name):
        assert name not in self.sems, name
        h = self.nc.alloc_semaphore(name=name)
        self.sems[name] = h
        self.epoch.append(h)
        return h

    def retire_epoch(self):
        hs = self.epoch
        self.epoch = []
        return hs

    def sbuf(self, name, shape, dtype=F32):
        assert name not in self.sb, name
        self.sb[name] = self.stack.enter_context(
            self.nc.sbuf_tensor(name, list(shape), dtype))
        return self.sb[name]


class DSem:

    def __init__(self, env, tag):
        self.s = [env.sem(tag + "a"), env.sem(tag + "b")]

    def h(self, i):
        return self.s[i % 2]

    def val(self, i):
        return 16 * (i // 2 + 1)

    def wait(self, eng, i):
        eng.wait_ge(self.s[i % 2], self.val(i))

    def done(self, n):
        out = []
        if n >= 1:
            out.append((self.s[(n - 1) % 2], self.val(n - 1)))
        if n >= 2:
            out.append((self.s[(n - 2) % 2], self.val(n - 2)))
        return out


def make_env(nc, stack, arena_words=46080):
    env = Env(nc, stack)
    for i in range(8):
        env.pb.append(stack.enter_context(nc.psum_tensor("pb%d" % i, [128, 512], F32)))
    env.sbuf("ones128", (128, 1))
    env.sbuf("onesrow", (1, 128))
    env.sbuf("arena", (128, arena_words))
    env.arena = env.sb["arena"].ap()
    return env


def carve(env, spec):
    return {name: env.arena[:, off:off + n] for name, (off, n) in spec.items()}


def chunk_ap(dram_ap, r0, nr, c0, ncols):
    return (dram_ap[r0:r0 + nr, c0:c0 + ncols]
            .rearrange("(k p) m -> k p m", p=128).transpose([1, 0, 2]))


def sb3(buf_ap, k):
    return buf_ap.rearrange("p (k m) -> p k m", k=k)


def emit_const_init(env, block, loads):
    s_ld = env.sem("init_ld")
    total = 16 * len(loads) + 1

    @block.sync
    def _(q):
        for dst, src in loads:
            q.dma_start(out=dst, in_=src).then_inc(s_ld, 16)

    @block.vector
    def _(v):
        v.memset(env.sb["ones128"].ap()[:, :], 1.0)
        v.memset(env.sb["onesrow"].ap()[:, :], 1.0).then_inc(s_ld, 1)

    return [(s_ld, total)]


def emit_boundary(env, block, bar, bno, w_prev):
    nc = env.nc
    base = 6 * (bno - 1)

    def mk(engname):
        def body(e):
            for s, v in w_prev:
                e.wait_ge(s, v)
            e.maybe_drain_then_inc((bar, 1))
        return body

    block.vector(mk("vector"))
    block.scalar(mk("scalar"))
    block.tensor(mk("tensor"))
    block.sync(mk("sync"))
    retired = env.retire_epoch()

    @block.gpsimd
    def _(g):
        for s, v in w_prev:
            g.wait_ge(s, v)
        g.maybe_drain_then_inc((bar, 1))
        g.wait_ge(bar, base + 5)
        nc.clear_and_free_semaphores(retired)
        g.maybe_drain_then_inc((bar, 1))

    for h in retired:
        env.sems.pop(h.name, None)

    def gate(e):
        e.wait_ge(bar, base + 6)

    block.vector(gate)
    block.scalar(gate)
    block.tensor(gate)
    block.sync(gate)
    block.gpsimd(gate)
    return [(bar, base + 6)]


def emit_ln(env, block, tag, xin_ap, xn_dram_ap, bufs, waits=()):
    ds_x = DSem(env, tag + "_x")
    s_sq = env.sem(tag + "_sq")
    s_te = env.sem(tag + "_te")
    s_rw = env.sem(tag + "_rw")
    s_bc = env.sem(tag + "_bc")
    ds_x2 = DSem(env, tag + "_y")
    s_vc = env.sem(tag + "_vc")
    s_ap = env.sem(tag + "_ap")
    ds_wr = DSem(env, tag + "_w")
    xb = [bufs["xb0"], bufs["xb1"]]
    sq = [bufs["sq0"], bufs["sq1"]]
    xn = [bufs["xn0"], bufs["xn1"]]
    rows = bufs["rows"]
    ones128 = env.sb["ones128"].ap()[:, :]
    onesrow = env.sb["onesrow"].ap()[:, :]
    pb = env.pb
    mu = rows[0:1, 0:TOK]
    msq = rows[0:1, TOK:2 * TOK]
    var = rows[0:1, 2 * TOK:3 * TOK]
    r0 = rows[0:1, 3 * TOK:4 * TOK]
    s0 = rows[0:1, 4 * TOK:5 * TOK]
    t1 = rows[0:1, 5 * TOK:6 * TOK]
    t3 = rows[0:1, 6 * TOK:7 * TOK]
    rstd = rows[0:1, 7 * TOK:8 * TOK]

    @block.sync
    def _(q):
        for s, v in waits:
            q.wait_ge(s, v)
        for f in range(NF):
            if f >= 2:
                q.wait_ge(s_te, 4 * (f - 1))
                q.wait_ge(s_sq, f - 1)
            q.dma_start(out=xb[f % 2],
                        in_=xin_ap[f * 128:(f + 1) * 128, :]).then_inc(ds_x.h(f), 16)
        q.wait_ge(s_te, 64)
        for f in range(NF):
            if f >= 2:
                q.wait_ge(s_ap, f - 1)
            q.dma_start(out=xb[f % 2],
                        in_=xin_ap[f * 128:(f + 1) * 128, :]).then_inc(ds_x2.h(f), 16)

    @block.scalar
    def _(s):
        for f in range(NF):
            ds_x.wait(s, f)
            if f >= 2:
                s.wait_ge(s_te, 4 * (f - 1))
            s.square(sq[f % 2], xb[f % 2]).then_inc(s_sq, 1)
        s.wait_ge(s_rw, 1)
        s.sqrt(s0, r0).then_inc(s_rw, 1)

    @block.tensor
    def _(t):
        for f in range(NF):
            ds_x.wait(t, f)
            for h in range(2):
                t.matmul(pb[h].ap()[0:1, :], ones128,
                         xb[f % 2][:, h * 512:(h + 1) * 512],
                         start=(f == 0), stop=(f == NF - 1)).then_inc(s_te, 1)
            t.wait_ge(s_sq, f + 1)
            for h in range(2):
                t.matmul(pb[2 + h].ap()[0:1, :], ones128,
                         sq[f % 2][:, h * 512:(h + 1) * 512],
                         start=(f == 0), stop=(f == NF - 1)).then_inc(s_te, 1)
        t.wait_ge(s_rw, 3)
        for h in range(2):
            t.matmul(pb[4 + h].ap()[:, :], onesrow, mu[0:1, h * 512:(h + 1) * 512],
                     start=True, stop=True).then_inc(s_bc, 1)
        for h in range(2):
            t.matmul(pb[6 + h].ap()[:, :], onesrow, rstd[0:1, h * 512:(h + 1) * 512],
                     start=True, stop=True).then_inc(s_bc, 1)

    @block.vector
    def _(v):
        cnt = [0]

        def step(ins):
            cnt[0] += 1
            ins.then_inc(s_vc, 1)

        def sync():
            v.wait_ge(s_vc, cnt[0])

        v.wait_ge(s_te, 64)
        for h in range(2):
            sl = slice(h * 512, (h + 1) * 512)
            step(v.tensor_scalar(mu[0:1, sl], pb[h].ap()[0:1, :], 1.0 / D, None, ALU.mult))
            step(v.tensor_scalar(msq[0:1, sl], pb[2 + h].ap()[0:1, :], 1.0 / D, None, ALU.mult))
        sync()
        step(v.tensor_tensor(var, mu, mu, ALU.mult))
        sync()
        step(v.tensor_tensor(var, msq, var, ALU.subtract))
        sync()
        step(v.tensor_scalar(var, var, 1e-5, None, ALU.add))
        sync()
        v.reciprocal(r0, var).then_inc(s_rw, 1)
        v.wait_ge(s_rw, 2)
        step(v.tensor_tensor(t1, s0, s0, ALU.mult))
        sync()
        step(v.tensor_tensor(t1, t1, var, ALU.mult))
        sync()
        step(v.tensor_scalar(t3, t1, -0.5, 1.5, ALU.mult, ALU.add))
        sync()
        v.tensor_tensor(rstd, s0, t3, ALU.mult).then_inc(s_rw, 1)
        for f in range(NF):
            ds_x2.wait(v, f)
            v.wait_ge(s_bc, 4)
            if f >= 2:
                ds_wr.wait(v, f - 2)
            for h in range(2):
                sl = slice(h * 512, (h + 1) * 512)
                step(v.tensor_tensor(xn[f % 2][:, sl], xb[f % 2][:, sl],
                                     pb[4 + h].ap()[:, :], ALU.subtract))
                sync()
                ins = v.tensor_tensor(xn[f % 2][:, sl], xn[f % 2][:, sl],
                                      pb[6 + h].ap()[:, :], ALU.mult)
                if h == 0:
                    step(ins)
                    sync()
            ins.then_inc(s_ap, 1)

    @block.gpsimd
    def _(g):
        for f in range(NF):
            g.wait_ge(s_ap, f + 1)
            g.dma_start(out=xn_dram_ap[f * 128:(f + 1) * 128, :],
                        in_=xn[f % 2]).then_inc(ds_wr.h(f), 16)

    return ds_wr.done(NF)


def emit_scalepass(env, block, tag, w1sh_ap, w2sh_ap, ccin_ap, ccout_ap, bufs,
                   waits=()):
    ds_in = DSem(env, tag + "_i")
    s_red = env.sem(tag + "_red")
    s_mm = env.sem(tag + "_mm")
    s_io = env.sem(tag + "_io")
    s_io2 = env.sem(tag + "_io2")
    s_cc = env.sem(tag + "_cc")
    s_thr = env.sem(tag + "_thr")
    cb = [bufs["cbuf0"], bufs["cbuf1"]]
    accs = bufs["accs"]
    crow = bufs["crow"]
    thr4 = bufs["thr4"]
    pb = env.pb
    onesrow = env.sb["onesrow"].ap()[:, :]
    ones128 = env.sb["ones128"].ap()[:, :]
    chunks = [(w1sh_ap[0:128, :], False), (w1sh_ap[128:256, :], False),
              (chunk_ap(w2sh_ap, 0, 512, 0, 2048), True),
              (chunk_ap(w2sh_ap, 512, 512, 0, 2048), True)]

    @block.gpsimd
    def _(g):
        for s, v in waits:
            g.wait_ge(s, v)
        for i, (src, is3) in enumerate(chunks):
            if i >= 2:
                g.wait_ge(s_red, i - 1)
            g.dma_start(out=(sb3(cb[i % 2], 4) if is3 else cb[i % 2]),
                        in_=src).then_inc(ds_in.h(i), 16)
        g.wait_ge(s_red, 6)
        g.dma_start(out=ccin_ap, in_=crow[0:1, 0:2]).then_inc(s_io, 16)
        g.wait_ge(s_io, 16)
        g.collective_compute(
            "AllReduce", ALU.add, replica_groups=[list(range(N_CORES))],
            ins=[ccin_ap], outs=[ccout_ap]).then_inc(s_cc, 1)
        g.wait_ge(s_cc, 1)
        g.dma_start(out=crow[0:1, 2:4], in_=ccout_ap).then_inc(s_io2, 16)

    @block.vector
    def _(v):
        for i in range(4):
            ds_in.wait(v, i)
            v.tensor_reduce(accs[:, i:i + 1], cb[i % 2], AX, ALU.add,
                            apply_absolute_value=True).then_inc(s_red, 1)
        v.wait_ge(s_red, 4)
        v.tensor_tensor(accs[:, 4:5], accs[:, 0:1], accs[:, 1:2], ALU.add)
        v.tensor_tensor(accs[:, 5:6], accs[:, 2:3], accs[:, 3:4],
                        ALU.add).then_inc(s_red, 1)
        v.wait_ge(s_mm, 1)
        v.tensor_copy(crow[0:1, 0:2], pb[0].ap()[0:1, 0:2]).then_inc(s_red, 1)
        v.wait_ge(s_io2, 16)
        v.tensor_scalar(crow[0:1, 4:6], crow[0:1, 2:4], 1.0 / 16777216.0, None,
                        ALU.mult).then_inc(s_red, 1)
        v.wait_ge(s_red, 7)
        v.tensor_scalar(crow[0:1, 4:6], crow[0:1, 4:6], 1e-8, 0.5, ALU.max,
                        ALU.mult).then_inc(s_red, 1)
        v.wait_ge(s_red, 8)
        v.tensor_scalar(crow[0:1, 6:8], crow[0:1, 4:6], -1.0, None,
                        ALU.mult).then_inc(s_red, 1)
        v.wait_ge(s_mm, 2)
        v.tensor_copy(thr4, pb[0].ap()[:, 0:4]).then_inc(s_thr, 1)

    @block.tensor
    def _(t):
        t.wait_ge(s_red, 5)
        t.matmul(pb[0].ap()[0:1, 0:2], ones128, accs[:, 4:6],
                 start=True, stop=True).then_inc(s_mm, 1)
        t.wait_ge(s_red, 9)
        t.matmul(pb[0].ap()[:, 0:4], onesrow, crow[0:1, 4:8],
                 start=True, stop=True).then_inc(s_mm, 1)

    return [(s_thr, 1)]


def emit_quantize(env, block, tag, wt_ap, wq_ap, nrows, ncols, thr4, w2, bufs,
                  waits=()):
    ncb = ncols // 512
    nrb = nrows // 2048
    nchunks = ncb * nrb
    tc, ncc = (1, 3) if w2 else (0, 2)
    ds_in = DSem(env, tag + "_i")
    s_q = env.sem(tag + "_q")
    s_q2 = env.sem(tag + "_q2")
    ds_wr = DSem(env, tag + "_w")
    win = [bufs["win0"], bufs["win1"]]
    tmpq = bufs["tmpq"]
    qout = [bufs["qout0"], bufs["qout1"]]

    def blk(mc):
        rb, cbk = divmod(mc, ncb)
        return rb * 2048, cbk * 512

    @block.sync
    def _(q):
        for s, v in waits:
            q.wait_ge(s, v)
        for mc in range(nchunks):
            r0, c0 = blk(mc)
            if mc >= 2:
                q.wait_ge(s_q, mc - 1)
            q.dma_start(out=sb3(win[mc % 2], 16),
                        in_=chunk_ap(wt_ap, r0, 2048, c0, 512)).then_inc(ds_in.h(mc), 16)

    @block.vector
    def _(v):
        for mc in range(nchunks):
            ds_in.wait(v, mc)
            if mc >= 2:
                ds_wr.wait(v, mc - 2)
            if mc >= 1:
                v.wait_ge(s_q, mc)
            v.tensor_scalar(tmpq, win[mc % 2], thr4[:, ncc:ncc + 1], None,
                            ALU.is_lt).then_inc(s_q2, 1)
            v.wait_ge(s_q2, mc + 1)
            v.scalar_tensor_tensor(qout[mc % 2], win[mc % 2], thr4[:, tc:tc + 1],
                                   tmpq, ALU.is_gt, ALU.subtract).then_inc(s_q, 1)

    @block.gpsimd
    def _(g):
        for mc in range(nchunks):
            r0, c0 = blk(mc)
            g.wait_ge(s_q, mc + 1)
            g.dma_start(out=chunk_ap(wq_ap, r0, 2048, c0, 512),
                        in_=sb3(qout[mc % 2], 16)).then_inc(ds_wr.h(mc), 16)

    return ds_wr.done(nchunks)


def emit_ffn0_gemm(env, block, tag, xin_ap, xn_ap, w1q_ap, w2q_ap, b1_sb, halfb2_sb,
                   xout_ap, bufs, waits=()):
    s_xn = env.sem(tag + "_xn")
    ds_w1 = DSem(env, tag + "_a")
    s_hm = env.sem(tag + "_hm")
    s_he = env.sem(tag + "_he")
    ds_w2 = DSem(env, tag + "_b")
    s_kb = env.sem(tag + "_kb")
    s_ym = env.sem(tag + "_ym")
    s_yr = env.sem(tag + "_yr")
    ds_xr = DSem(env, tag + "_c")
    s_ad = env.sem(tag + "_ad")
    ds_out = DSem(env, tag + "_o")
    xntt = bufs["xntt"]
    wb = [bufs["wb0"], bufs["wb1"]]
    hb = bufs["hb"]
    w2b = [bufs["w2b0"], bufs["w2b1"]]
    tres = [bufs["tres0"], bufs["tres1"]]
    xres = [bufs["xres0"], bufs["xres1"]]
    pb = env.pb
    NTT, NMG, NDG, NKB = 4, 16, 4, 8

    @block.sync
    def _(q):
        for s, v in waits:
            q.wait_ge(s, v)
        for tt in range(NTT):
            if tt >= 1:
                q.wait_ge(s_hm, 64 * tt)
            q.dma_start(out=sb3(xntt, 16),
                        in_=chunk_ap(xn_ap, 0, 2048, tt * 256, 256)).then_inc(s_xn, 16)
            for mg in range(NMG):
                i = tt * NMG + mg
                if i >= 2:
                    q.wait_ge(s_hm, 4 * (i - 1))
                q.dma_start(out=sb3(wb[i % 2], 16),
                            in_=chunk_ap(w1q_ap, 0, 2048, mg * 512, 512)
                            ).then_inc(ds_w1.h(i), 16)
            for dg in range(NDG):
                for kb in range(NKB):
                    j = (tt * NDG + dg) * NKB + kb
                    if j >= 2:
                        gp, kp = divmod(j - 2, NKB)
                        if kp == NKB - 1:
                            q.wait_ge(s_ym, 4 * (gp + 1))
                        else:
                            q.wait_ge(s_kb, gp * (NKB - 1) + kp + 1)
                    q.dma_start(out=sb3(w2b[j % 2], 8),
                                in_=chunk_ap(w2q_ap, kb * 1024, 1024, dg * 512, 512)
                                ).then_inc(ds_w2.h(j), 16)

    @block.tensor
    def _(t):
        for tt in range(NTT):
            t.wait_ge(s_xn, 16 * (tt + 1))
            for mg in range(NMG):
                i = tt * NMG + mg
                ds_w1.wait(t, i)
                if i >= 2:
                    t.wait_ge(s_he, 4 * (i - 1))
                bp = (mg % 2) * 2
                for j in range(4):
                    for k in range(16):
                        ins = t.matmul(
                            pb[bp + j // 2].ap()[:, (j % 2) * 256:(j % 2) * 256 + 256],
                            wb[i % 2][:, k * 512 + j * 128: k * 512 + (j + 1) * 128],
                            xntt[:, k * 256:(k + 1) * 256],
                            start=(k == 0), stop=(k == 15))
                    ins.then_inc(s_hm, 1)
            for dg in range(NDG):
                gi = tt * NDG + dg
                if gi >= 1:
                    t.wait_ge(s_yr, 4 * gi)
                t.wait_ge(s_he, 64 * (tt + 1))
                for kb in range(NKB):
                    j = gi * NKB + kb
                    ds_w2.wait(t, j)
                    for jd in range(4):
                        for k in range(8):
                            kk = kb * 8 + k
                            ins = t.matmul(
                                pb[4 + jd].ap()[:, 0:256],
                                w2b[j % 2][:, k * 512 + jd * 128: k * 512 + (jd + 1) * 128],
                                hb[:, kk * 256:(kk + 1) * 256],
                                start=(kb == 0 and k == 0), stop=(kb == NKB - 1 and k == 7))
                        if kb == NKB - 1:
                            ins.then_inc(s_ym, 1)
                    if kb < NKB - 1:
                        ins.then_inc(s_kb, 1)

    @block.scalar
    def _(s):
        for tt in range(NTT):
            for mg in range(NMG):
                i = tt * NMG + mg
                bp = (mg % 2) * 2
                for j in range(4):
                    m = mg * 4 + j
                    s.wait_ge(s_hm, 4 * i + (2 if j < 2 else 4))
                    s.activation(hb[:, m * 256:(m + 1) * 256],
                                 pb[bp + j // 2].ap()[:, (j % 2) * 256:(j % 2) * 256 + 256],
                                 AF.Gelu, bias=b1_sb[:, m:m + 1]).then_inc(s_he, 1)
            for dg in range(NDG):
                gi = tt * NDG + dg
                for jd in range(4):
                    d = dg * 4 + jd
                    n = gi * 4 + jd
                    s.wait_ge(s_ym, gi * 4 + jd + 1)
                    if n >= 2:
                        ds_out.wait(s, n - 2)
                    s.activation(tres[n % 2],
                                 pb[4 + jd].ap()[:, 0:256],
                                 AF.Identity, bias=halfb2_sb[:, d:d + 1],
                                 scale=0.5).then_inc(s_yr, 1)

    @block.vector
    def _(v):
        for n in range(64):
            v.wait_ge(s_yr, n + 1)
            ds_xr.wait(v, n)
            v.tensor_tensor(tres[n % 2], tres[n % 2], xres[n % 2],
                            ALU.add).then_inc(s_ad, 1)

    @block.gpsimd
    def _(g):
        for tt in range(NTT):
            for dg in range(NDG):
                for jd in range(4):
                    d = dg * 4 + jd
                    n = (tt * NDG + dg) * 4 + jd
                    if n >= 2:
                        g.wait_ge(s_ad, n - 1)
                    g.dma_start(out=xres[n % 2],
                                in_=xin_ap[d * 128:(d + 1) * 128,
                                           tt * 256:(tt + 1) * 256]).then_inc(ds_xr.h(n), 16)
                    g.wait_ge(s_ad, n + 1)
                    g.dma_start(out=xout_ap[d * 128:(d + 1) * 128,
                                            tt * 256:(tt + 1) * 256],
                                in_=tres[n % 2]).then_inc(ds_out.h(n), 16)

    return ds_out.done(64)



FFN0_QZ = {"win0": (0, 8192), "win1": (8192, 8192), "tmpq": (16384, 8192),
           "qout0": (24576, 8192), "qout1": (32768, 8192)}
FFN0_SP = {"cbuf0": (0, 8192), "cbuf1": (8192, 8192)}
LN_BUFS = {"xb0": (0, 1024), "xb1": (1024, 1024), "sq0": (2048, 1024),
           "sq1": (3072, 1024), "xn0": (4096, 1024), "xn1": (5120, 1024),
           "rows": (6144, 8192)}
FFN0_GM = {"xntt": (0, 4096), "wb0": (4096, 8192), "wb1": (12288, 8192),
           "hb": (20480, 16384), "w2b0": (36864, 4096), "w2b1": (40960, 4096),
           "tres0": (45056, 256), "tres1": (45312, 256),
           "xres0": (45568, 256), "xres1": (45824, 256)}


def emit_ffn0_block(env, block, tag, xin_ap, xout_ap, ext, dram, first=True,
                    waits=()):
    sp = carve(env, FFN0_SP)
    qz = carve(env, FFN0_QZ)
    ln = carve(env, LN_BUFS)
    gm = carve(env, FFN0_GM)
    if "accs" not in env.sb:
        env.sbuf("accs", (128, 16))
        env.sbuf("crow", (1, 16))
        env.sbuf("thr4", (128, 4))
    accs = env.sb["accs"].ap()[:, :]
    crow = env.sb["crow"].ap()[:, :]
    thr4 = env.sb["thr4"].ap()[:, :]
    sp["accs"], sp["crow"], sp["thr4"] = accs, crow, thr4

    if first:
        w_thr = emit_scalepass(env, block, tag + "sp", ext["w1sh"], ext["w2sh"],
                               dram["ccin"], dram["ccout"], sp, waits=waits)
        w_q1 = emit_quantize(env, block, tag + "q1", ext["w1t"], dram["w1q"],
                             2048, 8192, thr4, False, qz, waits=w_thr)
        w_q2 = emit_quantize(env, block, tag + "q2", ext["w2t"], dram["w2q"],
                             8192, 2048, thr4, True, qz, waits=w_q1)
        qdone = w_q2
    else:
        qdone = []

    w_ln = emit_ln(env, block, tag + "ln", xin_ap, dram["xn"], ln,
                   waits=list(waits) + qdone)
    w_out = emit_ffn0_gemm(env, block, tag + "gm", xin_ap, dram["xn"],
                           dram["w1q"], dram["w2q"],
                           env.sb["b1c"].ap()[:, :], env.sb["halfb2"].ap()[:, :],
                           xout_ap, gm, waits=w_ln + qdone)
    return w_out


def host_ffn0_inputs(w1, b1, w2, b2, core):
    w1t = np.ascontiguousarray(w1.T)
    w2t = np.ascontiguousarray(w2.T)
    return {
        "f_w1t": w1t,
        "f_w2t": w2t,
        "f_w1sh": np.ascontiguousarray(w1t[core * 256:(core + 1) * 256, :]),
        "f_w2sh": np.ascontiguousarray(w2t[core * 1024:(core + 1) * 1024, :]),
        "f_b1c": np.ascontiguousarray(b1.reshape(64, 128).T),
        "f_b2c": np.ascontiguousarray(b2.reshape(16, 128).T),
    }



ENGINES = ("sync", "gpsimd", "tensor", "vector", "scalar")


class Sched:

    DPOOL = 6

    def __init__(self, env, tag, base_waits=()):
        self.env = env
        self.tag = tag
        self.base_waits = list(base_waits)
        self.steps = {e: [] for e in ENGINES}
        self.esem = {e: [env.sem(tag + "_e_" + e), 0] for e in ENGINES}
        self.dsem = {e: [[env.sem(tag + "_d%s%d" % (e, i)), 0]
                         for i in range(self.DPOOL)] for e in ("sync", "gpsimd")}
        self.dcnt = {e: 0 for e in ("sync", "gpsimd")}
        self.last_w = {}
        self.readers = {}
        self.clock = {e: {} for e in ENGINES}

    def _need(self, eng, ident):
        if ident is None:
            return False
        sem, val = ident
        if self.clock[eng].get(sem.name, 0) >= val:
            return False
        return True

    def _mark(self, eng, ident):
        sem, val = ident
        if self.clock[eng].get(sem.name, 0) < val:
            self.clock[eng][sem.name] = val

    def add(self, eng, fn, reads=(), writes=(), dma=False, extra_waits=()):
        deps = []
        for tk in reads:
            deps.append(self.last_w.get(tk))
        for tk in writes:
            deps.extend(self.readers.get(tk, ()))
            deps.append(self.last_w.get(tk))
        deps = [d for d in deps if d is not None]
        for w in extra_waits:
            deps.append(w)
        deps.extend(self.base_waits)
        if dma:
            slot = self.dcnt[eng] % self.DPOOL
            rounds = self.dcnt[eng] // self.DPOOL
            self.dcnt[eng] += 1
            sem = self.dsem[eng][slot][0]
            val = 16 * (rounds + 1)
            ident = (sem, val)
            if rounds >= 1:
                deps.append((sem, 16 * rounds))
            inc = 16
        else:
            es = self.esem[eng]
            es[1] += 1
            ident = (es[0], es[1])
            inc = 1
        self.steps[eng].append((fn, deps, ident, inc))
        for tk in writes:
            self.last_w[tk] = ident
            self.readers[tk] = []
        for tk in reads:
            self.readers.setdefault(tk, []).append(ident)
        return ident

    def barrier_ident(self):
        out = []
        for e in ENGINES:
            if self.esem[e][1] > 0:
                out.append((self.esem[e][0], self.esem[e][1]))
        for e in ("sync", "gpsimd"):
            n = self.dcnt[e]
            for i in range(self.DPOOL):
                rounds = (n - 1 - i) // self.DPOOL + 1 if n > i else 0
                if rounds > 0:
                    out.append((self.dsem[e][i][0], 16 * rounds))
        return out

    def finalize(self, block):
        for ename in ENGINES:
            steps = self.steps[ename]
            if not steps:
                continue

            def make(ename=ename, steps=steps):
                def body(eng):
                    for fn, deps, ident, inc in steps:
                        for sem, val in deps:
                            if self._need(ename, (sem, val)):
                                eng.wait_ge(sem, val)
                                self._mark(ename, (sem, val))
                        ins = fn(eng)
                        ins.then_inc(ident[0], inc)
                return body
            getattr(block, ename)(make())



"""L0 mamba (SSD) block emitter: xout = x + Mamba(LN(x)).

Mirrors proto_l0.device_mamba_core (validated vs reference in fp64).
Built on mklib.Sched (auto semaphore insertion from read/write tokens).
Per core: 1024 timesteps, 8 chunks of 128, 64 heads = 32 pairs.
Pair-collectives: dtsum (cumdt offset) and H_final (state handoff).
"""


L = 128
NC = 8
NP = 32
INV2PI = float(1.0 / (2.0 * np.pi))
C1 = float(np.float32(6.283185005187988))
C2 = float(2.0 * np.pi - 6.283185005187988)
HPI = float(np.pi / 2.0)

L0_BUFS = {
    "xn": (0, 16384),
    "dtw": (16384, 1024), "dts": (17408, 1024), "clw": (18432, 1024),
    "cls": (19456, 1024), "clf": (20480, 1024), "s64": (21504, 1024),
    "onesd": (22528, 1024),
    "dtT": (23552, 512), "ecT": (24064, 512), "dcT": (24576, 512),
    "lamb": (25088, 256), "lamr": (25344, 64), "lrow": (25408, 128),
    "clpk": (25536, 128), "rpk": (25664, 256),
    "bc": (25920, 1024), "B2": (26944, 1024), "B2s": (27968, 1024),
    "C2": (28992, 1024), "C2s": (30016, 1024),
    "sqs": (31040, 1024), "stg": (32064, 1024), "rmr": (33088, 1024),
    "wob0": (34112, 2048), "wob1": (36160, 2048),
    "ph0": (34112, 4096), "ph1": (38208, 4096),
    "zw": (34112, 2048),
    "chm": (38208, 1024), "shm": (39232, 1024),
    "t1s": (40256, 1024), "t2s": (41280, 1024),
    "xsp": (42304, 1024), "brt": (43328, 1024), "crt": (44352, 1024),
    "ytl": (45376, 128), "Hc": (45504, 64), "ut": (45568, 128),
    "bdt": (45696, 128), "et": (45824, 256),
    "gsc": (16384, 1024), "zsl": (17408, 1024), "ysc": (18432, 1024),
    "gtl": (19456, 256), "ggt": (20480, 1024),
    "Hall": (21504, 2048),
    "tres0": (25920, 512), "tres1": (26432, 512),
    "xres0": (26944, 512), "xres1": (27456, 512),
    "H0": (28992, 2048),
}


def emit_l0(env, block, tag, xin_ap, xout_ap, ext, dram, bufs, waits=()):
    pb = env.pb
    ones128 = env.sb["ones128"].ap()[:, :]
    onesrow = env.sb["onesrow"].ap()[:, :]
    S = Sched(env, tag, base_waits=list(waits))
    Bf = bufs
    identsb = ext["ident"]
    dtb, alog = ext["dtb"], ext["alog"]
    DrepT, nwrep = ext["Drep"], ext["nwrep"]
    BbT, BbTs, CbT, CbTs = ext["BbT"], ext["BbTs"], ext["CbT"], ext["CbTs"]
    freqs = ext["freqs"]
    is1 = ext["is1"]
    triu2 = ext["triu2"]
    qsel = ext["qsel"]
    win_ap, wout_ap = ext["win"], ext["wout"]
    xn_ap, xs_ap = dram["xn"], dram["xs"]
    brd, crd = dram["brd"], dram["crd"]
    cosd, sind, nsind = dram["cosd"], dram["sind"], dram["nsind"]
    cld, ncld, clfd = dram["cld"], dram["ncld"], dram["clfd"]
    bcd = dram["bcd"]
    gd, ggd = dram["gd"], dram["ggd"]
    cc1i, cc1o = dram["cc1i"], dram["cc1o"]
    cc2i, cc2o = dram["cc2i"], dram["cc2o"]
    cos3 = cosd.rearrange("h (k t) -> h k t", k=32)
    sin3 = sind.rearrange("h (k t) -> h k t", k=32)
    nsin3 = nsind.rearrange("h (k t) -> h k t", k=32)

    S.add("sync", lambda q: q.dma_start(out=sb3(Bf["xn"], 16),
                                        in_=chunk_ap(xn_ap, 0, 2048, 0, TOK)),
          reads=["d:xn"], writes=["xn"], dma=True, extra_waits=list(waits))

    wbuf = [Bf["wob0"], Bf["wob1"]]

    def proj_cols(i):
        if i < 32:
            return 4096 + i * 128, 128
        if i == 32:
            return 8192, 128
        return 8320, 64

    for i in range(34):
        c0, mw = proj_cols(i)
        wb = wbuf[i % 2]
        wtk = "wob%d" % (i % 2)
        S.add("sync", lambda q, c0=c0, mw=mw, wb=wb:
              q.dma_start(out=sb3(wb[:, 0:16 * mw], 16),
                          in_=chunk_ap(win_ap, 0, 2048, c0, mw)),
              writes=[wtk], dma=True)
        bank = (i % 2) * 2
        for h in range(2):
            ptk = "p%d" % (bank + h)

            def mmfn(t, mw=mw, wb=wb, bank=bank, h=h):
                for k in range(16):
                    ins = t.matmul(pb[bank + h].ap()[0:mw, 0:512],
                                   wb[:, k * mw:(k + 1) * mw],
                                   Bf["xn"][:, k * 1024 + h * 512:
                                            k * 1024 + h * 512 + 512],
                                   start=(k == 0), stop=(k == 15))
                return ins
            S.add("tensor", mmfn, reads=[wtk, "xn"], writes=[ptk])
            if i < 32:
                S.add("scalar", lambda s, bank=bank, h=h:
                      s.activation(Bf["stg"][:, h * 512:(h + 1) * 512],
                                   pb[bank + h].ap()[0:128, 0:512], AF.Identity),
                      reads=[ptk], writes=["stg%d" % h])
                S.add("gpsimd", lambda g, i=i, h=h:
                      g.dma_start(out=xs_ap[i * 128:(i + 1) * 128,
                                            h * 512:(h + 1) * 512],
                                  in_=Bf["stg"][:, h * 512:(h + 1) * 512]),
                      reads=["stg%d" % h], writes=["d:xs"], dma=True)
            elif i == 32:
                S.add("scalar", lambda s, bank=bank, h=h:
                      s.activation(Bf["bc"][:, h * 512:(h + 1) * 512],
                                   pb[bank + h].ap()[0:128, 0:512], AF.Identity),
                      reads=[ptk], writes=["bc"])
            else:
                S.add("scalar", lambda s, bank=bank, h=h:
                      s.activation(Bf["dtw"][0:64, h * 512:(h + 1) * 512],
                                   pb[bank + h].ap()[0:64, 0:512], AF.Identity,
                                   bias=dtb[0:64, 0:1]),
                      reads=[ptk], writes=["dtw"])

    dt0 = Bf["dtw"][0:64, :]
    dtv = Bf["dts"][0:64, :]
    lg = Bf["clw"][0:64, :]
    clv = Bf["cls"][0:64, :]
    clfv = Bf["clf"][0:64, :]
    s64 = Bf["s64"]
    lrow = Bf["lrow"]
    S.add("gpsimd", lambda g: g.memset(Bf["onesd"][0:64, :], 1.0),
          writes=["onesd"])
    S.add("vector", lambda v: v.memset(Bf["lrow"][:, 72:73], HPI),
          writes=["hpic"])
    S.add("scalar", lambda s: s.activation(s64[0:64, :], dt0, AF.Abs),
          reads=["dtw"], writes=["s64"])
    S.add("scalar", lambda s: s.activation(s64[0:64, :], s64[0:64, :], AF.Exp,
                                           scale=-1.0),
          reads=["s64"], writes=["s64"])
    S.add("scalar", lambda s: s.activation(s64[0:64, :], s64[0:64, :], AF.Ln,
                                           bias=1.0),
          reads=["s64"], writes=["s64"])
    S.add("vector", lambda v: v.tensor_scalar(dtv, dt0, 0.0, None, ALU.max),
          reads=["dtw"], writes=["dts"])
    S.add("vector", lambda v: v.tensor_tensor(dtv, dtv, s64[0:64, :], ALU.add),
          reads=["dts", "s64"], writes=["dts"])
    S.add("vector", lambda v: v.tensor_reduce(lrow[0:64, 0:1], dtv, AX, ALU.add),
          reads=["dts"], writes=["lrow0"])
    S.add("gpsimd", lambda g: g.dma_start(out=cc1i, in_=lrow[0:64, 0:1]),
          reads=["lrow0"], writes=["d:cc1i"], dma=True)
    S.add("gpsimd", lambda g: g.collective_compute(
        "AllGather", ALU.bypass, replica_groups=[[0, 1], [2, 3], [4, 5], [6, 7]],
        ins=[cc1i], outs=[cc1o]),
        reads=["d:cc1i"], writes=["d:cc1o"])
    S.add("gpsimd", lambda g: g.dma_start(out=lrow[0:64, 3:4],
                                          in_=cc1o[0:64, 0:1]),
          reads=["d:cc1o"], writes=["lrow3"], dma=True)
    S.add("scalar", lambda s: s.activation(lrow[0:64, 1:2], alog[0:64, 0:1],
                                           AF.Exp),
          writes=["lrow1"])
    S.add("vector", lambda v: v.tensor_scalar(lrow[0:64, 2:3], lrow[0:64, 1:2],
                                              -1.0, None, ALU.mult),
          reads=["lrow1"], writes=["lrow2"])
    S.add("vector", lambda v: v.tensor_scalar(lg, dtv, lrow[0:64, 2:3], None,
                                              ALU.mult),
          reads=["dts", "lrow2"], writes=["clw"])
    S.add("vector", lambda v: v.tensor_tensor(lrow[0:64, 4:5], lrow[0:64, 3:4],
                                              is1[0:64, 0:1], ALU.mult),
          reads=["lrow3"], writes=["lrow4"])
    S.add("vector", lambda v: v.tensor_tensor_scan(
        Bf["dtw"][0:64, :], Bf["onesd"][0:64, :], dtv, lrow[0:64, 4:5],
        ALU.mult, ALU.add),
        reads=["onesd", "dts", "lrow4", "dtw"], writes=["dtw"])
    for c in range(NC):
        S.add("vector", lambda v, c=c: v.tensor_tensor_scan(
            clv[:, c * L:(c + 1) * L], Bf["onesd"][0:64, c * L:(c + 1) * L],
            lg[:, c * L:(c + 1) * L], 0.0, ALU.mult, ALU.add),
            reads=["onesd", "clw"], writes=["cls"])
    S.add("vector", lambda v: v.tensor_tensor_scan(
        clfv, Bf["onesd"][0:64, :], lg, 0.0, ALU.mult, ALU.add),
        reads=["onesd", "clw"], writes=["clf"])
    S.add("vector", lambda v: v.tensor_scalar(s64[0:64, :], clv, -1.0, None,
                                              ALU.mult),
          reads=["cls"], writes=["s64"])
    S.add("gpsimd", lambda g: g.dma_start(out=cld, in_=clv),
          reads=["cls"], writes=["d:cld"], dma=True)
    S.add("gpsimd", lambda g: g.dma_start(out=ncld, in_=s64[0:64, :]),
          reads=["s64"], writes=["d:ncld"], dma=True)
    S.add("gpsimd", lambda g: g.dma_start(out=clfd, in_=clfv),
          reads=["clf"], writes=["d:clfd"], dma=True)

    for c in range(NC):
        S.add("tensor", lambda t, c=c: t.transpose(
            pb[4].ap()[0:128, 0:64], Bf["dts"][0:64, c * L:(c + 1) * L],
            identsb[0:64, 0:64]),
            reads=["dts"], writes=["p4"])
        S.add("vector", lambda v, c=c: v.tensor_copy(
            Bf["dtT"][:, c * 64:(c + 1) * 64], pb[4].ap()[0:128, 0:64]),
            reads=["p4"], writes=["dtT"])
        S.add("tensor", lambda t, c=c: t.transpose(
            pb[5].ap()[0:128, 0:64], Bf["cls"][0:64, c * L:(c + 1) * L],
            identsb[0:64, 0:64]),
            reads=["cls"], writes=["p5"])
        S.add("vector", lambda v, c=c: v.tensor_copy(
            Bf["ecT"][:, c * 64:(c + 1) * 64], pb[5].ap()[0:128, 0:64]),
            reads=["p5"], writes=["ecT"])
        S.add("gpsimd", lambda g, c=c: g.dma_start(
            out=lrow[0:1, 8:72],
            in_=cld[0:64, c * L + 127:c * L + 128].rearrange("h a -> a h"),
            allow_slow_non_contiguous=True),
            reads=["d:cld"], writes=["lrow8"], dma=True)
        S.add("tensor", lambda t: t.matmul(pb[6].ap()[:, 0:64], onesrow,
                                           lrow[0:1, 8:72], start=True, stop=True),
              reads=["lrow8"], writes=["p6"])
        S.add("vector", lambda v, c=c: v.tensor_tensor(
            Bf["dcT"][:, c * 64:(c + 1) * 64], pb[6].ap()[:, 0:64],
            Bf["ecT"][:, c * 64:(c + 1) * 64], ALU.subtract),
            reads=["p6", "ecT"], writes=["dcT"])
        S.add("gpsimd", lambda g, c=c: g.dma_start(
            out=Bf["lamr"][0:2, 0:32],
            in_=cld.rearrange("(p q) t -> q p t", q=2)[:, :, c * L + 127:c * L + 128],
            allow_slow_non_contiguous=True),
            reads=["d:cld"], writes=["lamr"], dma=True)
        S.add("scalar", lambda s: s.activation(Bf["lamr"][0:2, 0:32],
                                               Bf["lamr"][0:2, 0:32], AF.Exp),
              reads=["lamr"], writes=["lamr"])
        S.add("tensor", lambda t: t.matmul(pb[7].ap()[:, 0:32], qsel,
                                           Bf["lamr"][0:2, 0:32],
                                           start=True, stop=True),
              reads=["lamr"], writes=["p7"])
        S.add("vector", lambda v, c=c: v.tensor_copy(
            Bf["lamb"][:, c * 32:(c + 1) * 32], pb[7].ap()[:, 0:32]),
            reads=["p7"], writes=["lamb"])
    S.add("scalar", lambda s: s.activation(Bf["ecT"][:, :], Bf["ecT"][:, :],
                                           AF.Exp),
          reads=["ecT"], writes=["ecT"])
    S.add("scalar", lambda s: s.activation(Bf["dcT"][:, :], Bf["dcT"][:, :],
                                           AF.Exp),
          reads=["dcT"], writes=["dcT"])

    S.add("scalar", lambda s: s.square(Bf["sqs"][:, :], Bf["bc"][:, :]),
          reads=["bc"], writes=["sqs"])
    for h in range(2):
        S.add("tensor", lambda t, h=h: t.matmul(
            pb[6].ap()[0:1, 0:512], ones128[0:64, 0:1],
            Bf["sqs"][0:64, h * 512:(h + 1) * 512], start=True, stop=True),
            reads=["sqs"], writes=["p6"])
        S.add("vector", lambda v, h=h: v.tensor_scalar(
            Bf["rmr"][0:1, h * 512:(h + 1) * 512], pb[6].ap()[0:1, 0:512],
            1.0 / 64, None, ALU.mult),
            reads=["p6"], writes=["rmrB"])
        S.add("tensor", lambda t, h=h: t.matmul(
            pb[7].ap()[0:1, 0:512], ones128[64:128, 0:1],
            Bf["sqs"][64:128, h * 512:(h + 1) * 512], start=True, stop=True),
            reads=["sqs"], writes=["p7"])
        S.add("vector", lambda v, h=h: v.tensor_scalar(
            s64[0:1, h * 512:(h + 1) * 512], pb[7].ap()[0:1, 0:512],
            1.0 / 64, None, ALU.mult),
            reads=["p7"], writes=["rmrC"])

    def newton_rstd(rowtk, row_ap, scr_ap, scr_alias=()):
        S.add("vector", lambda v: v.tensor_scalar(row_ap, row_ap, 1e-6, None,
                                                  ALU.add),
              reads=[rowtk], writes=[rowtk])
        S.add("vector", lambda v: v.reciprocal(scr_ap, row_ap),
              reads=[rowtk], writes=["nsc"] + list(scr_alias))
        S.add("scalar", lambda s: s.sqrt(scr_ap, scr_ap),
              reads=["nsc"], writes=["nsc"])
        S.add("vector", lambda v: v.tensor_tensor(row_ap, row_ap, scr_ap,
                                                  ALU.mult),
              reads=[rowtk, "nsc"], writes=[rowtk])
        S.add("vector", lambda v: v.tensor_tensor(row_ap, row_ap, scr_ap,
                                                  ALU.mult),
              reads=[rowtk, "nsc"], writes=[rowtk])
        S.add("vector", lambda v: v.tensor_scalar(row_ap, row_ap, -0.5, 1.5,
                                                  ALU.mult, ALU.add),
              reads=[rowtk], writes=[rowtk])
        S.add("vector", lambda v: v.tensor_tensor(row_ap, row_ap, scr_ap,
                                                  ALU.mult),
              reads=[rowtk, "nsc"], writes=[rowtk])

    newton_rstd("rmrB", Bf["rmr"][0:1, 0:1024], Bf["stg"][0:1, 0:1024],
                scr_alias=["stg0", "stg1"])
    newton_rstd("rmrC", s64[0:1, 0:1024], Bf["stg"][0:1, 0:1024])
    for h in range(2):
        S.add("tensor", lambda t, h=h: t.matmul(
            pb[6].ap()[:, 0:512], onesrow,
            Bf["rmr"][0:1, h * 512:(h + 1) * 512], start=True, stop=True),
            reads=["rmrB"], writes=["p6"])
        S.add("vector", lambda v, h=h: v.tensor_tensor(
            Bf["bc"][0:64, h * 512:(h + 1) * 512],
            Bf["bc"][0:64, h * 512:(h + 1) * 512],
            pb[6].ap()[0:64, 0:512], ALU.mult),
            reads=["p6", "bc"], writes=["bc"])
        S.add("tensor", lambda t, h=h: t.matmul(
            pb[7].ap()[:, 0:512], onesrow,
            s64[0:1, h * 512:(h + 1) * 512], start=True, stop=True),
            reads=["rmrC"], writes=["p7"])
        S.add("vector", lambda v, h=h: v.tensor_tensor(
            Bf["bc"][64:128, h * 512:(h + 1) * 512],
            Bf["bc"][64:128, h * 512:(h + 1) * 512],
            pb[7].ap()[64:128, 0:512], ALU.mult),
            reads=["p7", "bc"], writes=["bc"])
    S.add("gpsimd", lambda g: g.dma_start(out=bcd, in_=Bf["bc"][:, :]),
          reads=["bc"], writes=["d:bcd"], dma=True)
    for nm, rows in (("B2", [0, 32, 0, 32]), ("B2s", [32, 0, 32, 0]),
                     ("C2", [64, 96, 64, 96]), ("C2s", [96, 64, 96, 64])):
        for blk in range(4):
            S.add("sync", lambda q, nm=nm, blk=blk, r0=rows[blk]: q.dma_start(
                out=Bf[nm][blk * 32:(blk + 1) * 32, :],
                in_=bcd[r0:r0 + 32, :]),
                reads=["d:bcd"], writes=[nm], dma=True)

    ph0 = Bf["ph0"]
    ph1 = Bf["ph1"]
    ph2 = env.arena[:, 17408:21504]
    PI = float(np.pi)
    A = ph0[0:64, 0:4096]
    Bx = ph1[0:64, 0:4096]
    Cx = ph2[0:64, 0:4096]
    for c in range(NC):
        cslice = Bf["dtw"][0:64, c * L:(c + 1) * L]
        al = (["wob0", "wob1"] if c == 0 else [])
        al2 = (["dts", "clw", "cls", "clf"] if c == 0 else [])
        S.add("vector", lambda v, cslice=cslice: v.tensor_tensor(
            sb3(A, 32),
            freqs[0:64, :].unsqueeze(2).broadcast_to([64, 32, L]),
            cslice.unsqueeze(1).broadcast_to([64, 32, L]), ALU.mult),
            reads=["dtw"], writes=["ph0"] + al)
        S.add("vector", lambda v: v.tensor_scalar(Bx, A, INV2PI, None, ALU.mult),
              reads=["ph0"], writes=["ph1"] + al)
        S.add("vector", lambda v: v.tensor_copy(Cx.bitcast(I32), Bx),
              reads=["ph1"], writes=["ph2"] + al2)
        S.add("vector", lambda v: v.tensor_copy(Bx, Cx.bitcast(I32)),
              reads=["ph2"], writes=["ph1"])
        S.add("vector", lambda v: v.scalar_tensor_tensor(
            Cx, Bx, -C1, A, ALU.mult, ALU.add),
            reads=["ph1", "ph0"], writes=["ph2"])
        S.add("vector", lambda v: v.scalar_tensor_tensor(
            Cx, Bx, -C2, Cx, ALU.mult, ALU.add),
            reads=["ph1", "ph2"], writes=["ph2"])
        S.add("vector", lambda v: v.tensor_scalar(Bx, Cx, PI, None, ALU.is_gt),
              reads=["ph2"], writes=["ph1"])
        S.add("vector", lambda v: v.scalar_tensor_tensor(
            Cx, Bx, -C1, Cx, ALU.mult, ALU.add),
            reads=["ph1", "ph2"], writes=["ph2"])
        S.add("vector", lambda v: v.scalar_tensor_tensor(
            Cx, Bx, -C2, Cx, ALU.mult, ALU.add),
            reads=["ph1", "ph2"], writes=["ph2"])
        S.add("vector", lambda v: v.tensor_scalar(Bx, Cx, -PI, None, ALU.is_lt),
              reads=["ph2"], writes=["ph1"])
        S.add("vector", lambda v: v.scalar_tensor_tensor(
            Cx, Bx, C1, Cx, ALU.mult, ALU.add),
            reads=["ph1", "ph2"], writes=["ph2"])
        S.add("vector", lambda v: v.scalar_tensor_tensor(
            Cx, Bx, C2, Cx, ALU.mult, ALU.add),
            reads=["ph1", "ph2"], writes=["ph2"])
        S.add("scalar", lambda s: s.activation(A, Cx, AF.Abs),
              reads=["ph2"], writes=["ph0"])
        S.add("scalar", lambda s: s.activation(Bx, Cx, AF.Sin),
              reads=["ph2"], writes=["ph1"])
        S.add("scalar", lambda s: s.activation(A, A, AF.Sin, scale=-1.0,
                                               bias=Bf["lrow"][0:64, 72:73]),
              reads=["ph0", "hpic"], writes=["ph0"])
        S.add("vector", lambda v: v.tensor_scalar(Cx, Bx, -1.0, None, ALU.mult),
              reads=["ph1"], writes=["ph2"])
        S.add("gpsimd", lambda g, c=c: g.dma_start(
            out=cos3[:, :, c * L:(c + 1) * L], in_=sb3(A, 32)),
            reads=["ph0"], writes=["d:cosd"], dma=True)
        S.add("gpsimd", lambda g, c=c: g.dma_start(
            out=sin3[:, :, c * L:(c + 1) * L], in_=sb3(Bx, 32)),
            reads=["ph1"], writes=["d:sind"], dma=True)
        S.add("gpsimd", lambda g, c=c: g.dma_start(
            out=nsin3[:, :, c * L:(c + 1) * L], in_=sb3(Cx, 32)),
            reads=["ph2"], writes=["d:nsind"], dma=True)

    def rope_pair(p):
        for blk in range(4):
            q_ = blk // 2
            S.add("sync", lambda q, p=p, blk=blk, q_=q_: q.dma_start(
                out=Bf["chm"][blk * 32:(blk + 1) * 32, :],
                in_=cos3[2 * p + q_:2 * p + q_ + 1, :, :].squeeze(0)),
                reads=["d:cosd"],
                writes=(["chm", "ph1"] if p == 0 and blk == 0 else ["chm"]),
                dma=True)
        for blk in range(4):
            dd3 = nsin3 if blk % 2 == 0 else sin3
            tkn = "d:nsind" if blk % 2 == 0 else "d:sind"
            q_ = blk // 2
            S.add("sync", lambda q, p=p, dd3=dd3, blk=blk, q_=q_: q.dma_start(
                out=Bf["shm"][blk * 32:(blk + 1) * 32, :],
                in_=dd3[2 * p + q_:2 * p + q_ + 1, :, :].squeeze(0)),
                reads=[tkn], writes=(["shm", "ph1"] if p == 0 and blk == 0
                                     else ["shm"]), dma=True)
        for (base, basw, bias, biasw, dst) in (
                ("B2", "B2s", BbT, BbTs, "brt"), ("C2", "C2s", CbT, CbTs, "crt")):
            S.add("vector", lambda v, base=base, bias=bias, p=p:
                  v.tensor_scalar(Bf["t1s"], Bf[base], bias[:, p:p + 1], None,
                                  ALU.add),
                  reads=[base],
                  writes=(["t1s", "ph1"] if p == 0 and base == "B2" else ["t1s"]))
            S.add("vector", lambda v, basw=basw, biasw=biasw, p=p:
                  v.tensor_scalar(Bf["t2s"], Bf[basw], biasw[:, p:p + 1], None,
                                  ALU.add),
                  reads=[basw],
                  writes=(["t2s", "ph1"] if p == 0 and basw == "B2s" else ["t2s"]))
            S.add("vector", lambda v: v.tensor_tensor(
                Bf["t1s"], Bf["t1s"], Bf["chm"], ALU.mult),
                reads=["t1s", "chm"], writes=["t1s"])
            S.add("vector", lambda v: v.tensor_tensor(
                Bf["t2s"], Bf["t2s"], Bf["shm"], ALU.mult),
                reads=["t2s", "shm"], writes=["t2s"])
            S.add("vector", lambda v, dst=dst: v.tensor_tensor(
                Bf[dst], Bf["t1s"], Bf["t2s"], ALU.add),
                reads=["t1s", "t2s"], writes=[dst])

    S.add("vector", lambda v: v.memset(Bf["Hall"][:, :], 0.0),
          writes=["Hall", "s64", "onesd"])
    for p in range(NP):
        rope_pair(p)
        S.add("gpsimd", lambda g, p=p: g.dma_start(
            out=brd[p * 128:(p + 1) * 128, :], in_=Bf["brt"]),
            reads=["brt"], writes=["d:brd"], dma=True)
        S.add("gpsimd", lambda g, p=p: g.dma_start(
            out=crd[p * 128:(p + 1) * 128, :], in_=Bf["crt"]),
            reads=["crt"], writes=["d:crd"], dma=True)
        S.add("sync", lambda q, p=p: q.dma_start(
            out=Bf["xsp"], in_=xs_ap[p * 128:(p + 1) * 128, :]),
            reads=["d:xs"], writes=["xsp"], dma=True)
        for c in range(NC):
            S.add("tensor", lambda t, c=c: t.transpose(
                pb[4].ap()[0:128, 0:128], Bf["xsp"][:, c * L:(c + 1) * L],
                identsb),
                reads=["xsp"], writes=["p4"])
            for q_ in range(2):
                S.add("vector", lambda v, c=c, p=p, q_=q_: v.tensor_scalar(
                    Bf["ut"][:, q_ * 64:(q_ + 1) * 64],
                    pb[4].ap()[0:128, q_ * 64:(q_ + 1) * 64],
                    Bf["dtT"][:, c * 64 + 2 * p + q_:c * 64 + 2 * p + q_ + 1],
                    None, ALU.mult),
                    reads=["p4", "dtT"], writes=["ut"])
            S.add("tensor", lambda t, c=c: t.transpose(
                pb[5].ap()[0:128, 0:128], Bf["brt"][:, c * L:(c + 1) * L],
                identsb),
                reads=["brt"], writes=["p5"])
            for q_ in range(2):
                S.add("vector", lambda v, c=c, p=p, q_=q_: v.tensor_scalar(
                    Bf["bdt"][:, q_ * 64:(q_ + 1) * 64],
                    pb[5].ap()[0:128, q_ * 64:(q_ + 1) * 64],
                    Bf["dcT"][:, c * 64 + 2 * p + q_:c * 64 + 2 * p + q_ + 1],
                    None, ALU.mult),
                    reads=["p5", "dcT"], writes=["bdt"])
            for q_ in range(2):
                S.add("tensor", lambda t, q_=q_: t.matmul(
                    pb[6].ap()[q_ * 64:(q_ + 1) * 64, 0:64],
                    Bf["bdt"][:, q_ * 64:(q_ + 1) * 64],
                    Bf["ut"][:, q_ * 64:(q_ + 1) * 64],
                    start=True, stop=True,
                    tile_position=(0, q_ * 64)),
                    reads=["bdt", "ut"], writes=["p6"])
            S.add("vector", lambda v, c=c, p=p: v.scalar_tensor_tensor(
                Bf["Hall"][:, p * 64:(p + 1) * 64],
                Bf["Hall"][:, p * 64:(p + 1) * 64],
                Bf["lamb"][:, c * 32 + p:c * 32 + p + 1],
                pb[6].ap()[:, 0:64], ALU.mult, ALU.add),
                reads=["Hall", "lamb", "p6"], writes=["Hall"])

    S.add("gpsimd", lambda g: g.dma_start(out=cc2i, in_=Bf["Hall"][:, :]),
          reads=["Hall"], writes=["d:cc2i"], dma=True)
    S.add("gpsimd", lambda g: g.collective_compute(
        "AllGather", ALU.bypass, replica_groups=[[0, 1], [2, 3], [4, 5], [6, 7]],
        ins=[cc2i], outs=[cc2o]),
        reads=["d:cc2i"], writes=["d:cc2o"])
    S.add("gpsimd", lambda g: g.dma_start(out=Bf["H0"][:, :],
                                          in_=cc2o[0:128, :]),
          reads=["d:cc2o"], writes=["H0", "C2", "C2s"], dma=True)
    S.add("vector", lambda v: v.tensor_scalar(Bf["H0"][:, :], Bf["H0"][:, :],
                                              is1[:, 0:1], None, ALU.mult),
          reads=["H0"], writes=["H0"])

    S.add("sync", lambda q: q.dma_start(out=Bf["clpk"][0:1, :],
                                        in_=ext["ones1_d"]),
          writes=["clpk1r"], dma=True)
    S.add("sync", lambda q: q.dma_start(out=Bf["rpk"][1:3, :],
                                        in_=ext["rpkc_d"]),
          writes=["rpkc"], dma=True)
    for p in range(NP):
        S.add("sync", lambda q, p=p: q.dma_start(
            out=Bf["xsp"], in_=xs_ap[p * 128:(p + 1) * 128, :]),
            reads=["d:xs"], writes=["xsp"], dma=True)
        S.add("sync", lambda q, p=p: q.dma_start(
            out=Bf["brt"], in_=brd[p * 128:(p + 1) * 128, :]),
            reads=["d:brd"], writes=["brt"], dma=True)
        S.add("sync", lambda q, p=p: q.dma_start(
            out=Bf["crt"], in_=crd[p * 128:(p + 1) * 128, :]),
            reads=["d:crd"], writes=["crt"], dma=True)
        S.add("vector", lambda v, p=p: v.tensor_copy(
            Bf["Hc"][:, :], Bf["H0"][:, p * 64:(p + 1) * 64]),
            reads=["H0"], writes=["Hc"])
        for c in range(NC):
            S.add("tensor", lambda t, c=c: t.transpose(
                pb[4].ap()[0:128, 0:128], Bf["xsp"][:, c * L:(c + 1) * L],
                identsb),
                reads=["xsp"], writes=["p4"])
            for q_ in range(2):
                S.add("vector", lambda v, c=c, p=p, q_=q_: v.tensor_scalar(
                    Bf["ut"][:, q_ * 64:(q_ + 1) * 64],
                    pb[4].ap()[0:128, q_ * 64:(q_ + 1) * 64],
                    Bf["dtT"][:, c * 64 + 2 * p + q_:c * 64 + 2 * p + q_ + 1],
                    None, ALU.mult),
                    reads=["p4", "dtT"], writes=["ut"])
            for q_ in range(2):
                S.add("tensor", lambda t, c=c, q_=q_: t.matmul(
                    pb[0].ap()[:, q_ * 128:(q_ + 1) * 128],
                    Bf["brt"][q_ * 64:(q_ + 1) * 64, c * L:(c + 1) * L],
                    Bf["crt"][q_ * 64:(q_ + 1) * 64, c * L:(c + 1) * L],
                    start=True, stop=True),
                    reads=["brt", "crt"], writes=["p0"])
            S.add("gpsimd", lambda g, p=p, c=c: g.dma_start(
                out=Bf["clpk"][1:3, 0:128],
                in_=ncld[2 * p:2 * p + 2, c * L:(c + 1) * L]),
                reads=["d:ncld"], writes=["clpk2"], dma=True)
            S.add("gpsimd", lambda g, p=p, c=c: g.dma_start(
                out=Bf["rpk"][0:1, 0:256],
                in_=cld[2 * p:2 * p + 2, c * L:(c + 1) * L]),
                reads=["d:cld"], writes=["rpk0"], dma=True)
            S.add("tensor", lambda t: t.matmul(
                pb[1].ap()[:, 0:256], Bf["clpk"][0:3, 0:128],
                Bf["rpk"][0:3, 0:256], start=True, stop=True),
                reads=["clpk1r", "clpk2", "rpk0", "rpkc"], writes=["p1"])
            S.add("vector", lambda v: v.tensor_scalar(
                Bf["et"], pb[1].ap()[:, 0:256], 0.0, None, ALU.min),
                reads=["p1"], writes=["et"])
            S.add("scalar", lambda s: s.activation(Bf["et"], Bf["et"], AF.Exp),
                  reads=["et"], writes=["et"])
            S.add("vector", lambda v, c=c, p=p: v.tensor_tensor(
                Bf["gtl"], pb[0].ap()[:, 0:256], Bf["et"], ALU.mult),
                reads=["p0", "et"],
                writes=(["gtl", "ph2"] if p == 0 and c == 0 else ["gtl"]))
            S.add("vector", lambda v: v.tensor_tensor(
                Bf["gtl"], Bf["gtl"], triu2, ALU.mult),
                reads=["gtl"], writes=["gtl"])
            for q_ in range(2):
                S.add("tensor", lambda t, q_=q_: t.matmul(
                    pb[2].ap()[:, q_ * 64:(q_ + 1) * 64],
                    Bf["gtl"][:, q_ * 128:(q_ + 1) * 128],
                    Bf["ut"][:, q_ * 64:(q_ + 1) * 64],
                    start=True, stop=True),
                    reads=["gtl", "ut"], writes=["p2"])
                S.add("tensor", lambda t, c=c, q_=q_: t.matmul(
                    pb[3].ap()[:, q_ * 64:(q_ + 1) * 64],
                    Bf["crt"][q_ * 64:(q_ + 1) * 64, c * L:(c + 1) * L],
                    Bf["Hc"][q_ * 64:(q_ + 1) * 64, :],
                    start=True, stop=True),
                    reads=["crt", "Hc"], writes=["p3"])
            for q_ in range(2):
                S.add("vector", lambda v, q_=q_: v.tensor_copy(
                    Bf["ytl"][:, q_ * 64:(q_ + 1) * 64],
                    pb[2].ap()[:, q_ * 64:(q_ + 1) * 64]),
                    reads=["p2"], writes=["ytl"])
                S.add("vector", lambda v, c=c, p=p, q_=q_: v.scalar_tensor_tensor(
                    Bf["ytl"][:, q_ * 64:(q_ + 1) * 64],
                    pb[3].ap()[:, q_ * 64:(q_ + 1) * 64],
                    Bf["ecT"][:, c * 64 + 2 * p + q_:c * 64 + 2 * p + q_ + 1],
                    Bf["ytl"][:, q_ * 64:(q_ + 1) * 64], ALU.mult, ALU.add),
                    reads=["p3", "ecT", "ytl"], writes=["ytl"])
            S.add("tensor", lambda t: t.transpose(
                pb[5].ap()[0:128, 0:128], Bf["ytl"][:, 0:128], identsb),
                reads=["ytl"], writes=["p5"])
            S.add("vector", lambda v, c=c, p=p: v.tensor_copy(
                Bf["ysc"][:, c * L:(c + 1) * L], pb[5].ap()[0:128, 0:128]),
                reads=["p5"],
                writes=(["ysc", "ph2"] if p == 0 and c == 0 else ["ysc"]))
            S.add("tensor", lambda t, c=c: t.transpose(
                pb[4].ap()[0:128, 0:128], Bf["brt"][:, c * L:(c + 1) * L],
                identsb),
                reads=["brt"], writes=["p4"])
            for q_ in range(2):
                S.add("vector", lambda v, c=c, p=p, q_=q_: v.tensor_scalar(
                    Bf["bdt"][:, q_ * 64:(q_ + 1) * 64],
                    pb[4].ap()[0:128, q_ * 64:(q_ + 1) * 64],
                    Bf["dcT"][:, c * 64 + 2 * p + q_:c * 64 + 2 * p + q_ + 1],
                    None, ALU.mult),
                    reads=["p4", "dcT"], writes=["bdt"])
            for q_ in range(2):
                S.add("tensor", lambda t, q_=q_: t.matmul(
                    pb[6].ap()[q_ * 64:(q_ + 1) * 64, 0:64],
                    Bf["bdt"][:, q_ * 64:(q_ + 1) * 64],
                    Bf["ut"][:, q_ * 64:(q_ + 1) * 64],
                    start=True, stop=True,
                    tile_position=(0, q_ * 64)),
                    reads=["bdt", "ut"], writes=["p6"])
            S.add("vector", lambda v, c=c, p=p: v.scalar_tensor_tensor(
                Bf["Hc"][:, :], Bf["Hc"][:, :],
                Bf["lamb"][:, c * 32 + p:c * 32 + p + 1],
                pb[6].ap()[:, 0:64], ALU.mult, ALU.add),
                reads=["Hc", "lamb", "p6"], writes=["Hc"])
        S.add("vector", lambda v, p=p: v.tensor_scalar(
            Bf["t1s"], Bf["xsp"], DrepT[:, p:p + 1], None, ALU.mult),
            reads=["xsp"], writes=["t1s"])
        S.add("vector", lambda v: v.scalar_tensor_tensor(
            Bf["ysc"], Bf["ysc"], 0.125, Bf["t1s"], ALU.mult, ALU.add),
            reads=["ysc", "t1s"], writes=["ysc"])
        S.add("sync", lambda q, p=p: q.dma_start(
            out=sb3(Bf["zw"], 16),
            in_=chunk_ap(win_ap, 0, 2048, p * 128, 128)),
            writes=(["zw", "ph0"] if p == 0 else ["zw"]), dma=True)
        for h in range(2):
            S.add("tensor", lambda t, h=h: [
                t.matmul(pb[h].ap()[:, 0:512],
                         Bf["zw"][:, k * 128:(k + 1) * 128],
                         Bf["xn"][:, k * 1024 + h * 512:k * 1024 + h * 512 + 512],
                         start=(k == 0), stop=(k == 15))
                for k in range(16)][-1],
                  reads=["zw", "xn"], writes=["p%d" % h])
            S.add("scalar", lambda s, h=h, p=p: s.activation(
                Bf["zsl"][:, h * 512:(h + 1) * 512], pb[h].ap()[:, 0:512],
                AF.Silu),
                reads=["p%d" % h],
                writes=(["zsl", "ph2"] if p == 0 and h == 0 else ["zsl"]))
        S.add("vector", lambda v, p=p: v.tensor_tensor(Bf["gsc"], Bf["ysc"],
                                                  Bf["zsl"], ALU.mult),
              reads=["ysc", "zsl"],
              writes=(["gsc", "dtw"] if p == 0 else ["gsc"]))
        S.add("gpsimd", lambda g, p=p: g.dma_start(
            out=gd[p * 128:(p + 1) * 128, :], in_=Bf["gsc"]),
            reads=["gsc"], writes=["d:gd"], dma=True)
        S.add("scalar", lambda s: s.square(Bf["sqs"][:, :], Bf["gsc"]),
              reads=["gsc"], writes=["sqs"])
        for h in range(2):
            S.add("tensor", lambda t, h=h: t.matmul(
                pb[7].ap()[0:1, 0:512], ones128,
                Bf["sqs"][:, h * 512:(h + 1) * 512], start=True, stop=True),
                reads=["sqs"], writes=["p7"])
            if p == 0:
                S.add("vector", lambda v, h=h: v.tensor_copy(
                    Bf["rmr"][0:1, h * 512:(h + 1) * 512],
                    pb[7].ap()[0:1, 0:512]),
                    reads=["p7"], writes=["gsq"])
            else:
                S.add("vector", lambda v, h=h: v.tensor_tensor(
                    Bf["rmr"][0:1, h * 512:(h + 1) * 512],
                    Bf["rmr"][0:1, h * 512:(h + 1) * 512],
                    pb[7].ap()[0:1, 0:512], ALU.add),
                    reads=["p7", "gsq"], writes=["gsq"])

    S.add("vector", lambda v: v.tensor_scalar(
        Bf["rmr"][0:1, 0:1024], Bf["rmr"][0:1, 0:1024], 1.0 / 4096.0, None,
        ALU.mult),
        reads=["gsq"], writes=["gsq"])
    newton_rstd("gsq", Bf["rmr"][0:1, 0:1024], Bf["stg"][0:1, 0:1024])
    for h in range(2):
        S.add("tensor", lambda t, h=h: t.matmul(
            pb[6 + h].ap()[:, 0:512], onesrow,
            Bf["rmr"][0:1, h * 512:(h + 1) * 512], start=True, stop=True),
            reads=["gsq"], writes=["p%d" % (6 + h)])
    for p in range(NP):
        S.add("sync", lambda q, p=p: q.dma_start(
            out=Bf["gsc"], in_=gd[p * 128:(p + 1) * 128, :]),
            reads=["d:gd"], writes=["gsc"], dma=True)
        for h in range(2):
            S.add("vector", lambda v, h=h: v.tensor_tensor(
                Bf["gsc"][:, h * 512:(h + 1) * 512],
                Bf["gsc"][:, h * 512:(h + 1) * 512],
                pb[6 + h].ap()[:, 0:512], ALU.mult),
                reads=["gsc", "p%d" % (6 + h)], writes=["gsc"])
        S.add("vector", lambda v, p=p: v.tensor_scalar(
            Bf["gsc"], Bf["gsc"], nwrep[:, p:p + 1], None, ALU.mult),
            reads=["gsc"], writes=["gsc"])
        S.add("gpsimd", lambda g, p=p: g.dma_start(
            out=ggd[p * 128:(p + 1) * 128, :], in_=Bf["gsc"]),
            reads=["gsc"], writes=["d:ggd"], dma=True)

    for mg in range(4):
        for p in range(NP):
            wtk = "wob%d" % (p % 2)
            al9 = (["zw", "ph0"] if mg == 0 and p <= 1 else [])
            S.add("sync", lambda q, mg=mg, p=p: q.dma_start(
                out=wbuf[p % 2][:, 0:512],
                in_=wout_ap[p * 128:(p + 1) * 128, mg * 512:(mg + 1) * 512]),
                writes=[wtk] + al9, dma=True)
            S.add("sync", lambda q, p=p, mg=mg: q.dma_start(
                out=Bf["ggt"], in_=ggd[p * 128:(p + 1) * 128, :]),
                reads=["d:ggd"],
                writes=(["ggt", "ph2"] if mg == 0 and p == 0 else ["ggt"]),
                dma=True)
            for j in range(4):
                for h in range(2):
                    S.add("tensor", lambda t, p=p, j=j, h=h: t.matmul(
                        pb[j * 2 + h].ap()[:, 0:512],
                        wbuf[p % 2][:, j * 128:(j + 1) * 128],
                        Bf["ggt"][:, h * 512:(h + 1) * 512],
                        start=(p == 0), stop=(p == NP - 1)),
                        reads=[wtk, "ggt"], writes=["p%d" % (j * 2 + h)])
        for j in range(4):
            for h in range(2):
                n = (mg * 4 + j) * 2 + h
                dtile = mg * 4 + j
                S.add("gpsimd", lambda g, dtile=dtile, h=h, n=n: g.dma_start(
                    out=Bf["xres%d" % (n % 2)],
                    in_=xin_ap[dtile * 128:(dtile + 1) * 128,
                               h * 512:(h + 1) * 512]),
                    writes=["xres%d" % (n % 2), "B2"] if n <= 1 else
                    ["xres%d" % (n % 2)], dma=True)
                S.add("vector", lambda v, j=j, h=h, n=n: v.tensor_tensor(
                    Bf["tres%d" % (n % 2)], pb[j * 2 + h].ap()[:, 0:512],
                    Bf["xres%d" % (n % 2)], ALU.add),
                    reads=["p%d" % (j * 2 + h), "xres%d" % (n % 2)],
                    writes=["tres%d" % (n % 2), "bc"] if n <= 1 else
                    ["tres%d" % (n % 2)])
                S.add("gpsimd", lambda g, dtile=dtile, h=h, n=n: g.dma_start(
                    out=xout_ap[dtile * 128:(dtile + 1) * 128,
                                h * 512:(h + 1) * 512],
                    in_=Bf["tres%d" % (n % 2)]),
                    reads=["tres%d" % (n % 2)], writes=["d:xout"], dma=True)

    S.finalize(block)
    return S.barrier_ident()


def host_l0_inputs(W_in, W_out, dt_bias, A_log, D_param, B_bias, C_bias,
                   rope_freqs, norm_w, core):
    perm = np.concatenate([np.arange(0, 64, 2), np.arange(1, 64, 2)])
    W_in_p = np.array(W_in)
    W_in_p[:, 8192:8256] = W_in[:, 8192:8256][:, perm]
    W_in_p[:, 8256:8320] = W_in[:, 8256:8320][:, perm]
    BbT = np.zeros((128, 32), np.float32)
    CbT = np.zeros((128, 32), np.float32)
    for p in range(32):
        for q in range(2):
            h = 2 * p + q
            BbT[q * 64:q * 64 + 32, p] = B_bias[h, 0::2]
            BbT[q * 64 + 32:q * 64 + 64, p] = B_bias[h, 1::2]
            CbT[q * 64:q * 64 + 32, p] = C_bias[h, 0::2]
            CbT[q * 64 + 32:q * 64 + 64, p] = C_bias[h, 1::2]
    BbTs = np.concatenate([BbT[32:64], BbT[0:32], BbT[96:128], BbT[64:96]], 0)
    CbTs = np.concatenate([CbT[32:64], CbT[0:32], CbT[96:128], CbT[64:96]], 0)
    Drep = np.zeros((128, 32), np.float32)
    nwr = np.zeros((128, 32), np.float32)
    for p in range(32):
        Drep[0:64, p] = D_param[2 * p]
        Drep[64:128, p] = D_param[2 * p + 1]
        nwr[:, p] = norm_w[p * 128:(p + 1) * 128]
    col = np.zeros((128, 1), np.float32)
    col[0:64, 0] = dt_bias
    acol = np.zeros((128, 1), np.float32)
    acol[0:64, 0] = A_log
    fr = np.zeros((128, 32), np.float32)
    fr[0:64, :] = rope_freqs
    is1v = np.full((128, 1), float(core % 2), np.float32)
    tri = np.triu(np.ones((128, 128), np.float32))
    triu2 = np.concatenate([tri, tri], axis=1)
    qsel = np.zeros((2, 128), np.float32)
    qsel[0, 0:64] = 1.0
    qsel[1, 64:128] = 1.0
    rpkc = np.zeros((2, 256), np.float32)
    rpkc[0, 0:128] = 1.0
    rpkc[1, 128:256] = 1.0
    return {
        "l_rpkc": rpkc, "l_ones1": np.ones((1, 128), np.float32),
        "l_win": W_in_p.astype(np.float32),
        "l_wout": W_out.astype(np.float32),
        "l_dtb": col, "l_alog": acol, "l_Drep": Drep, "l_nwrep": nwr,
        "l_BbT": BbT, "l_BbTs": BbTs, "l_CbT": CbT, "l_CbTs": CbTs,
        "l_freqs": fr, "l_is1": is1v, "l_triu2": triu2, "l_qsel": qsel,
    }



"""L1 GQA causal attention block emitter (see plan.md).

xout = x + (Attn(LN(x)) @ Wo + bo)   [LN emitted separately by caller]

Uniform full-key program; causality via per-tile stt mask (per-core position
inputs); RoPE via DRAM block-swap re-read + sign-folded per-core host tables
(q tables pre-scaled by 1/8); per-head stabilizer c_qh = max_t||q_h/8|| *
max_t||k_h|| (exact-canceling); pair AllGather for K/V; denominator via
ones-column in AV lhsT; normalization via reciprocal + K=1 broadcast matmul.

PSUM banks: proj/out: 0..3 / 4..7 ping. attention: 0,1 scores; 2 AV; 3 bcast;
4,5 norm scratch; 6 transposes; 7 bias cols (0:32 = -c_qh).
"""


NQH = 32
NKV = 8
NTQ = 2
NTK = 16


def mask_needed(tqt, tkt):
    return not (128 * (tkt + 1) <= 512 * tqt)


MK_CUM = []
_c = 0
for _a in range(64):
    for _t in range(NTK):
        MK_CUM.append(_c)
        if mask_needed(_a % NTQ, _t):
            _c += 1

NRQ = 96


def emit_l1attn(env, block, tag, xin_ap, xout_ap, ext, dram, bufs, waits=()):
    pb = env.pb
    ones128 = env.sb["ones128"].ap()[:, :]
    onesrow = env.sb["onesrow"].ap()[:, :]

    s_xl = env.sem(tag + "xl")
    ds_w = DSem(env, tag + "w")
    s_pm = env.sem(tag + "pm")
    s_pe = env.sem(tag + "pe")
    s_qw = env.sem(tag + "qw")
    ds_sw = DSem(env, tag + "s")
    s_rp = env.sem(tag + "rp")
    s_kw = env.sem(tag + "kw")
    s_cc = env.sem(tag + "cc")
    s_nq = env.sem(tag + "nq")
    s_n2 = env.sem(tag + "n2")
    s_nr = env.sem(tag + "nr")
    s_nk = env.sem(tag + "nk")
    s_n2k = env.sem(tag + "n2k")
    s_sq2 = env.sem(tag + "sq2")
    s_bb = env.sem(tag + "bb")
    s_ng = env.sem(tag + "ng")
    s_kt = env.sem(tag + "kt")
    s_vt = env.sem(tag + "vt")
    s_tr = env.sem(tag + "tr")
    s_tv = env.sem(tag + "tv")
    s_vs = env.sem(tag + "vs")
    s_sm = env.sem(tag + "sm")
    s_ex = env.sem(tag + "ex")
    s_mk = env.sem(tag + "mk")
    s_av = env.sem(tag + "av")
    s_b2 = env.sem(tag + "b2")
    s_nm = env.sem(tag + "nm")
    s_ym = env.sem(tag + "ym")
    ds_wo = DSem(env, tag + "o")
    s_om = env.sem(tag + "om")
    s_or = env.sem(tag + "or")
    ds_xr = DSem(env, tag + "x")
    s_ad = env.sem(tag + "ad")
    ds_out = DSem(env, tag + "z")

    qsb = bufs["qsb"]
    ysb = bufs["ysb"]
    kvsb = bufs["kvsb"]
    wbuf = [bufs["wb0"], bufs["wb1"]]
    swt = bufs["swt"]
    kscr = bufs["kscr"]
    vtin = bufs["vtin"]
    vts = bufs["vts"]
    pt = [bufs["pt0"], bufs["pt1"]]
    osb = bufs["osb"]
    nrow = bufs["nrow"]
    sqs = bufs["sqs"]
    tres = [bufs["tres0"], bufs["tres1"]]
    xres = [bufs["xres0"], bufs["xres1"]]
    negc = bufs["negc"]
    onesw = bufs["onesw"]
    qcos, qsin = ext["qcos"], ext["qsin"]
    kcos, ksin = ext["kcos"], ext["ksin"]
    tqpos, tkpos = ext["tqpos"], ext["tkpos"]
    identsb = ext["ident"]
    bqc, bkc, bvc, boc = ext["bqc"], ext["bkc"], ext["bvc"], ext["boc"]
    wq_ap, wk_ap, wv_ap, wo_ap = ext["wq"], ext["wk"], ext["wv"], ext["wo"]
    xn_ap, q_ap = dram["xn"], dram["q"]
    kvin_ap, kvall_ap = dram["kvin"], dram["kvall"]
    xn_sb = ysb

    proj = [("q", wq_ap, m) for m in range(16)] + \
           [("k", wk_ap, m) for m in range(4)] + \
           [("v", wv_ap, m) for m in range(4)]

    def pdst(i):
        kind, wap, m = proj[i]
        if kind == "q":
            return qsb[:, m * 1024:(m + 1) * 1024], bqc[:, m:m + 1]
        if kind == "k":
            return kvsb[:, m * 1024:(m + 1) * 1024], bkc[:, m:m + 1]
        return kvsb[:, (4 + m) * 1024:(5 + m) * 1024], bvc[:, m:m + 1]

    omap = []
    _o = 0
    for a in range(64):
        if (a // NTQ) % 2 == 1:
            _o += 1
        omap.append(_o)

    def nrk(kvh):
        return NRQ + 10 * kvh

    @block.sync
    def _(q):
        for s, v in waits:
            q.wait_ge(s, v)
        q.dma_start(out=sb3(xn_sb, 16),
                    in_=chunk_ap(xn_ap, 0, 2048, 0, TOK)).then_inc(s_xl, 16)
        for i in range(24):
            kind, wap, m = proj[i]
            if i >= 2:
                q.wait_ge(s_pm, 2 * (i - 1))
            q.dma_start(out=sb3(wbuf[i % 2], 16),
                        in_=chunk_ap(wap, 0, 2048, m * 128, 128)).then_inc(ds_w.h(i), 16)
        for i in range(20):
            q.wait_ge(s_qw, 16 * 24)
            if i >= 1:
                q.wait_ge(s_rp, 3 * i)
            src = q_ap if i < 16 else kvin_ap
            row0 = (i % 16) * 128
            for blkk in range(4):
                soff = row0 + [32, 0, 96, 64][blkk]
                q.dma_start(out=swt[blkk * 32:(blkk + 1) * 32, :],
                            in_=src[soff:soff + 32, :]).then_inc(ds_sw.h(i), 16)
        for i in range(16):
            if i < 2:
                q.wait_ge(s_pm, 48)
            else:
                q.wait_ge(s_om, 2 * (i - 1))
            q.dma_start(out=sb3(wbuf[i % 2], 16),
                        in_=chunk_ap(wo_ap, 0, 2048, i * 128, 128)).then_inc(ds_wo.h(i), 16)

    @block.gpsimd
    def _(g):
        for i in range(24):
            g.wait_ge(s_pe, 2 * (i + 1))
            if i < 16:
                g.dma_start(out=q_ap[i * 128:(i + 1) * 128, :],
                            in_=qsb[:, i * 1024:(i + 1) * 1024]).then_inc(s_qw, 16)
            elif i < 20:
                m = i - 16
                g.dma_start(out=kvin_ap[m * 128:(m + 1) * 128, :],
                            in_=kvsb[:, m * 1024:(m + 1) * 1024]).then_inc(s_qw, 16)
            else:
                m = i - 20
                g.dma_start(out=kvin_ap[(4 + m) * 128:(5 + m) * 128, :],
                            in_=kvsb[:, (4 + m) * 1024:(5 + m) * 1024]).then_inc(s_qw, 16)
        for m in range(4):
            g.wait_ge(s_rp, 3 * (16 + m) + 3)
            g.dma_start(out=kvin_ap[m * 128:(m + 1) * 128, :],
                        in_=kvsb[:, m * 1024:(m + 1) * 1024]).then_inc(s_kw, 16)
        g.wait_ge(s_kw, 64)
        g.wait_ge(s_qw, 16 * 24)
        g.collective_compute(
            "AllGather", ALU.bypass,
            replica_groups=[[0, 1], [2, 3], [4, 5], [6, 7]],
            ins=[kvin_ap], outs=[kvall_ap]).then_inc(s_cc, 1)
        g.wait_ge(s_cc, 1)
        for kvh in range(NKV):
            if kvh >= 1:
                g.wait_ge(s_sm, 128 * kvh)
                g.wait_ge(s_nk, 4 * kvh)
            for half in range(2):
                for dup in range(2):
                    g.dma_start(
                        out=kscr[dup * 64:(dup + 1) * 64, half * 1024:(half + 1) * 1024],
                        in_=kvall_ap[half * 1024 + kvh * 64: half * 1024 + kvh * 64 + 64, :]
                        ).then_inc(s_kt, 16)
            for tkt in range(NTK):
                idx = kvh * NTK + tkt
                if idx >= 1:
                    g.wait_ge(s_tr, idx)
                half, loc = divmod(tkt, 8)
                g.dma_start(out=vtin[0:64, :],
                            in_=kvall_ap[half * 1024 + 512 + kvh * 64:
                                         half * 1024 + 512 + kvh * 64 + 64,
                                         loc * 128:(loc + 1) * 128]).then_inc(s_vt, 16)
            for a in range(kvh * 8, (kvh + 1) * 8):
                qh, tqt = divmod(a, NTQ)
                if qh % 2 == 1:
                    g.wait_ge(s_nm, 3 * a + 3)
                    g.dma_start(out=ysb[64:128, (qh // 2) * 1024 + tqt * 512:
                                        (qh // 2) * 1024 + tqt * 512 + 512],
                                in_=osb[0:64, :]).then_inc(s_ym, 16)
        for n in range(32):
            m, h = divmod(n, 2)
            if n >= 2:
                g.wait_ge(s_ad, n - 1)
            g.dma_start(out=xres[n % 2],
                        in_=xin_ap[m * 128:(m + 1) * 128,
                                   h * 512:(h + 1) * 512]).then_inc(ds_xr.h(n), 16)
            g.wait_ge(s_ad, n + 1)
            g.dma_start(out=xout_ap[m * 128:(m + 1) * 128, h * 512:(h + 1) * 512],
                        in_=tres[n % 2]).then_inc(ds_out.h(n), 16)

    @block.tensor
    def _(t):
        t.wait_ge(s_xl, 16)
        for i in range(24):
            ds_w.wait(t, i)
            if i >= 2:
                t.wait_ge(s_pe, 2 * (i - 1))
            bank = (i % 2) * 2
            for h in range(2):
                for k in range(16):
                    ins = t.matmul(pb[bank + h].ap()[:, :],
                                   wbuf[i % 2][:, k * 128:(k + 1) * 128],
                                   xn_sb[:, k * 1024 + h * 512: k * 1024 + h * 512 + 512],
                                   start=(k == 0), stop=(k == 15))
                ins.then_inc(s_pm, 1)
        for i in range(16):
            t.wait_ge(s_nq, i + 1)
            if i >= 1:
                t.wait_ge(s_nr, 6 * i)
            t.matmul(pb[4].ap()[0:1, :], ones128[0:64, 0:1], sqs[0:64, 0:512],
                     start=True, stop=True).then_inc(s_n2, 1)
            t.matmul(pb[5].ap()[0:1, :], ones128[64:128, 0:1], sqs[64:128, 0:512],
                     start=True, stop=True).then_inc(s_n2, 1)
            t.wait_ge(s_nr, 6 * i + 2)
            t.matmul(pb[4].ap()[0:1, :], ones128[0:64, 0:1], sqs[0:64, 512:1024],
                     start=True, stop=True).then_inc(s_n2, 1)
            t.matmul(pb[5].ap()[0:1, :], ones128[64:128, 0:1], sqs[64:128, 512:1024],
                     start=True, stop=True).then_inc(s_n2, 1)
        for kvh in range(NKV):
            for p in range(4):
                j = 4 * kvh + p
                t.wait_ge(s_nk, j + 1)
                t.wait_ge(s_nr, nrk(kvh) + 2 * p)
                t.matmul(pb[4].ap()[0:1, :], ones128[0:64, 0:1], sqs[0:64, 0:512],
                         start=True, stop=True).then_inc(s_n2k, 1)
            t.wait_ge(s_sq2, kvh + 1)
            t.wait_ge(s_nr, nrk(kvh) + 10)
            t.matmul(pb[7].ap()[:, kvh * 4:kvh * 4 + 4], onesrow,
                     nrow[0:1, 52:56], start=True, stop=True).then_inc(s_bb, 1)
            for tkt in range(NTK):
                idx = kvh * NTK + tkt
                t.wait_ge(s_vt, 16 * (idx + 1))
                if idx >= 1:
                    t.wait_ge(s_tv, idx)
                t.transpose(pb[6].ap()[0:128, 0:64], vtin[0:64, :],
                            identsb[0:64, 0:64]).then_inc(s_tr, 1)
            t.wait_ge(s_kt, 64 * (kvh + 1))
            t.wait_ge(s_tv, 16 * (kvh + 1))
            t.wait_ge(s_vs, 1)
            for g_ in range(4):
                qh = kvh * 4 + g_
                qb = (qh % 2) * 64
                for tqt in range(NTQ):
                    aidx = qh * NTQ + tqt
                    if aidx >= 1:
                        t.wait_ge(s_nm, 3 * (aidx - 1) + 2)
                    for tkt in range(NTK):
                        sidx = aidx * NTK + tkt
                        if sidx >= 2:
                            t.wait_ge(s_ex, sidx - 1)
                        t.matmul(pb[sidx % 2].ap()[:, :],
                                 kscr[qb:qb + 64, tkt * 128:(tkt + 1) * 128],
                                 qsb[qb:qb + 64, (qh // 2) * 1024 + tqt * 512:
                                     (qh // 2) * 1024 + tqt * 512 + 512],
                                 start=True, stop=True).then_inc(s_sm, 1)
                        if mask_needed(tqt, tkt):
                            t.wait_ge(s_mk, MK_CUM[sidx] + 1)
                        else:
                            t.wait_ge(s_ex, sidx + 1)
                        ins = t.matmul(pb[2].ap()[0:65, :],
                                       vts[:, tkt * 65:(tkt + 1) * 65],
                                       pt[sidx % 2], start=(tkt == 0),
                                       stop=(tkt == NTK - 1))
                    ins.then_inc(s_av, 1)
                    t.wait_ge(s_nm, 3 * aidx + 1)
                    t.matmul(pb[3].ap()[0:64, :], onesw[64:65, 0:64],
                             osb[64:65, :], start=True, stop=True).then_inc(s_b2, 1)
        for i in range(16):
            ds_wo.wait(t, i)
            if i >= 2:
                t.wait_ge(s_or, 2 * (i - 1))
            t.wait_ge(s_nm, 192)
            t.wait_ge(s_ym, 16 * 32)
            bank = 4 + (i % 2) * 2
            for h in range(2):
                for k in range(16):
                    ins = t.matmul(pb[bank + h].ap()[:, :],
                                   wbuf[i % 2][:, k * 128:(k + 1) * 128],
                                   ysb[:, k * 1024 + h * 512: k * 1024 + h * 512 + 512],
                                   start=(k == 0), stop=(k == 15))
                ins.then_inc(s_om, 1)

    @block.scalar
    def _(s):
        for i in range(24):
            bank = (i % 2) * 2
            dst, bias = pdst(i)
            for h in range(2):
                s.wait_ge(s_pm, 2 * i + h + 1)
                s.activation(dst[:, h * 512:(h + 1) * 512], pb[bank + h].ap()[:, :],
                             AF.Identity, bias=bias).then_inc(s_pe, 1)
        for i in range(16):
            s.wait_ge(s_rp, 3 * (i + 1))
            s.wait_ge(s_qw, 16 * 24)
            if i >= 1:
                s.wait_ge(s_n2, 4 * i)
            s.square(sqs[:, 0:1024], qsb[:, i * 1024:(i + 1) * 1024]).then_inc(s_nq, 1)
        for kvh in range(NKV):
            for p in range(4):
                j = 4 * kvh + p
                s.wait_ge(s_kt, 64 * (kvh + 1))
                if j == 0:
                    s.wait_ge(s_n2, 64)
                else:
                    s.wait_ge(s_n2k, j)
                s.square(sqs[0:64, 0:512],
                         kscr[0:64, p * 512:(p + 1) * 512]).then_inc(s_nk, 1)
            s.wait_ge(s_nr, nrk(kvh) + 9)
            s.sqrt(nrow[0:1, 52:56], nrow[0:1, 48:52]).then_inc(s_sq2, 1)
            for g_ in range(4):
                for tqt in range(NTQ):
                    aidx = (kvh * 4 + g_) * NTQ + tqt
                    qh = aidx // NTQ
                    for tkt in range(NTK):
                        sidx = aidx * NTK + tkt
                        s.wait_ge(s_sm, sidx + 1)
                        s.wait_ge(s_ng, qh // 4 + 1)
                        s.activation(pt[sidx % 2], pb[sidx % 2].ap()[:, :], AF.Exp,
                                     bias=negc[:, qh:qh + 1]).then_inc(s_ex, 1)
        for n in range(32):
            i = n // 2
            bank = 4 + (i % 2) * 2 + (n % 2)
            s.wait_ge(s_om, n + 1)
            if n >= 2:
                ds_out.wait(s, n - 2)
            s.activation(tres[n % 2], pb[bank].ap()[:, :],
                         AF.Identity, bias=boc[:, i:i + 1]).then_inc(s_or, 1)

    @block.vector
    def _(v):
        for i in range(20):
            v.wait_ge(ds_sw.s[i % 2], 64 * (i // 2 + 1))
            if i < 16:
                base = qsb[:, i * 1024:(i + 1) * 1024]
                cos, sin = qcos, qsin
            else:
                base = kvsb[:, (i - 16) * 1024:(i - 15) * 1024]
                cos, sin = kcos, ksin
            v.tensor_tensor(swt, swt, sin, ALU.mult).then_inc(s_rp, 1)
            v.wait_ge(s_rp, 3 * i + 1)
            v.tensor_tensor(base, base, cos, ALU.mult).then_inc(s_rp, 1)
            v.wait_ge(s_rp, 3 * i + 2)
            v.tensor_tensor(base, base, swt, ALU.add).then_inc(s_rp, 1)
        v.wait_ge(s_cc, 1)
        v.memset(onesw[:, :], 1.0)
        v.memset(vts[:, :], 1.0).then_inc(s_vs, 1)
        v.memset(nrow[0:1, 32:48], 0.0)
        for i in range(16):
            v.wait_ge(s_n2, 4 * i + 2)
            v.tensor_reduce(nrow[0:1, 2 * i:2 * i + 1], pb[4].ap()[0:1, :],
                            AX, ALU.max).then_inc(s_nr, 1)
            v.tensor_reduce(nrow[0:1, 2 * i + 1:2 * i + 2], pb[5].ap()[0:1, :],
                            AX, ALU.max).then_inc(s_nr, 1)
            v.wait_ge(s_n2, 4 * i + 4)
            v.tensor_reduce(nrow[0:1, 44:45], pb[4].ap()[0:1, :],
                            AX, ALU.max).then_inc(s_nr, 1)
            v.wait_ge(s_nr, 6 * i + 3)
            v.tensor_tensor(nrow[0:1, 2 * i:2 * i + 1], nrow[0:1, 2 * i:2 * i + 1],
                            nrow[0:1, 44:45], ALU.max).then_inc(s_nr, 1)
            v.tensor_reduce(nrow[0:1, 45:46], pb[5].ap()[0:1, :],
                            AX, ALU.max).then_inc(s_nr, 1)
            v.wait_ge(s_nr, 6 * i + 5)
            v.tensor_tensor(nrow[0:1, 2 * i + 1:2 * i + 2],
                            nrow[0:1, 2 * i + 1:2 * i + 2],
                            nrow[0:1, 45:46], ALU.max).then_inc(s_nr, 1)
        for kvh in range(NKV):
            for p in range(4):
                j = 4 * kvh + p
                v.wait_ge(s_n2k, j + 1)
                v.tensor_reduce(nrow[0:1, 44:45], pb[4].ap()[0:1, :], AX,
                                ALU.max).then_inc(s_nr, 1)
                v.wait_ge(s_nr, nrk(kvh) + 2 * p + 1)
                v.tensor_tensor(nrow[0:1, 32 + kvh:33 + kvh],
                                nrow[0:1, 32 + kvh:33 + kvh],
                                nrow[0:1, 44:45], ALU.max).then_inc(s_nr, 1)
            v.wait_ge(s_nr, nrk(kvh) + 8)
            v.tensor_scalar(nrow[0:1, 48:52], nrow[0:1, 4 * kvh:4 * kvh + 4],
                            nrow[0:1, 32 + kvh:33 + kvh], None,
                            ALU.mult).then_inc(s_nr, 1)
            v.wait_ge(s_sq2, kvh + 1)
            v.tensor_scalar(nrow[0:1, 52:56], nrow[0:1, 52:56], -1.0, None,
                            ALU.mult).then_inc(s_nr, 1)
            v.wait_ge(s_bb, kvh + 1)
            v.tensor_copy(negc[:, 4 * kvh:4 * kvh + 4],
                          pb[7].ap()[:, 4 * kvh:4 * kvh + 4]).then_inc(s_ng, 1)
            for tkt in range(NTK):
                idx = kvh * NTK + tkt
                v.wait_ge(s_tr, idx + 1)
                v.tensor_copy(vts[:, tkt * 65:tkt * 65 + 64],
                              pb[6].ap()[0:128, 0:64]).then_inc(s_tv, 1)
            for g_ in range(4):
                for tqt in range(NTQ):
                    aidx = (kvh * 4 + g_) * NTQ + tqt
                    qh = aidx // NTQ
                    for tkt in range(NTK):
                        sidx = aidx * NTK + tkt
                        if mask_needed(tqt, tkt):
                            v.wait_ge(s_ex, sidx + 1)
                            v.scalar_tensor_tensor(
                                pt[sidx % 2], tqpos[:, tqt * 512:(tqt + 1) * 512],
                                tkpos[:, tkt:tkt + 1], pt[sidx % 2],
                                ALU.is_ge, ALU.mult).then_inc(s_mk, 1)
                    v.wait_ge(s_av, aidx + 1)
                    if aidx >= 1 and omap[aidx - 1] >= 1:
                        v.wait_ge(s_ym, 16 * omap[aidx - 1])
                    v.reciprocal(osb[64:65, :], pb[2].ap()[64:65, :]).then_inc(s_nm, 1)
                    v.wait_ge(s_nm, 3 * aidx + 1)
                    v.tensor_copy(osb[0:64, :], pb[2].ap()[0:64, :]).then_inc(s_nm, 1)
                    v.wait_ge(s_b2, aidx + 1)
                    v.wait_ge(s_nm, 3 * aidx + 2)
                    if qh % 2 == 0:
                        v.tensor_tensor(
                            ysb[0:64, (qh // 2) * 1024 + tqt * 512:
                                (qh // 2) * 1024 + tqt * 512 + 512],
                            osb[0:64, :], pb[3].ap()[0:64, :],
                            ALU.mult).then_inc(s_nm, 1)
                    else:
                        v.tensor_tensor(osb[0:64, :], osb[0:64, :],
                                        pb[3].ap()[0:64, :],
                                        ALU.mult).then_inc(s_nm, 1)
        for n in range(32):
            v.wait_ge(s_or, n + 1)
            ds_xr.wait(v, n)
            v.tensor_tensor(tres[n % 2], tres[n % 2], xres[n % 2],
                            ALU.add).then_inc(s_ad, 1)

    return ds_out.done(32)


L1_BUFS = {
    "qsb": (0, 16384), "ysb": (16384, 16384), "kvsb": (32768, 8192),
    "wb0": (40960, 2048), "wb1": (43008, 2048),
    "swt": (45056, 1024),
}
L1_BUFS2 = {
    "kscr": (32768, 2048), "vts": (34816, 1040), "pt0": (35856, 512),
    "pt1": (36368, 512), "osb": (36880, 512), "nrow": (37392, 64),
    "sqs": (37456, 1024), "vtin": (38480, 128),
    "tres0": (38608, 512), "tres1": (39120, 512),
    "xres0": (39632, 512), "xres1": (40144, 512), "negc": (40656, 32),
    "onesw": (40688, 64),
}


def host_l1_inputs(Wq, bq, Wk, bk, Wv, bv, Wo, bo, core):
    s = core % 2
    pos = np.arange(1, 2048 + 1, dtype=np.float32)
    dpos = pos[s * 1024:(s + 1) * 1024]
    dhalf = 32
    freqs = 10000.0 ** (-(2.0 * np.arange(dhalf, dtype=np.float32)) / 64.0)
    theta = dpos[None, :] * freqs[:, None]
    cos = np.cos(theta).astype(np.float32)
    sin = np.sin(theta).astype(np.float32)
    cos4 = np.tile(cos, (4, 1))
    sin4 = np.concatenate([-sin, sin, -sin, sin], axis=0)
    tq = (s * 1024 + np.arange(1024, dtype=np.float32))
    tqpos = np.broadcast_to(tq[None, :], (128, 1024)).copy()
    tkp = np.arange(2048, dtype=np.float32).reshape(16, 128).T.copy()
    return {
        "a_wq": Wq, "a_wk": Wk, "a_wv": Wv, "a_wo": Wo,
        "a_bqc": np.ascontiguousarray(bq.reshape(16, 128).T),
        "a_bkc": np.ascontiguousarray(np.pad(bk, (0, 0)).reshape(4, 128).T),
        "a_bvc": np.ascontiguousarray(bv.reshape(4, 128).T),
        "a_boc": np.ascontiguousarray(bo.reshape(16, 128).T),
        "a_qcos": np.ascontiguousarray(cos4 * 0.125),
        "a_qsin": np.ascontiguousarray(sin4 * 0.125),
        "a_kcos": np.ascontiguousarray(cos4),
        "a_ksin": np.ascontiguousarray(sin4),
        "a_tqpos": tqpos,
        "a_tkpos": tkp,
        "a_ident": np.eye(128, dtype=np.float32),
    }



"""MoE block emitter (dense v1): xout = x + 0.5*MoE(LN(x)).  See plan.md.

Routing (per tq-tile of 128): logitsT [tq, 8] = xn.T @ gwT + gb (bias via K=1
ones mm); m1 = rowmax; eqm = (L==m1); lm = L - 1e30*eqm; m2 = rowmax(lm);
w1r = sigmoid(m1-m2), w2r = sigmoid(m2-m1) (== softmax of top2, exact);
wts = 0.5*(w1r*eqm + w2r*(lm==m2)); transpose -> wtsT [8, 1024] -> DRAM.

Dense experts per (tt of 256 tokens): seed acc = b2-term via K=8 mm
(b2stack x wtsT); for e: h = xn@w1p[e].T + b1p (GLU-permuted: glu rows 0:2048,
lin 2048:4096); y = (A*sigmoid(1.702A))*(B+1); ye = y@w2[e].T;
acc += ye * wbc_e (DRAM step-0 broadcast of wts row). xout = x + acc.

PSUM: routing: bank 0 (logits accum), bank 1 (transpose). seeds: banks 0/1.
experts: h m-groups of 4 -> banks {0..3}/{4..7} ping; y d-groups of 4 same.
"""


NE = 8
NTT = 4
NRT = 10

MOE_BUFS = {
    "rsb0": (0, 2048), "rsb1": (2048, 2048), "gwt": (4096, 128),
    "gbr": (4224, 16), "lts": (4240, 384), "wtsT": (4624, 1024),
    "b2s": (5648, 2048), "wbc": (7696, 256), "sgl": (7952, 512),
    "tres0": (8464, 256), "tres1": (8720, 256),
    "xres0": (8976, 256), "xres1": (9232, 256),
    "xnt": (10240, 4096), "hsb": (14336, 8192),
    "ws0": (22528, 8192), "ws1": (30720, 8192),
    "acc": (38912, 4096), "ascr": (43008, 256),
}


def emit_moe(env, block, tag, xin_ap, xout_ap, ext, dram, bufs, waits=()):
    pb = env.pb
    ones128 = env.sb["ones128"].ap()[:, :]
    onesrow = env.sb["onesrow"].ap()[:, :]

    s_gl = env.sem(tag + "gl")
    ds_rx = DSem(env, tag + "r")
    s_lm = env.sem(tag + "lm")
    s_rt = env.sem(tag + "rt")
    s_sg = env.sem(tag + "sg")
    s_tp = env.sem(tag + "tp")
    s_tv = env.sem(tag + "tv")
    s_ww = env.sem(tag + "ww")
    ds_wb = DSem(env, tag + "b")
    s_xt = env.sem(tag + "xt")
    ds_w = DSem(env, tag + "c")
    s_hm = env.sem(tag + "hm")
    s_he = env.sem(tag + "he")
    s_gs = env.sem(tag + "gs")
    s_gy = env.sem(tag + "gy")
    s_ym = env.sem(tag + "ym")
    s_sd = env.sem(tag + "sd")
    s_se = env.sem(tag + "se")
    s_ac = env.sem(tag + "ac")
    ds_xr = DSem(env, tag + "x")
    s_ad = env.sem(tag + "ad")
    ds_out = DSem(env, tag + "z")

    rsb = [bufs["rsb0"], bufs["rsb1"]]
    gwt, gbr, lts = bufs["gwt"], bufs["gbr"], bufs["lts"]
    wtsT, b2s, wbc, sgl = bufs["wtsT"], bufs["b2s"], bufs["wbc"], bufs["sgl"]
    xnt, hsb = bufs["xnt"], bufs["hsb"]
    wst = [bufs["ws0"], bufs["ws1"]]
    tres = [bufs["tres0"], bufs["tres1"]]
    xres = [bufs["xres0"], bufs["xres1"]]
    acc = bufs["acc"]
    ascr = bufs["ascr"]
    identsb = ext["ident"]
    gwt_ap, gbr_ap, b2s_ap = ext["gwt"], ext["gbr"], ext["b2s"]
    b1c_all = ext["b1c"]
    w1_ap, w2_ap = ext["w1all"], ext["w2all"]
    xn_ap = dram["xn"]
    wts_ap = dram["wts"]

    def wguard(q, i):
        if i < 2:
            q.wait_ge(s_tv, 8)
            return
        gp, kp = divmod(i - 2, 12)
        if kp < 8:
            q.wait_ge(s_hm, gp * 32 + (kp + 1) * 4)
        else:
            q.wait_ge(s_ym, gp * 16 + (kp - 8 + 1) * 4)

    @block.sync
    def _(q):
        for s, v in waits:
            q.wait_ge(s, v)
        q.dma_start(out=sb3(gwt, 16), in_=chunk_ap(gwt_ap, 0, 2048, 0, 8)
                    ).then_inc(s_gl, 16)
        q.dma_start(out=gbr[0:1, 0:8], in_=gbr_ap).then_inc(s_gl, 16)
        q.dma_start(out=b2s[0:8, :], in_=b2s_ap).then_inc(s_gl, 16)
        for tqt in range(8):
            if tqt >= 2:
                q.wait_ge(s_lm, tqt - 1)
            q.dma_start(out=sb3(rsb[tqt % 2], 16),
                        in_=chunk_ap(xn_ap, 0, 2048, tqt * 128, 128)
                        ).then_inc(ds_rx.h(tqt), 16)
        for tt in range(NTT):
            for e in range(NE):
                i12 = (tt * NE + e) * 12
                for mc in range(8):
                    i = i12 + mc
                    wguard(q, i)
                    q.dma_start(out=sb3(wst[i % 2], 16),
                                in_=chunk_ap(w1_ap, e * 2048, 2048, mc * 512, 512)
                                ).then_inc(ds_w.h(i), 16)
                for dc in range(4):
                    i = i12 + 8 + dc
                    wguard(q, i)
                    q.dma_start(out=sb3(wst[i % 2], 16),
                                in_=chunk_ap(w2_ap, e * 2048, 2048, dc * 512, 512)
                                ).then_inc(ds_w.h(i), 16)

    @block.gpsimd
    def _(g):
        g.wait_ge(s_tv, 8)
        g.dma_start(out=wts_ap, in_=wtsT[0:8, :]).then_inc(s_ww, 16)
        for tt in range(NTT):
            if tt >= 1:
                g.wait_ge(s_hm, 256 * tt)
            g.dma_start(out=sb3(xnt, 16),
                        in_=chunk_ap(xn_ap, 0, 2048, tt * 256, 256)).then_inc(s_xt, 16)
            for e in range(NE):
                i = tt * NE + e
                g.wait_ge(s_ww, 16)
                if i >= 1:
                    g.wait_ge(s_ac, 32 * i)
                g.dma_start(out=wbc,
                            in_=wts_ap[e:e + 1, tt * 256:(tt + 1) * 256]
                            .broadcast_to([128, 256])).then_inc(ds_wb.h(i), 16)
            for dt in range(16):
                n = tt * 16 + dt
                if n >= 2:
                    g.wait_ge(s_ad, n - 1)
                g.dma_start(out=xres[n % 2],
                            in_=xin_ap[dt * 128:(dt + 1) * 128,
                                       tt * 256:(tt + 1) * 256]).then_inc(ds_xr.h(n), 16)
                g.wait_ge(s_ad, n + 1)
                g.dma_start(out=xout_ap[dt * 128:(dt + 1) * 128,
                                        tt * 256:(tt + 1) * 256],
                            in_=tres[n % 2]).then_inc(ds_out.h(n), 16)

    @block.tensor
    def _(t):
        t.wait_ge(s_gl, 48)
        for tqt in range(8):
            ds_rx.wait(t, tqt)
            if tqt >= 1:
                t.wait_ge(s_rt, NRT * (tqt - 1) + 1)
            for k in range(16):
                t.matmul(pb[0].ap()[:, 0:8],
                         rsb[tqt % 2][:, k * 128:(k + 1) * 128],
                         gwt[:, k * 8:(k + 1) * 8],
                         start=(k == 0), stop=False)
            ins = t.matmul(pb[0].ap()[:, 0:8], onesrow, gbr[0:1, 0:8],
                           start=False, stop=True)
            ins.then_inc(s_lm, 1)
            t.wait_ge(s_rt, NRT * tqt + NRT)
            if tqt >= 1:
                t.wait_ge(s_tv, tqt)
            t.transpose(pb[1].ap()[0:8, 0:128], lts[:, tqt * 48 + 32:tqt * 48 + 40],
                        identsb).then_inc(s_tp, 1)
        for tt in range(NTT):
            t.wait_ge(s_tv, 8)
            t.wait_ge(s_xt, 16 * (tt + 1))
            for dt in range(16):
                if dt >= 2:
                    t.wait_ge(s_se, 16 * tt + dt - 1)
                elif tt >= 1:
                    t.wait_ge(s_ac, 256 * tt)
                t.matmul(pb[dt % 2].ap()[:, 0:256], b2s[0:8, dt * 128:(dt + 1) * 128],
                         wtsT[0:8, tt * 256:(tt + 1) * 256],
                         start=True, stop=True).then_inc(s_sd, 1)
            for e in range(NE):
                i = tt * NE + e
                i12 = i * 12
                for mg in range(8):
                    gidx = i * 8 + mg
                    ds_w.wait(t, i12 + mg)
                    if e == 0 and mg == 0:
                        t.wait_ge(s_se, 16 * (tt + 1))
                    if i >= 1 and mg < 2:
                        t.wait_ge(s_ac, 32 * i)
                    if gidx >= 2:
                        t.wait_ge(s_he, 4 * (gidx - 1))
                    bp = (mg % 2) * 4
                    for j in range(4):
                        m = mg * 4 + j
                        for k in range(16):
                            ins = t.matmul(
                                pb[bp + j].ap()[:, 0:256],
                                wst[(i12 + mg) % 2][:, k * 512 + j * 128:
                                                    k * 512 + (j + 1) * 128],
                                xnt[:, k * 256:(k + 1) * 256],
                                start=(k == 0), stop=(k == 15))
                        ins.then_inc(s_hm, 1)
                for dc in range(4):
                    ds_w.wait(t, i12 + 8 + dc)
                    t.wait_ge(s_gy, 32 * (i + 1))
                    if dc >= 2:
                        t.wait_ge(s_ac, 32 * i + 8 * (dc - 1))
                    bp = (dc % 2) * 4
                    for j in range(4):
                        for k in range(16):
                            ins = t.matmul(
                                pb[bp + j].ap()[:, 0:256],
                                wst[(i12 + 8 + dc) % 2][:, k * 512 + j * 128:
                                                        k * 512 + (j + 1) * 128],
                                hsb[:, k * 256:(k + 1) * 256],
                                start=(k == 0), stop=(k == 15))
                        ins.then_inc(s_ym, 1)

    @block.scalar
    def _(s):
        for tqt in range(8):
            c0 = tqt * 48
            s.wait_ge(s_rt, NRT * tqt + 6)
            s.activation(lts[:, c0 + 11:c0 + 12], lts[:, c0 + 10:c0 + 11],
                         AF.Sigmoid).then_inc(s_sg, 1)
            s.activation(lts[:, c0 + 12:c0 + 13], lts[:, c0 + 10:c0 + 11],
                         AF.Sigmoid, scale=-1.0).then_inc(s_sg, 1)
        for tt in range(NTT):
            for e in range(NE):
                i = tt * NE + e
                for mg in range(8):
                    bp = (mg % 2) * 4
                    for j in range(4):
                        m = mg * 4 + j
                        s.wait_ge(s_hm, 32 * i + mg * 4 + j + 1)
                        s.activation(hsb[:, m * 256:(m + 1) * 256],
                                     pb[bp + j].ap()[:, 0:256], AF.Identity,
                                     bias=b1c_all[:, e * 32 + m:e * 32 + m + 1]
                                     ).then_inc(s_he, 1)
                for gt in range(16):
                    s.wait_ge(s_he, 32 * (i + 1))
                    if gt >= 1:
                        s.wait_ge(s_gy, 32 * i + 2 * gt)
                    s.activation(sgl[:, 0:256], hsb[:, gt * 256:(gt + 1) * 256],
                                 AF.Sigmoid, scale=1.702).then_inc(s_gs, 1)

    @block.vector
    def _(v):
        for tqt in range(8):
            c0 = tqt * 48
            L = lts[:, c0:c0 + 8]
            m1 = lts[:, c0 + 8:c0 + 9]
            m2 = lts[:, c0 + 9:c0 + 10]
            dd = lts[:, c0 + 10:c0 + 11]
            w1r = lts[:, c0 + 11:c0 + 12]
            w2r = lts[:, c0 + 12:c0 + 13]
            eqm = lts[:, c0 + 16:c0 + 24]
            lm = lts[:, c0 + 24:c0 + 32]
            wt = lts[:, c0 + 32:c0 + 40]
            eq2 = lts[:, c0 + 40:c0 + 48]
            r0 = NRT * tqt
            v.wait_ge(s_lm, tqt + 1)
            v.tensor_copy(L, pb[0].ap()[:, 0:8]).then_inc(s_rt, 1)
            v.wait_ge(s_rt, r0 + 1)
            v.tensor_reduce(m1, L, AX, ALU.max).then_inc(s_rt, 1)
            v.wait_ge(s_rt, r0 + 2)
            v.tensor_scalar(eqm, L, m1, None, ALU.is_ge).then_inc(s_rt, 1)
            v.wait_ge(s_rt, r0 + 3)
            v.scalar_tensor_tensor(lm, eqm, -1e30, L, ALU.mult,
                                   ALU.add).then_inc(s_rt, 1)
            v.wait_ge(s_rt, r0 + 4)
            v.tensor_reduce(m2, lm, AX, ALU.max).then_inc(s_rt, 1)
            v.wait_ge(s_rt, r0 + 5)
            v.tensor_tensor(dd, m1, m2, ALU.subtract).then_inc(s_rt, 1)
            v.wait_ge(s_sg, 2 * tqt + 2)
            v.tensor_scalar(wt, eqm, w1r, None, ALU.mult).then_inc(s_rt, 1)
            v.wait_ge(s_rt, r0 + 7)
            v.tensor_scalar(eq2, lm, m2, None, ALU.is_ge).then_inc(s_rt, 1)
            v.wait_ge(s_rt, r0 + 8)
            v.scalar_tensor_tensor(wt, eq2, w2r, wt, ALU.mult,
                                   ALU.add).then_inc(s_rt, 1)
            v.wait_ge(s_rt, r0 + 9)
            v.tensor_scalar(wt, wt, 0.5, None, ALU.mult).then_inc(s_rt, 1)
            v.wait_ge(s_tp, tqt + 1)
            v.tensor_copy(wtsT[0:8, tqt * 128:(tqt + 1) * 128],
                          pb[1].ap()[0:8, 0:128]).then_inc(s_tv, 1)
        for tt in range(NTT):
            for dt in range(16):
                v.wait_ge(s_sd, 16 * tt + dt + 1)
                v.tensor_copy(acc[:, dt * 256:(dt + 1) * 256],
                              pb[dt % 2].ap()[:, 0:256]).then_inc(s_se, 1)
            for e in range(NE):
                i = tt * NE + e
                for gt in range(16):
                    v.wait_ge(s_gs, 16 * i + gt + 1)
                    v.tensor_tensor(sgl[:, 0:256], sgl[:, 0:256],
                                    hsb[:, gt * 256:(gt + 1) * 256],
                                    ALU.mult).then_inc(s_gy, 1)
                    v.wait_ge(s_gy, 32 * i + 2 * gt + 1)
                    v.scalar_tensor_tensor(hsb[:, gt * 256:(gt + 1) * 256],
                                           hsb[:, (16 + gt) * 256:(17 + gt) * 256],
                                           1.0, sgl[:, 0:256], ALU.add,
                                           ALU.mult).then_inc(s_gy, 1)
                ds_wb.wait(v, i)
                for dc in range(4):
                    bp = (dc % 2) * 4
                    for j in range(4):
                        dt = dc * 4 + j
                        v.wait_ge(s_ym, 16 * i + dc * 4 + j + 1)
                        v.wait_ge(s_ac, 32 * i + 8 * dc + 2 * j)
                        v.tensor_tensor(ascr, pb[bp + j].ap()[:, 0:256], wbc,
                                        ALU.mult).then_inc(s_ac, 1)
                        v.wait_ge(s_ac, 32 * i + 8 * dc + 2 * j + 1)
                        v.tensor_tensor(acc[:, dt * 256:(dt + 1) * 256],
                                        acc[:, dt * 256:(dt + 1) * 256],
                                        ascr, ALU.add).then_inc(s_ac, 1)
            for dt in range(16):
                n = tt * 16 + dt
                v.wait_ge(s_ac, 256 * (tt + 1))
                ds_xr.wait(v, n)
                if n >= 2:
                    ds_out.wait(v, n - 2)
                v.tensor_tensor(tres[n % 2], acc[:, dt * 256:(dt + 1) * 256],
                                xres[n % 2], ALU.add).then_inc(s_ad, 1)

    return ds_out.done(64)


def host_moe_inputs(gate_w, gate_b, w1, b1, w2, b2):
    perm = np.concatenate([np.arange(0, 4096, 2), np.arange(1, 4096, 2)])
    w1p = np.ascontiguousarray(
        np.transpose(w1, (0, 2, 1))[:, :, perm]).reshape(8 * 2048, 4096)
    w2t = np.ascontiguousarray(np.transpose(w2, (0, 2, 1))).reshape(8 * 2048, 2048)
    b1p = b1[:, perm]
    b1c = np.concatenate([np.ascontiguousarray(b1p[e].reshape(32, 128).T)
                          for e in range(8)], axis=1)
    return {
        "m_w1all": w1p.astype(np.float32),
        "m_w2all": w2t.astype(np.float32),
        "m_gwt": np.ascontiguousarray(gate_w.T),
        "m_gbr": gate_b.reshape(1, 8).astype(np.float32),
        "m_b2s": b2.astype(np.float32),
        "m_b1c": b1c.astype(np.float32),
    }



SMALL = {
    "l_dtb": [128, 1], "l_alog": [128, 1], "l_Drep": [128, 32],
    "l_nwrep": [128, 32], "l_BbT": [128, 32], "l_BbTs": [128, 32],
    "l_CbT": [128, 32], "l_CbTs": [128, 32], "l_freqs": [128, 32],
    "l_is1": [128, 1], "l_triu2": [128, 256], "l_qsel": [2, 128],
    "l_ident": [128, 128],
}

L1_EXT_SPECS = {
    "a_wq": [2048, 2048], "a_wk": [2048, 512], "a_wv": [2048, 512],
    "a_wo": [2048, 2048], "a_bqc": [128, 16], "a_bkc": [128, 4],
    "a_bvc": [128, 4], "a_boc": [128, 16],
    "a_qcos": [128, 1024], "a_qsin": [128, 1024],
    "a_kcos": [128, 1024], "a_ksin": [128, 1024],
    "a_tqpos": [128, 1024], "a_tkpos": [128, 16],
}

L0_DSHAPES = {
    "xn": [2048, 1024], "xs": [4096, 1024], "brd": [4096, 1024],
    "crd": [4096, 1024], "cosd": [64, 32768], "sind": [64, 32768],
    "nsind": [64, 32768], "cld": [64, 1024], "ncld": [64, 1024],
    "clfd": [64, 1024], "bcd": [128, 1024], "gd": [4096, 1024],
    "ggd": [4096, 1024], "cc1i": [64, 1], "cc1o": [128, 1],
    "cc2i": [128, 2048], "cc2o": [256, 2048],
}


def build_fused_program(nblocks=6):
    nc = bass.Bass("TRN2", target_bir_lowering=False)
    dt = nc.dram_tensor
    x = dt("x", [2048, 1024], F32, kind="ExternalInput").ap()
    xout = dt("xout", [2048, 1024], F32, kind="ExternalOutput").ap()
    E = {}
    for k, shp in [("f_w1t", [2048, 8192]), ("f_w2t", [8192, 2048]),
                   ("f_w1sh", [256, 8192]), ("f_w2sh", [1024, 2048]),
                   ("f_b1c", [128, 64]), ("f_b2c", [128, 16]),
                   ("l_win", [2048, 8384]), ("l_wout", [4096, 2048]),
                   ("l_rpkc", [2, 256]), ("l_ones1", [1, 128]),
                   ("m_w1all", [16384, 4096]), ("m_w2all", [16384, 2048]),
                   ("m_gwt", [2048, 8]), ("m_gbr", [1, 8]),
                   ("m_b2s", [8, 2048]), ("m_b1c", [128, 256]),
                   ("c_ident", [128, 128])]:
        E[k] = dt(k, shp, F32, kind="ExternalInput").ap()
    for k, shp in SMALL.items():
        if k == "l_ident":
            continue
        E[k] = dt(k, shp, F32, kind="ExternalInput").ap()
    for k, shp in L1_EXT_SPECS.items():
        E[k] = dt(k, shp, F32, kind="ExternalInput").ap()

    xs_mid = [dt("xmid%d" % i, [2048, 1024], F32).ap() for i in range(1, 6)]
    x1, x2, x3, x4, x5 = xs_mid
    dram_f1 = {
        "w1q": dt("w1q", [2048, 8192], F32).ap(),
        "w2q": dt("w2q", [8192, 2048], F32).ap(),
        "xn": dt("f1_xn", [2048, 1024], F32).ap(),
        "ccin": dt("f_ccin", [1, 2], F32).ap(),
        "ccout": dt("f_ccout", [1, 2], F32).ap(),
    }
    dram_f2 = dict(dram_f1)
    dram_f2["xn"] = dt("f2_xn", [2048, 1024], F32).ap()
    dram_l0 = {k: dt("l0_" + k, v, F32).ap() for k, v in L0_DSHAPES.items()}
    dram_a = {
        "xn": dt("a_xn", [2048, 1024], F32).ap(),
        "q": dt("q_d", [2048, 1024], F32).ap(),
        "kvin": dt("kvin_d", [1024, 1024], F32).ap(),
        "kvall": dt("kvall_d", [2048, 1024], F32).ap(),
    }
    dram_m1 = {"xn": dt("m1_xn", [2048, 1024], F32).ap(),
               "wts": dt("m1_wts", [8, 1024], F32).ap()}
    dram_m2 = {"xn": dt("m2_xn", [2048, 1024], F32).ap(),
               "wts": dt("m2_wts", [8, 1024], F32).ap()}

    with ExitStack() as stack:
        env = make_env(nc, stack)
        env.sbuf("b1c", (128, 64))
        env.sbuf("b2c", (128, 16))
        env.sbuf("halfb2", (128, 16))
        env.sbuf("sb_ident", (128, 128))
        l0_small = [k for k in SMALL if k != "l_ident"]
        for k in l0_small:
            env.sbuf("sb_" + k, tuple(SMALL[k]))
        l1_small = ["a_bqc", "a_bkc", "a_bvc", "a_boc", "a_qcos",
                    "a_qsin", "a_kcos", "a_ksin", "a_tqpos", "a_tkpos"]
        for k in l1_small:
            env.sbuf("sb_" + k, tuple(L1_EXT_SPECS[k]))
        env.sbuf("sb_m_b1c", (128, 256))

        block = stack.enter_context(nc.Block())
        bar = env.sem("fuse_bar")
        loads = [(env.sb["b1c"].ap()[:, :], E["f_b1c"]),
                 (env.sb["b2c"].ap()[:, :], E["f_b2c"]),
                 (env.sb["sb_ident"].ap()[:, :], E["c_ident"]),
                 (env.sb["sb_m_b1c"].ap()[:, :], E["m_b1c"])]
        loads += [(env.sb["sb_" + k].ap()[:, :], E[k]) for k in l0_small]
        loads += [(env.sb["sb_" + k].ap()[:, :], E[k]) for k in l1_small]
        w_init = emit_const_init(env, block, loads)
        s_hb = env.sem("halfb2")

        @block.vector
        def _(v):
            for s, v_ in w_init:
                v.wait_ge(s, v_)
            v.tensor_scalar(env.sb["halfb2"].ap()[:, :],
                            env.sb["b2c"].ap()[:, :], 0.5, None,
                            ALU.mult).then_inc(s_hb, 1)

        env.retire_epoch()

        ident = env.sb["sb_ident"].ap()[:, :]
        ext_f = {"w1t": E["f_w1t"], "w2t": E["f_w2t"],
                 "w1sh": E["f_w1sh"], "w2sh": E["f_w2sh"]}
        ext_l0 = {"win": E["l_win"], "wout": E["l_wout"], "ident": ident,
                  "rpkc_d": E["l_rpkc"], "ones1_d": E["l_ones1"]}
        for k in ("dtb", "alog", "Drep", "nwrep", "BbT", "BbTs", "CbT",
                  "CbTs", "freqs", "is1", "triu2", "qsel"):
            ext_l0[k] = env.sb["sb_l_" + k].ap()[:, :]
        ext_a = {"wq": E["a_wq"], "wk": E["a_wk"], "wv": E["a_wv"],
                 "wo": E["a_wo"], "ident": ident}
        for k in ("bqc", "bkc", "bvc", "boc", "qcos", "qsin", "kcos",
                  "ksin", "tqpos", "tkpos"):
            ext_a[k] = env.sb["sb_a_" + k].ap()[:, :]
        ext_m = {"gwt": E["m_gwt"], "gbr": E["m_gbr"], "b2s": E["m_b2s"],
                 "b1c": env.sb["sb_m_b1c"].ap()[:, :],
                 "w1all": E["m_w1all"], "w2all": E["m_w2all"],
                 "ident": ident}

        outs = [x1, x2, x3, x4, x5, xout]
        if nblocks <= 6:
            outs[nblocks - 1] = xout

        w_last = emit_ffn0_block(env, block, "f1", x, outs[0], ext_f,
                                   dram_f1, first=True,
                                   waits=w_init + [(s_hb, 1)])
        if nblocks == 23:
            wb = emit_boundary(env, block, bar, 1, w_last)
            w_last = emit_ffn0_block(env, block, "f2", x1, xout, ext_f,
                                       dram_f2, first=False,
                                       waits=wb + [(s_hb, 1)])
        if 2 <= nblocks <= 6:
            wb = emit_boundary(env, block, bar, 1, w_last)
            ln = carve(env, LN_BUFS)
            w_ln = emit_ln(env, block, "ln2", x1, dram_l0["xn"], ln,
                             waits=wb)
            w_last = emit_l0(env, block, "l", x1, outs[1], ext_l0, dram_l0,
                                carve(env, L0_BUFS), waits=w_ln)
        if 6 >= nblocks >= 3:
            wb = emit_boundary(env, block, bar, 2, w_last)
            w_last = emit_ffn0_block(env, block, "f2", x2, outs[2], ext_f,
                                       dram_f2, first=False,
                                       waits=wb + [(s_hb, 1)])
        if 6 >= nblocks >= 4:
            wb = emit_boundary(env, block, bar, 3, w_last)
            w_ln = emit_ln(env, block, "ln4", x3, dram_m1["xn"],
                             carve(env, LN_BUFS), waits=wb)
            w_last = emit_moe(env, block, "m1", x3, outs[3], ext_m,
                                 dram_m1, carve(env, MOE_BUFS),
                                 waits=w_ln)
        if 6 >= nblocks >= 5:
            wb = emit_boundary(env, block, bar, 4, w_last)
            w_ln = emit_ln(env, block, "ln5", x4, dram_a["xn"],
                             carve(env, LN_BUFS), waits=wb)
            w_last = emit_l1attn(env, block, "a", x4, outs[4], ext_a,
                                    dram_a,
                                    carve(env, {**L1_BUFS,
                                                  **L1_BUFS2}),
                                    waits=w_ln)
        if 6 >= nblocks >= 6:
            wb = emit_boundary(env, block, bar, 5, w_last)
            w_ln = emit_ln(env, block, "ln6", x5, dram_m2["xn"],
                             carve(env, LN_BUFS), waits=wb)
            w_last = emit_moe(env, block, "m2", x5, xout, ext_m, dram_m2,
                                 carve(env, MOE_BUFS), waits=w_ln)

        @block.gpsimd
        def _(g):
            for s, v in w_last:
                g.wait_ge(s, v)
    return nc




_PROGS = {}


def _get_prog():
    if "fused" not in _PROGS:
        _PROGS["fused"] = build_fused_program()
    return _PROGS["fused"]


def kernel(**inp):
    nc = _get_prog()
    f = lambda k: np.ascontiguousarray(np.asarray(inp[k], dtype=np.float32))
    x = f("L0_attn_x")

    moe_m = host_moe_inputs(f("L1_ffn_gate_w"), f("L1_ffn_gate_b"),
                            f("L1_ffn_w1"), f("L1_ffn_b1"),
                            f("L1_ffn_w2"), f("L1_ffn_b2"))
    in_maps = []
    for c in range(8):
        b, s = c // 2, c % 2
        m = {"x": np.ascontiguousarray(x[b, s * 1024:(s + 1) * 1024, :].T)}
        m.update(host_ffn0_inputs(f("L0_ffn_w1"), f("L0_ffn_b1"),
                                  f("L0_ffn_w2"), f("L0_ffn_b2"), c))
        m.update(host_l0_inputs(f("L0_attn_W_in"), f("L0_attn_W_out"),
                                f("L0_attn_dt_bias"), f("L0_attn_A_log"),
                                f("L0_attn_D_param"), f("L0_attn_B_bias"),
                                f("L0_attn_C_bias"), f("L0_attn_rope_freqs"),
                                f("L0_attn_norm_w"), c))
        m.update(host_l1_inputs(f("L1_attn_Wq"), f("L1_attn_bq"),
                                f("L1_attn_Wk"), f("L1_attn_bk"),
                                f("L1_attn_Wv"), f("L1_attn_bv"),
                                f("L1_attn_Wo"), f("L1_attn_bo"), c))
        m.update(moe_m)
        m["c_ident"] = np.eye(128, dtype=np.float32)
        in_maps.append(m)

    res = run_bass_kernel_spmd(nc, in_maps, core_ids=list(range(8)))
    out = np.zeros((4, 2048, 2048), np.float32)
    for c in range(8):
        out[c // 2, (c % 2) * 1024:(c % 2 + 1) * 1024, :] = \
            np.asarray(res.results[c]["xout"]).T
    return out
